# version 14
# baseline (speedup 1.0000x reference)
import numpy as np
from contextlib import ExitStack

import concourse.bass as bass
import concourse.mybir as mybir
from concourse.bass_utils import run_bass_kernel_spmd

F32 = mybir.dt.float32
BF16 = mybir.dt.bfloat16
AF = mybir.ActivationFunctionType
OP = mybir.AluOpType
AX = mybir.AxisListType

N_CORES = 8
D = 1024
TC = 4096
TT = 1024
NTILE = TC // TT
NG = TT // 512
NE = 16
FI = 512
NSQ = 14
NPW = 6
SIGMA_STAGE = 99
DEBUG_SIGMA = False
EPS_A = 1e-4
LN_EPS = 1e-5


class Sem:
    def __init__(self, es, nc, name):
        self.h = es.enter_context(nc.semaphore(name))
        self.n = 0

    def inc(self, k=1):
        self.n += k
        return self.n


def emit_sigma(nc, block, es, wT, scale_out, consts, sems, tag):
    s_mm, s_ve, s_sc, s_ld = sems["mm"], sems["ve"], sems["sc"], sems["ld"]
    ones_col, ones_row, ones_bf = consts["ones_col"], consts["ones_row"], consts["ones_bf"]

    wt = es.enter_context(nc.sbuf_tensor(f"sg_wt{tag}", [128, 8, 1024], F32))
    Bf = es.enter_context(nc.sbuf_tensor(f"sg_Bf{tag}", [128, 8, 1024], F32))
    M0 = es.enter_context(nc.sbuf_tensor(f"sg_M0{tag}", [128, 8, 1024], BF16))
    M1 = es.enter_context(nc.sbuf_tensor(f"sg_M1{tag}", [128, 8, 1024], BF16))
    junk = es.enter_context(nc.sbuf_tensor(f"sg_junk{tag}", [128, 512], BF16))
    acc = es.enter_context(nc.sbuf_tensor(f"sg_acc{tag}", [128, 16], F32))
    r1 = es.enter_context(nc.sbuf_tensor(f"sg_r1{tag}", [128, 1], F32))
    inv1 = es.enter_context(nc.sbuf_tensor(f"sg_inv1{tag}", [128, 1], F32))
    ssc = es.enter_context(nc.sbuf_tensor(f"sg_ssc{tag}", [1, 1], F32))
    vv = es.enter_context(nc.sbuf_tensor(f"sg_vv{tag}", [128, 8], F32))
    tt = es.enter_context(nc.sbuf_tensor(f"sg_tt{tag}", [128, 8], F32))
    prod = es.enter_context(nc.sbuf_tensor(f"sg_prod{tag}", [128, 8], F32))
    den_sb = es.enter_context(nc.sbuf_tensor(f"sg_den{tag}", [128, 8], F32))
    n2 = es.enter_context(nc.sbuf_tensor(f"sg_n2{tag}", [1, 2], F32))
    ps0 = es.enter_context(nc.psum_tensor(f"sg_ps0{tag}", [128, 512], F32))
    ps1 = es.enter_context(nc.psum_tensor(f"sg_ps1{tag}", [128, 512], F32))
    pv = es.enter_context(nc.psum_tensor(f"sg_pv{tag}", [128, 1], F32))
    pv2 = es.enter_context(nc.psum_tensor(f"sg_pv2{tag}", [128, 1], F32))

    ld_gate = s_ld.n + 16
    ve_prev = s_ve.n
    sc_prev = s_sc.n
    mm_prev = s_mm.n

    def _(s):
        s.wait_ge(s_ve.h, ve_prev)
        s.wait_ge(s_sc.h, sc_prev)
        s.wait_ge(s_mm.h, mm_prev)
        s.dma_start(out=wt[:, :, :], in_=wT.rearrange("(c p) n -> p c n", p=128)).then_inc(s_ld.h, 16)
    block.sync(_)
    s_ld.inc(16)

    def sq_pass(src, dst, is_b_build):
        mm_base = s_mm.n
        ve_base = s_ve.n
        sc_base = s_sc.n

        def _t(t):
            if is_b_build:
                t.wait_ge(s_ld.h, ld_gate)
            t.wait_ge(s_ve.h, ve_base)
            t.wait_ge(s_sc.h, sc_base)
            for k in range(16):
                iblk, jg = k // 2, k % 2
                ps = ps0 if jg == 0 else ps1
                if k >= 2:
                    t.wait_ge(s_ve.h, ve_base + k - 1)
                    t.wait_ge(s_sc.h, sc_base + k - 1)
                for d in range(8):
                    r = t.matmul(ps[:, :], src[:, d, iblk * 128:(iblk + 1) * 128],
                                 src[:, d, jg * 512:(jg + 1) * 512],
                                 start=(d == 0), stop=(d == 7))
                r.then_inc(s_mm.h, 1)
        block.tensor(_t)
        s_mm.inc(16)

        def _v(v):
            for k in range(16):
                iblk, jg = k // 2, k % 2
                ps = ps0 if jg == 0 else ps1
                v.wait_ge(s_mm.h, mm_base + k + 1)
                tgt = Bf if is_b_build else dst
                r = v.tensor_copy(out=tgt[:, iblk, jg * 512:(jg + 1) * 512], in_=ps[:, :])
                r.then_inc(s_ve.h, 1)
        block.vector(_v)
        s_ve.inc(16)

        def _s(s):
            for k in range(16):
                iblk, jg = k // 2, k % 2
                s.wait_ge(s_ve.h, ve_base + k + 1)
                src_sb = (Bf if is_b_build else dst)[:, iblk, jg * 512:(jg + 1) * 512]
                if is_b_build:
                    r = s.activation(out=M0[:, iblk, jg * 512:(jg + 1) * 512], in_=src_sb,
                                     func=AF.Copy, bias=0.0)
                else:
                    r = s.activation(out=junk[:, :], in_=src_sb, func=AF.Abs,
                                     accum_out=acc[:, k:k + 1])
                r.then_inc(s_sc.h, 1)
        block.scalar(_s)
        s_sc.inc(16)

    sq_pass(wt, None, True)

    if SIGMA_STAGE < 2:
        return
    bufs = [M0, M1]
    for it in range(NSQ):
        src, dst = bufs[it % 2], bufs[(it + 1) % 2]
        sq_pass(src, dst, False)

        sc_gate = s_sc.n
        def _v2(v, sc_gate=sc_gate):
            v.wait_ge(s_sc.h, sc_gate)
            r = v.tensor_reduce(out=r1[:, :], in_=acc[:, :], axis=AX.X, op=OP.add)
            r.then_inc(s_ve.h, 1)
        block.vector(_v2)
        s_ve.inc(1)
        ve_g = s_ve.n
        def _t2(t, ve_g=ve_g):
            t.wait_ge(s_ve.h, ve_g)
            r = t.matmul(pv[:1, :], ones_col[:, :], r1[:, :], start=True, stop=True)
            r.then_inc(s_mm.h, 1)
        block.tensor(_t2)
        s_mm.inc(1)
        mm_g = s_mm.n
        def _v3(v, mm_g=mm_g):
            v.wait_ge(s_mm.h, mm_g)
            r = v.tensor_copy(out=ssc[:, :], in_=pv[:1, :])
            r.then_inc(s_ve.h, 1)
        block.vector(_v3)
        s_ve.inc(1)
        ve_g2 = s_ve.n
        def _t3(t, ve_g2=ve_g2):
            t.wait_ge(s_ve.h, ve_g2)
            r = t.matmul(pv2[:, :], ones_row[:1, :], ssc[:1, :], start=True, stop=True)
            r.then_inc(s_mm.h, 1)
        block.tensor(_t3)
        s_mm.inc(1)
        mm_g2 = s_mm.n
        def _v4(v, dst=dst, mm_g2=mm_g2):
            v.wait_ge(s_mm.h, mm_g2)
            v.reciprocal(out=inv1[:, :], in_=pv2[:, :])
            v.drain()
            for c in range(8):
                r = v.tensor_scalar(out=dst[:, c, :], in0=dst[:, c, :],
                                    scalar1=inv1[:, :], scalar2=256.0,
                                    op0=OP.mult, op1=OP.mult)
            r.then_inc(s_ve.h, 1)
        block.vector(_v4)
        s_ve.inc(1)

    Mfin = bufs[NSQ % 2]
    if SIGMA_STAGE < 3:
        return

    for pblk in range(8):
        gate = s_ve.n
        def _t(t, pblk=pblk, gate=gate):
            t.wait_ge(s_ve.h, gate)
            for q in range(8):
                r = t.matmul(pv[:, :], Mfin[:, q, pblk * 128:(pblk + 1) * 128],
                             ones_bf[:, :], start=(q == 0), stop=(q == 7))
            r.then_inc(s_mm.h, 1)
        block.tensor(_t)
        s_mm.inc(1)
        mm_g = s_mm.n
        def _v(v, pblk=pblk, mm_g=mm_g):
            v.wait_ge(s_mm.h, mm_g)
            r = v.tensor_copy(out=vv[:, pblk:pblk + 1], in_=pv[:, :])
            r.then_inc(s_ve.h, 1)
        block.vector(_v)
        s_ve.inc(1)

    if SIGMA_STAGE < 4:
        return
    vbuf, tbuf = vv, tt
    for step in range(NPW + 1):
        for pblk in range(8):
            gate = s_ve.n
            def _t(t, pblk=pblk, vbuf=vbuf, gate=gate):
                t.wait_ge(s_ve.h, gate)
                for q in range(8):
                    r = t.matmul(pv[:, :], Bf[:, q, pblk * 128:(pblk + 1) * 128],
                                 vbuf[:, q:q + 1], start=(q == 0), stop=(q == 7))
                r.then_inc(s_mm.h, 1)
            block.tensor(_t)
            s_mm.inc(1)
            mm_g = s_mm.n
            def _v(v, pblk=pblk, tbuf=tbuf, mm_g=mm_g):
                v.wait_ge(s_mm.h, mm_g)
                r = v.tensor_copy(out=tbuf[:, pblk:pblk + 1], in_=pv[:, :])
                r.then_inc(s_ve.h, 1)
            block.vector(_v)
            s_ve.inc(1)
        vbuf, tbuf = tbuf, vbuf
    vlast, tlast = tbuf, vbuf

    if SIGMA_STAGE < 5:
        return
    def _v5(v):
        v.tensor_tensor(out=prod[:, :], in0=vlast[:, :], in1=tlast[:, :], op=OP.mult)
        r = v.tensor_tensor(out=den_sb[:, :], in0=vlast[:, :], in1=vlast[:, :], op=OP.mult)
        r.then_inc(s_ve.h, 1)
    block.vector(_v5)
    s_ve.inc(1)

    ve_g = s_ve.n
    def _t4(t, ve_g=ve_g):
        t.wait_ge(s_ve.h, ve_g)
        for q in range(8):
            t.matmul(pv[:1, :], ones_col[:, :], prod[:, q:q + 1], start=(q == 0), stop=(q == 7))
        for q in range(8):
            r = t.matmul(pv2[:1, :], ones_col[:, :], den_sb[:, q:q + 1], start=(q == 0), stop=(q == 7))
        r.then_inc(s_mm.h, 1)
    block.tensor(_t4)
    s_mm.inc(1)

    mm_g = s_mm.n
    def _v6(v, mm_g=mm_g):
        v.wait_ge(s_mm.h, mm_g)
        v.tensor_copy(out=n2[:, 0:1], in_=pv[:1, :])
        v.tensor_copy(out=n2[:, 1:2], in_=pv2[:1, :])
        v.drain()
        v.reciprocal(out=n2[:, 1:2], in_=n2[:, 1:2])
        v.drain()
        r = v.tensor_tensor(out=scale_out[:, :], in0=n2[:, 0:1], in1=n2[:, 1:2], op=OP.mult)
        r.then_inc(s_ve.h, 1)
    block.vector(_v6)
    s_ve.inc(1)

    ve_g = s_ve.n
    def _s2(s, ve_g=ve_g):
        s.wait_ge(s_ve.h, ve_g)
        r = s.activation(out=scale_out[:, :], in_=scale_out[:, :], func=AF.Sqrt)
        r.then_inc(s_sc.h, 1)
    block.scalar(_s2)
    s_sc.inc(1)
    sc_g = s_sc.n
    def _v7(v, sc_g=sc_g):
        v.wait_ge(s_sc.h, sc_g)
        r = v.tensor_scalar(out=scale_out[:, :], in0=scale_out[:, :], scalar1=1.0,
                            scalar2=None, op0=OP.max)
        r.then_inc(s_ve.h, 1)
    block.vector(_v7)
    s_ve.inc(1)


U32 = mybir.dt.uint32
TT = 512
NTILE_B = TC // TT
CEXP = 2048
NDUMP = 2048
DEBUG_DUMPS = False
TOKP = TC + NDUMP
BIGS = float(2 ** 24)


def build_program():
    nc = bass.Bass("TRN2", target_bir_lowering=False, detect_race_conditions=False,
                   dynamic_dma_scratch_size=16384)

    xT = nc.dram_tensor("xT", [D, TC], F32, kind="ExternalInput").ap()
    out_tok = nc.dram_tensor("out_tok", [TOKP, D], F32, kind="ExternalOutput").ap()
    _dk = dict(kind="ExternalOutput") if DEBUG_DUMPS else {}
    xn_tok = nc.dram_tensor("xn_tok", [TC, 2 * D], BF16, **_dk).ap()
    comb_tok = nc.dram_tensor("comb_tok", [TC, NE], F32, **_dk).ap()
    idx_tab = nc.dram_tensor("idx_tab", [NE * CEXP, 1], U32, **_dk).ap()
    w_inT = nc.dram_tensor("w_inT", [D, D], F32, kind="ExternalInput").ap()
    w_outT = nc.dram_tensor("w_outT", [D, D], F32, kind="ExternalInput").ap()
    wpT = nc.dram_tensor("wpT", [D, 3 * D], F32, kind="ExternalInput").ap()
    gate_wT = nc.dram_tensor("gate_wT", [D, NE], F32, kind="ExternalInput").ap()
    gate_b = nc.dram_tensor("gate_b", [NE, 1], F32, kind="ExternalInput").ap()
    w13T = nc.dram_tensor("w13T", [NE, D, 2 * FI], F32, kind="ExternalInput").ap()
    w2T = nc.dram_tensor("w2T", [NE, FI, D], F32, kind="ExternalInput").ap()
    w13sT = nc.dram_tensor("w13sT", [D, 2 * FI], F32, kind="ExternalInput").ap()
    w2sT = nc.dram_tensor("w2sT", [FI, D], F32, kind="ExternalInput").ap()
    alpha_log = nc.dram_tensor("alpha_log", [128, 8], F32, kind="ExternalInput").ap()
    delta_bias = nc.dram_tensor("delta_bias", [128, 8], F32, kind="ExternalInput").ap()
    bp = nc.dram_tensor("bp", [128, 24], F32, kind="ExternalInput").ap()
    log_gamma = nc.dram_tensor("log_gamma", [1, 1], F32, kind="ExternalInput").ap()
    eye = nc.dram_tensor("eye", [128, 128], F32, kind="ExternalInput").ap()
    ebase = nc.dram_tensor("ebase", [NE, 1], F32, kind="ExternalInput").ap()

    es0 = ExitStack()
    eye_sb = es0.enter_context(nc.sbuf_tensor("eye_sb", [128, 128], F32))
    eye_bf = es0.enter_context(nc.sbuf_tensor("eye_bf", [128, 128], BF16))
    ones_col = es0.enter_context(nc.sbuf_tensor("ones_col", [128, 1], F32))
    ones_row = es0.enter_context(nc.sbuf_tensor("ones_row", [1, 128], F32))
    ones_bf = es0.enter_context(nc.sbuf_tensor("ones_bf", [128, 1], BF16))
    ones16 = es0.enter_context(nc.sbuf_tensor("ones16", [NE, TT], F32))
    scale1 = es0.enter_context(nc.sbuf_tensor("scale1", [1, 1], F32))
    scale2 = es0.enter_context(nc.sbuf_tensor("scale2", [1, 1], F32))
    attn_scale = es0.enter_context(nc.sbuf_tensor("attn_scale", [128, 1], F32))
    lg_sb = es0.enter_context(nc.sbuf_tensor("lg_sb", [1, 1], F32))
    alpha_n = es0.enter_context(nc.sbuf_tensor("alpha_n", [128, 8], F32))
    db_comb = es0.enter_context(nc.sbuf_tensor("db_comb", [128, 8], F32))
    bp_b = es0.enter_context(nc.sbuf_tensor("bp_b", [128, 8], F32))
    bp_c = es0.enter_context(nc.sbuf_tensor("bp_c", [128, 8], F32))
    gate_b_sb = es0.enter_context(nc.sbuf_tensor("gate_b_sb", [NE, 1], F32))
    gb_bcast = es0.enter_context(nc.sbuf_tensor("gb_bcast", [128, NE], F32))
    ebase_sb = es0.enter_context(nc.sbuf_tensor("ebase_sb", [NE, 1], F32))
    gwt = es0.enter_context(nc.sbuf_tensor("gwt", [128, 8, NE], F32))
    carry = es0.enter_context(nc.sbuf_tensor("carry", [128, 8], F32))
    cnt_vec = es0.enter_context(nc.sbuf_tensor("cnt_vec", [NE, 1], F32))
    tmp8 = es0.enter_context(nc.sbuf_tensor("tmp8", [128, 8], F32))
    cpool = es0.enter_context(nc.sbuf_tensor("cpool", [128, 2], F32))
    Mcol = es0.enter_context(nc.sbuf_tensor("Mcol", [128, 1], F32))
    negM = es0.enter_context(nc.sbuf_tensor("negM", [128, 1], F32))

    sems = {}
    for name in ["ld", "mm", "ve", "sc", "xin", "otok", "isc", "gat", "acc", "wa", "we", "xe",
                 "gs0", "gs1", "gs2", "gs3", "gs4", "gs5", "gs6", "gs7"]:
        sems[name] = Sem(es0, nc, "s_" + name)
    s_ld, s_mm, s_ve, s_sc = sems["ld"], sems["mm"], sems["ve"], sems["sc"]
    s_xin, s_otok, s_isc = sems["xin"], sems["otok"], sems["isc"]
    s_gat, s_acc, s_wa, s_we, s_xe = sems["gat"], sems["acc"], sems["wa"], sems["we"], sems["xe"]
    gslot = [sems[f"gs{j}"] for j in range(8)]

    consts = {"ones_col": ones_col, "ones_row": ones_row, "ones_bf": ones_bf}

    with ExitStack() as esA:
        ztile = esA.enter_context(nc.sbuf_tensor("ztile", [128, 1024], F32))
        idxfill = esA.enter_context(nc.sbuf_tensor("idxfill", [128, 256], U32))
        block = esA.enter_context(nc.Block())

        def _(s):
            s.dma_start(out=eye_sb[:, :], in_=eye).then_inc(s_ld.h, 16)
            s.dma_start(out=lg_sb[:, :], in_=log_gamma).then_inc(s_ld.h, 16)
            s.dma_start(out=alpha_n[:, :], in_=alpha_log).then_inc(s_ld.h, 16)
            s.dma_start(out=db_comb[:, :], in_=delta_bias).then_inc(s_ld.h, 16)
            s.dma_start(out=tmp8[:, :], in_=bp[:, 0:8]).then_inc(s_ld.h, 16)
            s.dma_start(out=bp_b[:, :], in_=bp[:, 8:16]).then_inc(s_ld.h, 16)
            s.dma_start(out=bp_c[:, :], in_=bp[:, 16:24]).then_inc(s_ld.h, 16)
            s.dma_start(out=gate_b_sb[:, :], in_=gate_b).then_inc(s_ld.h, 16)
            s.dma_start(out=gwt[:, :, :], in_=gate_wT.rearrange("(c p) e -> p c e", p=128)).then_inc(s_ld.h, 16)
            s.dma_start(out=ebase_sb[:, :], in_=ebase).then_inc(s_ld.h, 16)
        block.sync(_)
        s_ld.inc(16 * 10)

        ld_g = s_ld.n
        def _(v, ld_g=ld_g):
            v.wait_ge(s_ld.h, ld_g)
            v.memset(ones_col[:, :], 1.0)
            v.memset(ones_row[:, :], 1.0)
            v.memset(ones_bf[:, :], 1.0)
            v.memset(ones16[:, :], 1.0)
            v.memset(carry[:, :], 0.0)
            v.memset(cnt_vec[:, :], 0.0)
            v.memset(ztile[:, :], 0.0)
            v.memset(cpool[:, 0:1], 100000.0)
            v.drain()
            v.reciprocal(out=cpool[:, 0:1], in_=cpool[:, 0:1])
            v.drain()
            v.memset(cpool[:, 1:2], 1e-12)
            v.memset(Mcol[:, :], 1.0)
            v.drain()
            v.tensor_scalar(out=Mcol[:, :], in0=Mcol[:, :], scalar1=float(2.0 ** -29),
                            scalar2=1.0, op0=OP.add, op1=OP.subtract)
            v.drain()
            v.tensor_scalar(out=Mcol[:, :], in0=Mcol[:, :], scalar1=float(2.0 ** 58),
                            scalar2=None, op0=OP.mult)
            v.drain()
            v.tensor_scalar(out=negM[:, :], in0=Mcol[:, :], scalar1=-1.0, scalar2=None, op0=OP.mult)
            v.drain()
            r = v.tensor_copy(out=eye_bf[:, :], in_=eye_sb[:, :])
            r.then_inc(s_ve.h, 1)
        block.vector(_)
        s_ve.inc(1)
        def _(s, ld_g=ld_g):
            s.wait_ge(s_ld.h, ld_g)
            s.activation(out=alpha_n[:, :], in_=alpha_n[:, :], func=AF.Exp)
            s.add(alpha_n[:, :], alpha_n[:, :], 1.0)
            s.activation(out=alpha_n[:, :], in_=alpha_n[:, :], func=AF.Ln)
            r = s.mul(alpha_n[:, :], alpha_n[:, :], -1.0)
            r.then_inc(s_sc.h, 1)
        block.scalar(_)
        s_sc.inc(1)
        def _(v):
            r = v.tensor_tensor(out=db_comb[:, :], in0=db_comb[:, :], in1=tmp8[:, :], op=OP.add)
            r.then_inc(s_ve.h, 1)
        block.vector(_)
        s_ve.inc(1)

        ve_g = s_ve.n
        regbox = {}
        def _(g, ve_g=ve_g):
            regbox["slotbnd"] = g.alloc_register("slotbnd")
            g.reg_mov(regbox["slotbnd"], NE * CEXP - 1)
            regbox["tokbnd"] = g.alloc_register("tokbnd")
            g.reg_mov(regbox["tokbnd"], TC - 1)
            g.wait_ge(s_ve.h, ve_g)
            for r0 in range(TC, TOKP, 128):
                g.dma_start(out=out_tok[r0:r0 + 128, :], in_=ztile[:, :]).then_inc(s_otok.h, 16)
            g.iota(idxfill[:, :], [[1, 16], [0, 16]], base=TC, channel_multiplier=16, allow_small_or_imprecise_dtypes=True)
            g.dma_start(out=idx_tab.rearrange("(p c) x -> p (c x)", p=128),
                        in_=idxfill[:, :]).then_inc(s_isc.h, 16)
        block.gpsimd(_)
        s_otok.inc(16 * (NDUMP // 128))
        s_isc.inc(16)

        with ExitStack() as esS:
            emit_sigma(nc, block, esS, w_inT, scale1, consts, sems, "a")
        with ExitStack() as esS:
            emit_sigma(nc, block, esS, w_outT, scale2, consts, sems, "b")

        with ExitStack() as esS:
            pvb = esS.enter_context(nc.psum_tensor("pvb", [128, 1], F32))
            n2gb = esS.enter_context(nc.sbuf_tensor("n2gb", [1, NE], F32))
            pgb = esS.enter_context(nc.psum_tensor("pgb", [128, NE], F32))
            ve_g = s_ve.n
            def _(s, ve_g=ve_g):
                s.wait_ge(s_ve.h, ve_g)
                r = s.activation(out=lg_sb[:, :], in_=lg_sb[:, :], func=AF.Exp)
                r.then_inc(s_sc.h, 1)
            block.scalar(_)
            s_sc.inc(1)
            sc_g = s_sc.n
            def _(v, sc_g=sc_g):
                v.wait_ge(s_sc.h, sc_g)
                v.tensor_tensor(out=scale1[:, :], in0=scale1[:, :], in1=scale2[:, :], op=OP.mult)
                v.drain()
                v.reciprocal(out=scale1[:, :], in_=scale1[:, :])
                v.drain()
                r = v.tensor_tensor(out=scale1[:, :], in0=scale1[:, :], in1=lg_sb[:, :], op=OP.mult)
                r.then_inc(s_ve.h, 1)
            block.vector(_)
            s_ve.inc(1)
            ve_g = s_ve.n
            def _(t, ve_g=ve_g):
                t.wait_ge(s_ve.h, ve_g)
                t.matmul(pvb[:, :], ones_row[:1, :], scale1[:1, :], start=True, stop=True).then_inc(s_mm.h, 1)
                r = t.matmul(pgb[:1, :NE], gate_b_sb[:, :], eye_sb[:NE, :NE], start=True, stop=True)
                r.then_inc(s_mm.h, 1)
            block.tensor(_)
            s_mm.inc(2)
            mm_g = s_mm.n
            def _(v, mm_g=mm_g):
                v.wait_ge(s_mm.h, mm_g)
                v.tensor_copy(out=attn_scale[:, :], in_=pvb[:, :])
                r = v.tensor_copy(out=n2gb[:1, :], in_=pgb[:1, :NE])
                r.then_inc(s_ve.h, 1)
            block.vector(_)
            s_ve.inc(1)
            ve_g = s_ve.n
            def _(t, ve_g=ve_g):
                t.wait_ge(s_ve.h, ve_g)
                r = t.matmul(pgb[:, :], ones_row[:1, :], n2gb[:1, :], start=True, stop=True)
                r.then_inc(s_mm.h, 1)
            block.tensor(_)
            s_mm.inc(1)
            mm_g = s_mm.n
            def _(v, mm_g=mm_g):
                v.wait_ge(s_mm.h, mm_g)
                r = v.tensor_copy(out=gb_bcast[:, :], in_=pgb[:, :])
                r.then_inc(s_ve.h, 1)
            block.vector(_)
            s_ve.inc(1)

    esB = ExitStack()
    x_sb = esB.enter_context(nc.sbuf_tensor("x_sb", [128, 8, TT], F32))
    xnh = esB.enter_context(nc.sbuf_tensor("xnh", [128, 8, TT], BF16))
    xnl = esB.enter_context(nc.sbuf_tensor("xnl", [128, 8, TT], BF16))
    Ra = esB.enter_context(nc.sbuf_tensor("Ra", [128, 8, TT], F32))
    Rb = esB.enter_context(nc.sbuf_tensor("Rb", [128, 8, TT], F32))
    Rc = esB.enter_context(nc.sbuf_tensor("Rc", [128, 8, TT], F32))
    scr = esB.enter_context(nc.sbuf_tensor("scr", [128, 8 * TT], F32))
    sgad = esB.enter_context(nc.sbuf_tensor("sgad", [128, 2, TT], F32))
    pland = [esB.enter_context(nc.sbuf_tensor(f"pland{i}", [128, 8, 256], F32)) for i in range(2)]
    phl = [(esB.enter_context(nc.sbuf_tensor(f"ph{i}", [128, 8, 256], BF16)),
            esB.enter_context(nc.sbuf_tensor(f"pl{i}", [128, 8, 256], BF16))) for i in range(2)]
    stok = esB.enter_context(nc.sbuf_tensor("stok", [128, 4, NE], F32))
    btok = esB.enter_context(nc.sbuf_tensor("btok", [128, 4, NE], F32))
    rtok = esB.enter_context(nc.sbuf_tensor("rtok", [128, 4, NE], F32))
    thr8 = esB.enter_context(nc.sbuf_tensor("thr8", [128, 4], F32))
    invd8 = esB.enter_context(nc.sbuf_tensor("invd8", [128, 4], F32))
    mske = esB.enter_context(nc.sbuf_tensor("mske", [128, 4, NE], F32))
    slots_f = esB.enter_context(nc.sbuf_tensor("slots_f", [128, 4, NE], F32))
    slots_u = esB.enter_context(nc.sbuf_tensor("slots_u", [128, 4 * NE], U32))
    ids_u = esB.enter_context(nc.sbuf_tensor("ids_u", [128, 4], U32))
    tth = [esB.enter_context(nc.sbuf_tensor(f"tth{i}", [128, 1024], BF16)) for i in range(2)]
    ttl = [esB.enter_context(nc.sbuf_tensor(f"ttl{i}", [128, 1024], BF16)) for i in range(2)]
    otok = [esB.enter_context(nc.sbuf_tensor(f"otok{i}", [128, 1024], F32)) for i in range(2)]
    P = [esB.enter_context(nc.psum_tensor(f"P{i}", [128, 512], F32)) for i in range(8)]
    blockB = esB.enter_context(nc.Block())
    block = blockB

    stats = [Rc[:1, 0, 0:512], Rc[:1, 1, 0:512], Rc[:1, 2, 0:512], Rc[:1, 3, 0:512], Rc[:1, 4, 0:512]]
    scores_em = Rc[0:NE, 5, :]
    comb_em = Rc[0:NE, 6, :]
    m01 = Rc[0:NE, 7, :]
    incl_t = Rc[0:NE, 0, :]
    srel_t = Rc[0:NE, 1, :]
    tmp_t = Rc[0:NE, 2, :]

    def mm3(t, ps, wh, wl, xh, xl, first, last, inc=None):
        t.matmul(ps, wh, xh, start=first, stop=False)
        t.matmul(ps, wh, xl, start=False, stop=False)
        r = t.matmul(ps, wl, xh, start=False, stop=last)
        if inc:
            r.then_inc(s_mm.h, 1)
        return r

    for j in range(NTILE_B):
        t0 = j * TT

        mm_pre = s_mm.n
        def _(g, t0=t0, mm_pre=mm_pre):
            g.wait_ge(s_mm.h, mm_pre)
            g.dma_start(out=x_sb[:, :, :],
                        in_=xT[:, t0:t0 + TT].rearrange("(c p) t -> p c t", p=128)).then_inc(s_xin.h, 16)
        block.gpsimd(_)
        s_xin.inc(16)

        panels = ([wpT[:, p * 256:(p + 1) * 256] for p in range(12)]
                  + [w_inT[:, p * 256:(p + 1) * 256] for p in range(4)]
                  + [w_outT[:, p * 256:(p + 1) * 256] for p in range(4)])
        wa_base = s_wa.n
        panel_conv_ve = [None] * 20

        def emit_panel(pi, gate_mm):
            slot = pi % 2
            def _(s, pi=pi, gate_mm=gate_mm, slot=slot):
                if gate_mm is not None:
                    s.wait_ge(s_mm.h, gate_mm)
                s.wait_ge(s_ve.h, s_ve.n)
                s.dma_start(out=pland[slot][:, :, :],
                            in_=panels[pi].rearrange("(c p) n -> p c n", p=128)).then_inc(s_wa.h, 16)
            block.sync(_)
            s_wa.inc(16)
            wa_g = s_wa.n
            def _(v, pi=pi, slot=slot, wa_g=wa_g):
                v.wait_ge(s_wa.h, wa_g)
                v.tensor_copy(out=phl[slot][0][:, :, :], in_=pland[slot][:, :, :])
                r = v.tensor_tensor(out=phl[slot][1][:, :, :], in0=pland[slot][:, :, :],
                                    in1=phl[slot][0][:, :, :], op=OP.subtract)
                r.then_inc(s_ve.h, 1)
            block.vector(_)
            s_ve.inc(1)
            panel_conv_ve[pi] = s_ve.n

        emit_panel(0, s_mm.n)
        emit_panel(1, s_mm.n)
        pi_loaded = 2

        xin_gate = s_xin.n
        ve_pre = s_ve.n
        def _(s, xin_gate=xin_gate, ve_pre=ve_pre):
            s.wait_ge(s_xin.h, xin_gate)
            s.wait_ge(s_ve.h, ve_pre)
            for c in range(8):
                r = s.activation(out=scr[:, c * TT:(c + 1) * TT], in_=x_sb[:, c, :], func=AF.Square)
            r.then_inc(s_sc.h, 1)
        block.scalar(_)
        s_sc.inc(1)
        sc_sq = s_sc.n
        ve_pre = s_ve.n
        def _(t, sc_sq=sc_sq, ve_pre=ve_pre, xin_gate=xin_gate):
            t.wait_ge(s_xin.h, xin_gate)
            t.wait_ge(s_sc.h, sc_sq)
            t.wait_ge(s_ve.h, ve_pre)
            for c in range(8):
                t.matmul(P[0][:1, :], ones_col[:, :], x_sb[:, c, :], start=(c == 0), stop=(c == 7))
            for c in range(8):
                r = t.matmul(P[1][:1, :], ones_col[:, :], scr[:, c * TT:(c + 1) * TT],
                             start=(c == 0), stop=(c == 7))
            r.then_inc(s_mm.h, 1)
        block.tensor(_)
        s_mm.inc(1)
        mm_g = s_mm.n
        def _(v, mm_g=mm_g):
            v.wait_ge(s_mm.h, mm_g)
            v.tensor_scalar(out=stats[0], in0=P[0][:1, :], scalar1=1.0 / D, scalar2=None, op0=OP.mult)
            v.tensor_scalar(out=stats[1], in0=P[1][:1, :], scalar1=1.0 / D, scalar2=None, op0=OP.mult)
            v.drain()
            v.tensor_tensor(out=stats[2], in0=stats[0], in1=stats[0], op=OP.mult)
            v.drain()
            r = v.tensor_tensor(out=stats[1], in0=stats[1], in1=stats[2], op=OP.subtract)
            r.then_inc(s_ve.h, 1)
        block.vector(_)
        s_ve.inc(1)
        ve_g = s_ve.n
        def _(s, ve_g=ve_g):
            s.wait_ge(s_ve.h, ve_g)
            r = s.activation(out=stats[2], in_=stats[1], func=AF.Sqrt, bias=cpool[:1, 0:1])
            r.then_inc(s_sc.h, 1)
        block.scalar(_)
        s_sc.inc(1)
        sc_g = s_sc.n
        def _(v, sc_g=sc_g):
            v.wait_ge(s_sc.h, sc_g)
            v.reciprocal(out=stats[3], in_=stats[2])
            v.drain()
            r = v.tensor_tensor(out=stats[4], in0=stats[0], in1=stats[3], op=OP.mult)
            r.then_inc(s_ve.h, 1)
        block.vector(_)
        s_ve.inc(1)
        ve_g = s_ve.n
        def _(t, ve_g=ve_g):
            t.wait_ge(s_ve.h, ve_g)
            t.matmul(P[2][:, :], ones_row[:1, :], stats[3], start=True, stop=True)
            r = t.matmul(P[3][:, :], ones_row[:1, :], stats[4], start=True, stop=True)
            r.then_inc(s_mm.h, 1)
        block.tensor(_)
        s_mm.inc(1)
        mm_g = s_mm.n
        def _(v, mm_g=mm_g):
            v.wait_ge(s_mm.h, mm_g)
            for c in range(8):
                sl = slice(c * TT, (c + 1) * TT)
                v.tensor_tensor(out=scr[:, sl], in0=x_sb[:, c, :], in1=P[2][:, :], op=OP.mult)
                v.tensor_tensor(out=scr[:, sl], in0=scr[:, sl], in1=P[3][:, :], op=OP.subtract)
            for c in range(8):
                sl = slice(c * TT, (c + 1) * TT)
                v.tensor_copy(out=xnh[:, c, :], in_=scr[:, sl])
                r = v.tensor_tensor(out=xnl[:, c, :], in0=scr[:, sl], in1=xnh[:, c, :], op=OP.subtract)
            r.then_inc(s_ve.h, 1)
        block.vector(_)
        s_ve.inc(1)
        xn_ready = s_ve.n

        gate_mm_base = s_mm.n
        for c in range(8):
            mm_pre = s_mm.n
            def _(v, c=c, mm_pre=mm_pre, gate_mm_base=gate_mm_base):
                if c < 2:
                    v.wait_ge(s_mm.h, mm_pre)
                else:
                    v.wait_ge(s_mm.h, gate_mm_base + c - 1)
                sl = slice(c * TT, (c + 1) * TT)
                v.scalar_tensor_tensor(out=sgad[:, c % 2, :], in0=scr[:, sl], scalar=Mcol[:, :],
                                       in1=scr[:, sl], op0=OP.mult, op1=OP.add)
                r = v.scalar_tensor_tensor(out=sgad[:, c % 2, :], in0=scr[:, sl], scalar=negM[:, :],
                                           in1=sgad[:, c % 2, :], op0=OP.mult, op1=OP.add)
                r.then_inc(s_ve.h, 1)
            block.vector(_)
            s_ve.inc(1)
            ve_g = s_ve.n
            sc_pre = s_sc.n
            def _(t, c=c, ve_g=ve_g, sc_pre=sc_pre):
                t.wait_ge(s_ve.h, ve_g)
                if c == 0:
                    t.wait_ge(s_sc.h, sc_pre)
                r = t.matmul(P[4][:NE, :], gwt[:, c, :], sgad[:, c % 2, :],
                             start=(c == 0), stop=(c == 7))
                r.then_inc(s_mm.h, 1)
            block.tensor(_)
            s_mm.inc(1)
        mm_g = s_mm.n
        def _(s, mm_g=mm_g):
            s.wait_ge(s_mm.h, mm_g)
            r = s.activation(out=scores_em, in_=P[4][:NE, :], func=AF.Sigmoid)
            r.then_inc(s_sc.h, 1)
        block.scalar(_)
        s_sc.inc(1)

        sc_scores = s_sc.n
        for sub in range(4):
            sc_g = s_sc.n
            def _(t, sub=sub, sc_g=sc_g):
                t.wait_ge(s_sc.h, sc_scores)
                if sub > 0:
                    t.wait_ge(s_sc.h, sc_g)
                r = t.matmul(P[5][:, :NE], scores_em[:, sub * 128:(sub + 1) * 128], eye_sb[:NE, :NE], start=True, stop=True)
                r.then_inc(s_mm.h, 1)
            block.tensor(_)
            s_mm.inc(1)
            mm_g = s_mm.n
            def _(s, sub=sub, mm_g=mm_g):
                s.wait_ge(s_mm.h, mm_g)
                r = s.copy(out=stok[:, sub, :], in_=P[5][:, :NE])
                r.then_inc(s_sc.h, 1)
            block.scalar(_)
            s_sc.inc(1)
        sc_routing = s_sc.n
        xe_pre_comb = s_xe.n
        def _routing(v, sc_routing=sc_routing, xe_pre_comb=xe_pre_comb):
            v.wait_ge(s_sc.h, sc_routing)
            v.wait_ge(s_xe.h, xe_pre_comb)
            for sub in range(4):
                v.tensor_tensor(out=btok[:, sub, :], in0=stok[:, sub, :], in1=gb_bcast[:, :], op=OP.add)
            for sub in range(4):
                v.tensor_copy(out=rtok[:, sub, :], in_=btok[:, sub, :])
            for rnd in range(4):
                for sub in range(4):
                    v.tensor_reduce(out=thr8[:, sub:sub + 1], in_=rtok[:, sub, :], axis=AX.X, op=OP.max)
                if rnd == 1:
                    v.drain()
                if rnd < 3:
                    for sub in range(4):
                        v.tensor_scalar(out=mske[:, sub, :], in0=rtok[:, sub, :],
                                        scalar1=thr8[:, sub:sub + 1], scalar2=None, op0=OP.is_ge)
                    for sub in range(4):
                        v.scalar_tensor_tensor(out=rtok[:, sub, :], in0=mske[:, sub, :], scalar=-1e30,
                                               in1=rtok[:, sub, :], op0=OP.mult, op1=OP.add)
                else:
                    v.drain()
            for sub in range(4):
                v.tensor_scalar(out=mske[:, sub, :], in0=btok[:, sub, :],
                                scalar1=thr8[:, sub:sub + 1], scalar2=None, op0=OP.is_ge)
            for sub in range(4):
                v.tensor_tensor(out=rtok[:, sub, :], in0=stok[:, sub, :], in1=mske[:, sub, :], op=OP.mult)
            for sub in range(4):
                v.tensor_reduce(out=invd8[:, sub:sub + 1], in_=rtok[:, sub, :], axis=AX.X, op=OP.add)
            v.drain()
            v.reciprocal(out=invd8[:, :], in_=invd8[:, :])
            v.drain()
            for sub in range(4):
                v.tensor_scalar(out=btok[:, sub, :], in0=rtok[:, sub, :], scalar1=invd8[:, sub:sub + 1],
                                scalar2=2.5, op0=OP.mult, op1=OP.mult)
            v.drain()
            v.tensor_copy(out=rtok[:, :, :], in_=btok[:, :, :])
            v.drain()
            v.scalar_tensor_tensor(out=btok[:, :, :], in0=rtok[:, :, :], scalar=Mcol[:, :],
                                   in1=rtok[:, :, :], op0=OP.mult, op1=OP.add)
            v.drain()
            r = v.scalar_tensor_tensor(out=btok[:, :, :], in0=rtok[:, :, :], scalar=negM[:, :],
                                       in1=btok[:, :, :], op0=OP.mult, op1=OP.add)
            r.then_inc(s_ve.h, 1)
        block.vector(_routing)
        s_ve.inc(1)
        comb_ready_ve = s_ve.n

        def _(s, t0=t0, comb_ready_ve=comb_ready_ve):
            s.wait_ge(s_ve.h, comb_ready_ve)
            s.dma_start(out=comb_tok[t0:t0 + TT, :].rearrange("(s p) e -> p s e", p=128),
                        in_=btok[:, :, :]).then_inc(s_xe.h, 16)
        block.sync(_)
        s_xe.inc(16)

        for sub in range(4):
            sc_g = s_sc.n
            def _(t, sub=sub, sc_g=sc_g):
                t.wait_ge(s_ve.h, comb_ready_ve)
                if sub > 0:
                    t.wait_ge(s_sc.h, sc_g)
                r = t.matmul(P[6][:NE, :128], btok[:, sub, :], eye_sb[:, :], start=True, stop=True)
                r.then_inc(s_mm.h, 1)
            block.tensor(_)
            s_mm.inc(1)
            mm_g = s_mm.n
            def _(s, sub=sub, mm_g=mm_g):
                s.wait_ge(s_mm.h, mm_g)
                r = s.copy(out=comb_em[:, sub * 128:(sub + 1) * 128], in_=P[6][:NE, :128])
                r.then_inc(s_sc.h, 1)
            block.scalar(_)
            s_sc.inc(1)
        sc_g = s_sc.n
        def _(v, sc_g=sc_g):
            v.wait_ge(s_sc.h, sc_g)
            v.tensor_scalar(out=m01, in0=comb_em, scalar1=0.0, scalar2=None, op0=OP.is_gt)
            v.tensor_tensor_scan(out=incl_t, data0=ones16[:, :], data1=m01,
                                 initial=cnt_vec[:, :], op0=OP.mult, op1=OP.add)
            v.tensor_scalar(out=srel_t, in0=incl_t, scalar1=-1.0, scalar2=None, op0=OP.add)
            v.drain()
            v.tensor_copy(out=cnt_vec[:, :], in_=incl_t[:, TT - 1:TT])
            v.tensor_scalar(out=tmp_t, in0=srel_t, scalar1=float(CEXP), scalar2=None, op0=OP.is_lt)
            v.tensor_tensor(out=m01, in0=m01, in1=tmp_t, op=OP.mult)
            v.tensor_scalar(out=srel_t, in0=srel_t, scalar1=ebase_sb[:NE, :], scalar2=-BIGS,
                            op0=OP.add, op1=OP.add)
            v.tensor_tensor(out=srel_t, in0=srel_t, in1=m01, op=OP.mult)
            r = v.tensor_scalar(out=srel_t, in0=srel_t, scalar1=BIGS, scalar2=None, op0=OP.add)
            r.then_inc(s_ve.h, 1)
        block.vector(_)
        s_ve.inc(1)
        for sub in range(4):
            ve_g = s_ve.n
            sc_g = s_sc.n
            def _(t, sub=sub, ve_g=ve_g, sc_g=sc_g):
                t.wait_ge(s_ve.h, ve_g)
                if sub > 0:
                    t.wait_ge(s_sc.h, sc_g)
                r = t.matmul(P[7][:, :NE], srel_t[:, sub * 128:(sub + 1) * 128], eye_sb[:NE, :NE], start=True, stop=True)
                r.then_inc(s_mm.h, 1)
            block.tensor(_)
            s_mm.inc(1)
            mm_g = s_mm.n
            def _(s, sub=sub, mm_g=mm_g):
                s.wait_ge(s_mm.h, mm_g)
                r = s.copy(out=slots_f[:, sub, :], in_=P[7][:, :NE])
                r.then_inc(s_sc.h, 1)
            block.scalar(_)
            s_sc.inc(1)
        sc_g = s_sc.n
        isc_pre = s_isc.n
        def _(v, sc_g=sc_g, isc_pre=isc_pre):
            v.wait_ge(s_sc.h, sc_g)
            v.wait_ge(s_isc.h, isc_pre)
            r = v.tensor_copy(out=slots_u[:, :], in_=slots_f[:, :, :])
            r.then_inc(s_ve.h, 1)
        block.vector(_)
        s_ve.inc(1)
        ve_slots = s_ve.n
        isc_base = s_isc.n
        def _(g, t0=t0, ve_slots=ve_slots, isc_base=isc_base):
            g.wait_ge(s_ve.h, ve_slots)
            g.wait_ge(s_isc.h, isc_base)
            g.iota(ids_u[:, :], [[128, 4]], base=t0, channel_multiplier=1,
                   allow_small_or_imprecise_dtypes=True)
            n = 0
            for sub in range(4):
                for e in range(NE):
                    if n and n % 8 == 0:
                        g.wait_ge(s_isc.h, isc_base + 16 * (n - 8))
                    g.indirect_dma_start(
                        out=idx_tab,
                        out_offset=bass.IndirectOffsetOnAxis(ap=slots_u[:, sub * NE + e:sub * NE + e + 1], axis=0),
                        in_=ids_u[:, sub:sub + 1], in_offset=None,
                        bounds_check=regbox["slotbnd"], oob_is_err=False,
                    ).then_inc(s_isc.h, 16)
                    n += 1
        block.gpsimd(_)
        s_isc.inc(16 * 64)

        for sub in range(4):
            par = sub % 2
            xe_pre = s_xe.n
            for half in range(2):
                cs = range(half * 4, half * 4 + 4)
                sc_g = s_sc.n
                def _(t, sub=sub, half=half, cs=cs, sc_g=sc_g):
                    t.wait_ge(s_ve.h, xn_ready)
                    t.wait_ge(s_sc.h, sc_g)
                    for i, c in enumerate(cs):
                        t.matmul(P[5][:, i * 128:(i + 1) * 128], xnh[:, c, sub * 128:(sub + 1) * 128],
                                 eye_bf[:, :], start=(i == 0), stop=(i == 3))
                        r = t.matmul(P[6][:, i * 128:(i + 1) * 128], xnl[:, c, sub * 128:(sub + 1) * 128],
                                     eye_bf[:, :], start=(i == 0), stop=(i == 3))
                    r.then_inc(s_mm.h, 1)
                block.tensor(_)
                s_mm.inc(1)
                mm_g = s_mm.n
                def _(s, sub=sub, half=half, par=par, mm_g=mm_g, xe_pre=xe_pre):
                    s.wait_ge(s_mm.h, mm_g)
                    if half == 0:
                        s.wait_ge(s_xe.h, xe_pre)
                    s.copy(out=tth[par][:, half * 512:(half + 1) * 512], in_=P[5][:, :])
                    r = s.copy(out=ttl[par][:, half * 512:(half + 1) * 512], in_=P[6][:, :])
                    r.then_inc(s_sc.h, 1)
                block.scalar(_)
                s_sc.inc(1)
            sc_g = s_sc.n
            def _(s, sub=sub, par=par, t0=t0, sc_g=sc_g):
                s.wait_ge(s_sc.h, sc_g)
                r0 = t0 + sub * 128
                s.dma_start(out=xn_tok[r0:r0 + 128, 0:D], in_=tth[par][:, :]).then_inc(s_xe.h, 16)
                s.dma_start(out=xn_tok[r0:r0 + 128, D:2 * D], in_=ttl[par][:, :]).then_inc(s_xe.h, 16)
            block.sync(_)
            s_xe.inc(32)

        for pi in range(12):
            wa_gate = panel_conv_ve[pi]
            slot = pi % 2
            for blk in range(2):
                kblk = pi * 2 + blk
                sect = kblk // 8
                loc = kblk % 8
                pband = P[2 * (kblk % 2)]
                ve_g = s_ve.n
                sc_g = s_sc.n
                def _(t, pi=pi, blk=blk, slot=slot, wa_gate=wa_gate, pband=pband, ve_g=ve_g, sc_g=sc_g):
                    t.wait_ge(s_ve.h, max(wa_gate, ve_g))
                    t.wait_ge(s_sc.h, sc_g)
                    for c in range(8):
                        mm3(t, pband[:, :],
                            phl[slot][0][:, c, blk * 128:(blk + 1) * 128],
                            phl[slot][1][:, c, blk * 128:(blk + 1) * 128],
                            xnh[:, c, :], xnl[:, c, :],
                            first=(c == 0), last=(c == 7), inc=(c == 7))
                block.tensor(_)
                s_mm.inc(1)
                mm_g = s_mm.n
                if sect == 0:
                    def _(s, loc=loc, pband=pband, mm_g=mm_g):
                        s.wait_ge(s_mm.h, mm_g)
                        s.activation(out=scr[:, 0:TT], in_=pband[:, :], func=AF.Exp,
                                     bias=db_comb[:, loc:loc + 1])
                        s.add(scr[:, 0:TT], scr[:, 0:TT], 1.0)
                        s.activation(out=scr[:, 0:TT], in_=scr[:, 0:TT], func=AF.Ln)
                        r = s.activation(out=Ra[:, loc, :], in_=scr[:, 0:TT],
                                         func=AF.Exp, scale=alpha_n[:, loc:loc + 1])
                        r.then_inc(s_sc.h, 1)
                    block.scalar(_)
                    s_sc.inc(1)
                    sc_g = s_sc.n
                    def _(v, loc=loc, sc_g=sc_g):
                        v.wait_ge(s_sc.h, sc_g)
                        r = v.tensor_scalar(out=Ra[:, loc, :], in0=Ra[:, loc, :],
                                            scalar1=1.0 - EPS_A, scalar2=None, op0=OP.min)
                        r.then_inc(s_ve.h, 1)
                    block.vector(_)
                    s_ve.inc(1)
                else:
                    dst = Rb if sect == 1 else Rc
                    bias_t = bp_b if sect == 1 else bp_c
                    def _(s, loc=loc, pband=pband, dst=dst, bias_t=bias_t, mm_g=mm_g):
                        s.wait_ge(s_mm.h, mm_g)
                        r = s.activation(out=dst[:, loc, :], in_=pband[:, :],
                                         func=AF.Tanh, bias=bias_t[:, loc:loc + 1])
                        r.then_inc(s_sc.h, 1)
                    block.scalar(_)
                    s_sc.inc(1)
            if pi_loaded < 20:
                emit_panel(pi_loaded, s_mm.n)
                pi_loaded += 1

        for c in range(8):
            sc_g = s_sc.n
            def _(v, c=c, sc_g=sc_g):
                v.wait_ge(s_sc.h, sc_g)
                a_, b_, c_ = Ra[:, c, :], Rb[:, c, :], Rc[:, c, :]
                p_, r_, q_, d_ = scr[:, 512:1024], scr[:, 1024:1536], scr[:, 1536:2048], scr[:, 2048:2560]
                v.tensor_tensor(out=p_, in0=a_, in1=a_, op=OP.mult)
                v.tensor_tensor(out=q_, in0=c_, in1=c_, op=OP.mult)
                v.tensor_tensor(out=p_, in0=p_, in1=q_, op=OP.add)
                v.tensor_tensor(out=r_, in0=b_, in1=b_, op=OP.mult)
                v.tensor_tensor(out=q_, in0=a_, in1=b_, op=OP.mult)
                v.tensor_tensor(out=d_, in0=p_, in1=r_, op=OP.subtract)
                v.tensor_tensor(out=d_, in0=d_, in1=d_, op=OP.mult)
                v.tensor_tensor(out=q_, in0=q_, in1=q_, op=OP.mult)
                v.tensor_scalar(out=q_, in0=q_, scalar1=4.0, scalar2=None, op0=OP.mult)
                v.tensor_tensor(out=d_, in0=d_, in1=q_, op=OP.add)
                r2 = v.tensor_tensor(out=p_, in0=p_, in1=r_, op=OP.add)
                r2.then_inc(s_ve.h, 1)
            block.vector(_)
            s_ve.inc(1)
            ve_g = s_ve.n
            def _(s, ve_g=ve_g):
                s.wait_ge(s_ve.h, ve_g)
                r2 = s.activation(out=scr[:, 2560:3072], in_=scr[:, 2048:2560], func=AF.Sqrt,
                                  bias=cpool[:, 1:2])
                r2.then_inc(s_sc.h, 1)
            block.scalar(_)
            s_sc.inc(1)
            sc_g = s_sc.n
            def _(v, sc_g=sc_g):
                v.wait_ge(s_sc.h, sc_g)
                r2 = v.tensor_tensor(out=scr[:, 512:1024], in0=scr[:, 512:1024],
                                     in1=scr[:, 2560:3072], op=OP.add)
                r2.then_inc(s_ve.h, 1)
            block.vector(_)
            s_ve.inc(1)
            ve_g = s_ve.n
            def _(s, ve_g=ve_g):
                s.wait_ge(s_ve.h, ve_g)
                r2 = s.activation(out=scr[:, 512:1024], in_=scr[:, 512:1024], func=AF.Sqrt,
                                  scale=0.5, bias=cpool[:, 1:2])
                r2.then_inc(s_sc.h, 1)
            block.scalar(_)
            s_sc.inc(1)
            sc_g = s_sc.n
            def _(v, c=c, sc_g=sc_g):
                v.wait_ge(s_sc.h, sc_g)
                a_, b_, c_ = Ra[:, c, :], Rb[:, c, :], Rc[:, c, :]
                v.tensor_scalar(out=scr[:, 512:1024], in0=scr[:, 512:1024], scalar1=1.0,
                                scalar2=None, op0=OP.max)
                v.drain()
                v.reciprocal(out=scr[:, 1024:1536], in_=scr[:, 512:1024])
                v.drain()
                v.tensor_tensor(out=a_, in0=a_, in1=scr[:, 1024:1536], op=OP.mult)
                v.tensor_tensor(out=b_, in0=b_, in1=scr[:, 1024:1536], op=OP.mult)
                r2 = v.tensor_tensor(out=c_, in0=c_, in1=scr[:, 1024:1536], op=OP.mult)
                r2.then_inc(s_ve.h, 1)
            block.vector(_)
            s_ve.inc(1)

        for pp in range(4):
            pi = 12 + pp
            slot = pi % 2
            wa_gate = panel_conv_ve[pi]
            for blk in range(2):
                nblk = pp * 2 + blk
                pband = P[4 + (nblk % 2)]
                ve_g = s_ve.n
                def _(t, pi=pi, blk=blk, slot=slot, wa_gate=wa_gate, pband=pband, ve_g=ve_g):
                    t.wait_ge(s_ve.h, max(wa_gate, ve_g))
                    for c in range(8):
                        mm3(t, pband[:, :],
                            phl[slot][0][:, c, blk * 128:(blk + 1) * 128],
                            phl[slot][1][:, c, blk * 128:(blk + 1) * 128],
                            xnh[:, c, :], xnl[:, c, :],
                            first=(c == 0), last=(c == 7), inc=(c == 7))
                block.tensor(_)
                s_mm.inc(1)
                mm_g = s_mm.n
                def _(v, nblk=nblk, pband=pband, mm_g=mm_g):
                    v.wait_ge(s_mm.h, mm_g)
                    r = v.tensor_tensor(out=Rb[:, nblk, :], in0=Rb[:, nblk, :],
                                        in1=pband[:, :], op=OP.mult)
                    r.then_inc(s_ve.h, 1)
                block.vector(_)
                s_ve.inc(1)
            if pi_loaded < 20:
                emit_panel(pi_loaded, s_mm.n)
                pi_loaded += 1

        for c in range(8):
            def _(v, c=c):
                v.tensor_tensor_scan(out=scr[:, 0:TT], data0=Ra[:, c, :], data1=Rb[:, c, :],
                                     initial=carry[:, c:c + 1], op0=OP.mult, op1=OP.add)
                v.tensor_tensor(out=Ra[:, c, 0:1], in0=Rc[:, c, 0:1], in1=carry[:, c:c + 1], op=OP.mult)
                v.tensor_tensor(out=Ra[:, c, 1:], in0=Rc[:, c, 1:], in1=scr[:, 0:TT - 1], op=OP.mult)
                v.tensor_copy(out=carry[:, c:c + 1], in_=scr[:, TT - 1:TT])
                v.tensor_copy(out=xnh[:, c, :], in_=Ra[:, c, :])
                r = v.tensor_tensor(out=xnl[:, c, :], in0=Ra[:, c, :], in1=xnh[:, c, :], op=OP.subtract)
                r.then_inc(s_ve.h, 1)
            block.vector(_)
            s_ve.inc(1)

        for pp in range(4):
            pi = 16 + pp
            slot = pi % 2
            wa_gate = panel_conv_ve[pi]
            for blk in range(2):
                oblk = pp * 2 + blk
                pband = P[4 + (oblk % 2)]
                ve_g = s_ve.n
                def _(t, pi=pi, blk=blk, slot=slot, wa_gate=wa_gate, pband=pband, ve_g=ve_g):
                    t.wait_ge(s_ve.h, max(wa_gate, ve_g))
                    for c in range(8):
                        mm3(t, pband[:, :],
                            phl[slot][0][:, c, blk * 128:(blk + 1) * 128],
                            phl[slot][1][:, c, blk * 128:(blk + 1) * 128],
                            xnh[:, c, :], xnl[:, c, :],
                            first=(c == 0), last=(c == 7), inc=(c == 7))
                block.tensor(_)
                s_mm.inc(1)
                mm_g = s_mm.n
                def _(v, oblk=oblk, pband=pband, mm_g=mm_g):
                    v.wait_ge(s_mm.h, mm_g)
                    r = v.scalar_tensor_tensor(out=x_sb[:, oblk, :], in0=pband[:, :],
                                               scalar=attn_scale[:, :], in1=x_sb[:, oblk, :],
                                               op0=OP.mult, op1=OP.add)
                    r.then_inc(s_ve.h, 1)
                block.vector(_)
                s_ve.inc(1)
            if pi_loaded < 20:
                emit_panel(pi_loaded, s_mm.n)
                pi_loaded += 1

        for sub in range(4):
            par = sub % 2
            ot_pre = s_otok.n
            for half in range(2):
                cs = range(half * 4, half * 4 + 4)
                ve_g = s_ve.n
                sc_g = s_sc.n
                def _(t, sub=sub, half=half, cs=cs, ve_g=ve_g, sc_g=sc_g):
                    t.wait_ge(s_ve.h, ve_g)
                    t.wait_ge(s_sc.h, sc_g)
                    for i, c in enumerate(cs):
                        r = t.matmul(P[6][:, i * 128:(i + 1) * 128], x_sb[:, c, sub * 128:(sub + 1) * 128],
                                     eye_sb[:, :], start=(i == 0), stop=(i == 3))
                    r.then_inc(s_mm.h, 1)
                block.tensor(_)
                s_mm.inc(1)
                mm_g = s_mm.n
                def _(s, sub=sub, half=half, par=par, mm_g=mm_g, ot_pre=ot_pre):
                    s.wait_ge(s_mm.h, mm_g)
                    if half == 0:
                        s.wait_ge(s_otok.h, ot_pre)
                    r = s.copy(out=otok[par][:, half * 512:(half + 1) * 512], in_=P[6][:, :])
                    r.then_inc(s_sc.h, 1)
                block.scalar(_)
                s_sc.inc(1)
            sc_g = s_sc.n
            def _(g, sub=sub, par=par, t0=t0, sc_g=sc_g):
                g.wait_ge(s_sc.h, sc_g)
                r0 = t0 + sub * 128
                g.dma_start(out=out_tok[r0:r0 + 128, :], in_=otok[par][:, :]).then_inc(s_otok.h, 16)
            block.gpsimd(_)
            s_otok.inc(16)

    esB.close()

    esC = ExitStack()
    land13 = esC.enter_context(nc.sbuf_tensor("land13", [128, 8, 1024], F32))
    land2 = esC.enter_context(nc.sbuf_tensor("land2", [128, 4, 1024], F32))
    w13h = esC.enter_context(nc.sbuf_tensor("w13h", [128, 8, 1024], BF16))
    w13l = esC.enter_context(nc.sbuf_tensor("w13l", [128, 8, 1024], BF16))
    w2h = esC.enter_context(nc.sbuf_tensor("w2h", [128, 4, 1024], BF16))
    w2l = esC.enter_context(nc.sbuf_tensor("w2l", [128, 4, 1024], BF16))
    xg = esC.enter_context(nc.sbuf_tensor("xg", [128, 8, 2 * D], BF16))
    cg = esC.enter_context(nc.sbuf_tensor("cg", [128, 8, NE], F32))
    cekeep = esC.enter_context(nc.sbuf_tensor("cekeep", [128, 32], F32))
    xfh = esC.enter_context(nc.sbuf_tensor("xfh", [128, 8, 512], BF16))
    xfl = esC.enter_context(nc.sbuf_tensor("xfl", [128, 8, 512], BF16))
    s_sb = esC.enter_context(nc.sbuf_tensor("s_sb", [128, 4, 512], F32))
    hh = esC.enter_context(nc.sbuf_tensor("hh", [128, 4, 512], BF16))
    hl = esC.enter_context(nc.sbuf_tensor("hl", [128, 4, 512], BF16))
    y_fm = esC.enter_context(nc.sbuf_tensor("y_fm", [128, 8, 512], F32))
    y_tok = [esC.enter_context(nc.sbuf_tensor(f"y_tok{i}", [128, 1024], F32)) for i in range(4)]
    idx_sb = esC.enter_context(nc.sbuf_tensor("idx_sb", [128, 64], U32))
    PC = [esC.enter_context(nc.psum_tensor(f"PC{i}", [128, 512], F32)) for i in range(8)]
    blockC = esC.enter_context(nc.Block())
    block = blockC

    xe_total = s_xe.n
    isc_total = s_isc.n
    otok_total = s_otok.n

    def _(g):
        g.memset(xg[:, :, :], 0.0)
        g.memset(cg[:, :, :], 0.0)
        g.wait_ge(s_xe.h, xe_total)
        g.wait_ge(s_isc.h, isc_total)
        g.wait_ge(s_otok.h, otok_total)
    block.gpsimd(_)

    pslot_mm = {}
    pslot_ve = {}
    prev_yout_ve = s_ve.n
    for e in range(NE + 1):
        shared = (e == NE)
        NK = 32 if shared else NE
        epar = e % 2
        acc_expert_base = s_acc.n
        add_cnt = {}

        gat_pre = [gslot[j].n for j in range(8)]
        ve_conv_pre = s_ve.n
        acc_pre_e = s_acc.n
        def _(s, e=e, shared=shared, epar=epar, gat_pre=gat_pre, ve_conv_pre=ve_conv_pre, acc_pre_e=acc_pre_e):
            if not shared:
                for j in range(8):
                    s.wait_ge(gslot[j].h, gat_pre[j])
                s.wait_ge(s_isc.h, isc_total)
                s.wait_ge(s_acc.h, acc_pre_e)
                s.dma_start(out=idx_sb[:, epar * 32:epar * 32 + 16],
                            in_=idx_tab[e * CEXP:(e + 1) * CEXP, :].rearrange("(p k) x -> p (k x)", p=128)
                            ).then_inc(s_we.h, 16)
            s.wait_ge(s_ve.h, ve_conv_pre)
            if shared:
                s.dma_start(out=land13[:, :, :],
                            in_=w13sT.rearrange("(c p) f -> p c f", p=128)).then_inc(s_we.h, 16)
                s.dma_start(out=land2[:, :, :],
                            in_=w2sT.rearrange("(c p) d2 -> p c d2", p=128)).then_inc(s_we.h, 16)
            else:
                s.dma_start(out=land13[:, :, :],
                            in_=w13T[e].rearrange("(c p) f -> p c f", p=128)).then_inc(s_we.h, 16)
                s.dma_start(out=land2[:, :, :],
                            in_=w2T[e].rearrange("(c p) d2 -> p c d2", p=128)).then_inc(s_we.h, 16)
        block.sync(_)
        s_we.inc(32 if shared else 48)
        we_ready = s_we.n
        mm_pre = s_mm.n
        def _(v, we_ready=we_ready, mm_pre=mm_pre):
            v.wait_ge(s_we.h, we_ready)
            v.wait_ge(s_mm.h, mm_pre)
            v.tensor_copy(out=w13h[:, :, :], in_=land13[:, :, :])
            v.tensor_tensor(out=w13l[:, :, :], in0=land13[:, :, :], in1=w13h[:, :, :], op=OP.subtract)
            v.tensor_copy(out=w2h[:, :, :], in_=land2[:, :, :])
            r = v.tensor_tensor(out=w2l[:, :, :], in0=land2[:, :, :], in1=w2h[:, :, :], op=OP.subtract)
            r.then_inc(s_ve.h, 1)
        block.vector(_)
        s_ve.inc(1)
        w_ready = s_ve.n

        idx_ready = 0 if shared else we_ready - 32
        trans_mm_cnt = {}
        gat_cnt = {}
        ce_ve_cnt = {}

        def emit_gather(k, mm_pre, ve_pre_g, we_pre, e=e, epar=epar, shared=shared):
            kk8 = k % 8
            def _(g, k=k, kk8=kk8, mm_pre=mm_pre, ve_pre_g=ve_pre_g, we_pre=we_pre):
                g.wait_ge(s_mm.h, mm_pre)
                g.wait_ge(s_ve.h, ve_pre_g)
                if we_pre:
                    g.wait_ge(s_we.h, we_pre)
                if shared:
                    g.dma_start(out=xg[:, kk8, :], in_=xn_tok[k * 128:(k + 1) * 128, :]
                                ).then_inc(gslot[kk8].h, 16)
                else:
                    g.indirect_dma_start(
                        out=xg[:, kk8, :], out_offset=None,
                        in_=xn_tok,
                        in_offset=bass.IndirectOffsetOnAxis(ap=idx_sb[:, epar * 32 + k:epar * 32 + k + 1], axis=0),
                        bounds_check=regbox["tokbnd"], oob_is_err=False,
                    ).then_inc(gslot[kk8].h, 16)
                    g.indirect_dma_start(
                        out=cg[:, kk8, :], out_offset=None,
                        in_=comb_tok,
                        in_offset=bass.IndirectOffsetOnAxis(ap=idx_sb[:, epar * 32 + k:epar * 32 + k + 1], axis=0),
                        bounds_check=regbox["tokbnd"], oob_is_err=False,
                    ).then_inc(gslot[kk8].h, 16)
            block.gpsimd(_)
            gslot[kk8].inc(16 if shared else 32)
            gat_cnt[k] = gslot[kk8].n

        def emit_cecopy(k, e=e, shared=shared):
            if shared:
                ce_ve_cnt[k] = s_ve.n
                return
            kk8 = k % 8
            def _(v, k=k, kk8=kk8, gat_g=gat_cnt[k]):
                v.wait_ge(gslot[kk8].h, gat_g)
                r = v.tensor_copy(out=cekeep[:, k:k + 1], in_=cg[:, kk8, e:e + 1])
                r.then_inc(s_ve.h, 1)
            block.vector(_)
            s_ve.inc(1)
            ce_ve_cnt[k] = s_ve.n

        for k in range(min(8, NK)):
            emit_gather(k, pslot_mm.get(k % 8, 0), pslot_ve.get(k % 8, 0), idx_ready)
            emit_cecopy(k)

        prev_silu_sc = s_sc.n
        prev_yfm_ve = s_ve.n
        for grp in range(NK // 4):
            for kk in range(4):
                k = grp * 4 + kk
                kk8 = k % 8
                gat_g = gat_cnt[k]
                for half in range(2):
                    cs = range(half * 4, half * 4 + 4)
                    sc_g = s_sc.n
                    ve_g2 = s_ve.n
                    def _(t, k=k, kk8=kk8, half=half, cs=cs, gat_g=gat_g, sc_g=sc_g, ve_g2=ve_g2):
                        t.wait_ge(gslot[kk8].h, gat_g)
                        t.wait_ge(s_sc.h, sc_g)
                        t.wait_ge(s_ve.h, ve_g2)
                        for i, c in enumerate(cs):
                            t.matmul(PC[6][:, i * 128:(i + 1) * 128], xg[:, kk8, c * 128:(c + 1) * 128],
                                     eye_bf[:, :], start=(i == 0), stop=(i == 3))
                            r = t.matmul(PC[7][:, i * 128:(i + 1) * 128], xg[:, kk8, D + c * 128:D + (c + 1) * 128],
                                         eye_bf[:, :], start=(i == 0), stop=(i == 3))
                        r.then_inc(s_mm.h, 1)
                    block.tensor(_)
                    s_mm.inc(1)
                    mm_g = s_mm.n
                    def _(s, kk=kk, half=half, mm_g=mm_g):
                        s.wait_ge(s_mm.h, mm_g)
                        r = s.copy(out=xfh[:, half * 4:(half + 1) * 4, kk * 128:(kk + 1) * 128], in_=PC[6][:, :])
                        r.then_inc(s_sc.h, 1)
                    block.scalar(_)
                    s_sc.inc(1)
                    def _(v, kk=kk, half=half, mm_g=mm_g):
                        v.wait_ge(s_mm.h, mm_g)
                        r = v.tensor_copy(out=xfl[:, half * 4:(half + 1) * 4, kk * 128:(kk + 1) * 128], in_=PC[7][:, :])
                        r.then_inc(s_ve.h, 1)
                    block.vector(_)
                    s_ve.inc(1)
                trans_mm_cnt[k] = s_mm.n
                if k + 8 < NK:
                    emit_gather(k + 8, trans_mm_cnt[k], ce_ve_cnt[k], 0)
            xf_ready_sc = s_sc.n
            xf_ready_ve = s_ve.n

            def _(t, xf_ready_sc=xf_ready_sc, w_ready=w_ready, xf_ready_ve=xf_ready_ve,
                  prev_silu_sc=prev_silu_sc, prev_yfm_ve=prev_yfm_ve):
                t.wait_ge(s_sc.h, max(xf_ready_sc, prev_silu_sc))
                t.wait_ge(s_ve.h, max(w_ready, xf_ready_ve, prev_yfm_ve))
                for fb in range(4):
                    for c in range(8):
                        mm3(t, PC[fb][:, :],
                            w13h[:, c, fb * 128:(fb + 1) * 128], w13l[:, c, fb * 128:(fb + 1) * 128],
                            xfh[:, c, :], xfl[:, c, :], first=(c == 0), last=(c == 7), inc=(c == 7))
            block.tensor(_)
            s_mm.inc(4)
            mm_gate = s_mm.n
            def _(s, mm_gate=mm_gate):
                for fb in range(4):
                    s.wait_ge(s_mm.h, mm_gate - 3 + fb)
                    r = s.activation(out=s_sb[:, fb, :], in_=PC[fb][:, :], func=AF.Silu)
                r.then_inc(s_sc.h, 1)
            block.scalar(_)
            s_sc.inc(1)
            prev_silu_sc = s_sc.n
            def _(t, prev_yout_ve=prev_yout_ve):
                t.wait_ge(s_ve.h, prev_yout_ve)
                for fb in range(4):
                    for c in range(8):
                        mm3(t, PC[4 + fb][:, :],
                            w13h[:, c, 512 + fb * 128:512 + (fb + 1) * 128],
                            w13l[:, c, 512 + fb * 128:512 + (fb + 1) * 128],
                            xfh[:, c, :], xfl[:, c, :], first=(c == 0), last=(c == 7), inc=(c == 7))
            block.tensor(_)
            s_mm.inc(4)
            mm_up = s_mm.n
            sc_silu = s_sc.n
            def _(v, mm_up=mm_up, sc_silu=sc_silu):
                v.wait_ge(s_sc.h, sc_silu)
                for fb in range(4):
                    v.wait_ge(s_mm.h, mm_up - 3 + fb)
                    v.tensor_tensor(out=s_sb[:, fb, :], in0=s_sb[:, fb, :], in1=PC[4 + fb][:, :], op=OP.mult)
                for fb in range(4):
                    v.tensor_copy(out=hh[:, fb, :], in_=s_sb[:, fb, :])
                    r = v.tensor_tensor(out=hl[:, fb, :], in0=s_sb[:, fb, :], in1=hh[:, fb, :], op=OP.subtract)
                r.then_inc(s_ve.h, 1)
            block.vector(_)
            s_ve.inc(1)
            h_ready = s_ve.n
            def _(t, h_ready=h_ready):
                t.wait_ge(s_ve.h, h_ready)
                for db in range(8):
                    for fb in range(4):
                        mm3(t, PC[db][:, :],
                            w2h[:, fb, db * 128:(db + 1) * 128], w2l[:, fb, db * 128:(db + 1) * 128],
                            hh[:, fb, :], hl[:, fb, :], first=(fb == 0), last=(fb == 3), inc=(fb == 3))
            block.tensor(_)
            s_mm.inc(8)
            mm_y = s_mm.n
            def _(v, mm_y=mm_y):
                for db in range(8):
                    v.wait_ge(s_mm.h, mm_y - 7 + db)
                    r = v.tensor_copy(out=y_fm[:, db, :], in_=PC[db][:, :])
                r.then_inc(s_ve.h, 1)
            block.vector(_)
            s_ve.inc(1)
            yfm_ready = s_ve.n
            prev_yfm_ve = yfm_ready

            for kk in range(4):
                k = grp * 4 + kk
                ypar = k % 4
                acc_pre = add_cnt.get(k - 4, acc_expert_base)
                for half in range(2):
                    dbs = range(half * 4, half * 4 + 4)
                    ve_g = s_ve.n
                    def _(t, kk=kk, half=half, dbs=dbs, yfm_ready=yfm_ready, ve_g=ve_g):
                        t.wait_ge(s_ve.h, max(yfm_ready, ve_g))
                        for i, db in enumerate(dbs):
                            r = t.matmul(PC[4 + half][:, i * 128:(i + 1) * 128], y_fm[:, db, kk * 128:(kk + 1) * 128],
                                         eye_sb[:, :], start=(i == 0), stop=(i == 3))
                        r.then_inc(s_mm.h, 1)
                    block.tensor(_)
                    s_mm.inc(1)
                    mm_g = s_mm.n
                    def _(v, k=k, kk=kk, half=half, ypar=ypar, mm_g=mm_g, shared=shared, acc_pre=acc_pre):
                        v.wait_ge(s_mm.h, mm_g)
                        if half == 0:
                            v.wait_ge(s_acc.h, acc_pre)
                        if shared:
                            r = v.tensor_copy(out=y_tok[ypar][:, half * 512:(half + 1) * 512],
                                              in_=PC[4 + half][:, :])
                        else:
                            r = v.tensor_scalar(out=y_tok[ypar][:, half * 512:(half + 1) * 512],
                                                in0=PC[4 + half][:, :],
                                                scalar1=cekeep[:, k:k + 1], scalar2=None, op0=OP.mult)
                        r.then_inc(s_ve.h, 1)
                    block.vector(_)
                    s_ve.inc(1)
                ve_g = s_ve.n
                def _(g, e=e, k=k, ypar=ypar, epar=epar, shared=shared, ve_g=ve_g,
                      acc_expert_base=acc_expert_base, first=(k == 0)):
                    g.wait_ge(s_ve.h, ve_g)
                    if first:
                        g.wait_ge(s_acc.h, acc_expert_base)
                    if shared:
                        g.dma_start(out=out_tok[k * 128:(k + 1) * 128, :], in_=y_tok[ypar][:, :],
                                    accum_op=OP.add).then_inc(s_acc.h, 16)
                    else:
                        g.indirect_dma_start(
                            out=out_tok,
                            out_offset=bass.IndirectOffsetOnAxis(ap=idx_sb[:, epar * 32 + k:epar * 32 + k + 1], axis=0),
                            in_=y_tok[ypar][:, :], in_offset=None,
                            compute_op=OP.add,
                        ).then_inc(s_acc.h, 16)
                block.gpsimd(_)
                s_acc.inc(16)
                add_cnt[k] = s_acc.n
            prev_yout_ve = s_ve.n

            for kk in range(4):
                k2 = (grp + 2) * 4 + kk
                if k2 < NK:
                    emit_cecopy(k2)

        for j in range(min(8, NK)):
            last = NK - 8 + j
            pslot_mm[j] = trans_mm_cnt[last]
            pslot_ve[j] = ce_ve_cnt[last]

    acc_total = s_acc.n
    def _(g, acc_total=acc_total):
        g.wait_ge(s_acc.h, acc_total)
    block.gpsimd(_)
    esC.close()

    es0.close()
    return nc


_NC_CACHE = None


def kernel(L0_attn_u, L0_attn_W_in, L0_attn_W_out, L0_attn_Wp, L0_attn_bp,
           L0_attn_alpha_log, L0_attn_delta_bias, L0_attn_log_gamma,
           L0_ffn_input_ids, L0_ffn_gate_w, L0_ffn_gate_b, L0_ffn_w13, L0_ffn_w2,
           L0_ffn_w13_s, L0_ffn_w2_s):
    global _NC_CACHE
    u = np.asarray(L0_attn_u, dtype=np.float32)

    c = np.ascontiguousarray
    w_inT = c(np.asarray(L0_attn_W_in, np.float32).T)
    w_outT = c(np.asarray(L0_attn_W_out, np.float32).T)
    wpT = c(np.asarray(L0_attn_Wp, np.float32).T)
    gate_wT = c(np.asarray(L0_ffn_gate_w, np.float32).T)
    gate_b = c(np.asarray(L0_ffn_gate_b, np.float32).reshape(NE, 1))
    w13T = c(np.asarray(L0_ffn_w13, np.float32).transpose(0, 2, 1))
    w2T = c(np.asarray(L0_ffn_w2, np.float32).transpose(0, 2, 1))
    w13sT = c(np.asarray(L0_ffn_w13_s, np.float32).T)
    w2sT = c(np.asarray(L0_ffn_w2_s, np.float32).T)
    alpha_log = c(np.asarray(L0_attn_alpha_log, np.float32).reshape(8, 128).T)
    delta_bias = c(np.asarray(L0_attn_delta_bias, np.float32).reshape(8, 128).T)
    bp = c(np.asarray(L0_attn_bp, np.float32).reshape(3, 8, 128).transpose(2, 0, 1).reshape(128, 24))
    log_gamma = c(np.asarray(L0_attn_log_gamma, np.float32).reshape(1, 1))
    eye = np.eye(128, dtype=np.float32)
    ebase = (np.arange(NE, dtype=np.float32) * CEXP).reshape(NE, 1)

    shared = dict(w_inT=w_inT, w_outT=w_outT, wpT=wpT, gate_wT=gate_wT, gate_b=gate_b,
                  w13T=w13T, w2T=w2T, w13sT=w13sT, w2sT=w2sT, alpha_log=alpha_log,
                  delta_bias=delta_bias, bp=bp, log_gamma=log_gamma, eye=eye, ebase=ebase)

    in_maps = []
    for b in range(N_CORES):
        m = dict(shared)
        m["xT"] = c(u[b].T)
        in_maps.append(m)

    if _NC_CACHE is None:
        _NC_CACHE = build_program()
    nc = _NC_CACHE

    res = run_bass_kernel_spmd(nc, in_maps, core_ids=list(range(N_CORES)))
    out = np.stack([np.asarray(res.results[b]["out_tok"])[:TC] for b in range(N_CORES)], axis=0)
    return out


# version 15
# speedup vs baseline: 1.0028x; 1.0028x over previous
import numpy as np
from contextlib import ExitStack

import concourse.bass as bass
import concourse.mybir as mybir
from concourse.bass_utils import run_bass_kernel_spmd

F32 = mybir.dt.float32
BF16 = mybir.dt.bfloat16
AF = mybir.ActivationFunctionType
OP = mybir.AluOpType
AX = mybir.AxisListType

N_CORES = 8
D = 1024
TC = 4096
TT = 1024
NTILE = TC // TT
NG = TT // 512
NE = 16
FI = 512
NSQ = 14
NPW = 6
SIGMA_STAGE = 99
DEBUG_SIGMA = False
EPS_A = 1e-4
LN_EPS = 1e-5


class Sem:
    def __init__(self, es, nc, name):
        self.h = es.enter_context(nc.semaphore(name))
        self.n = 0

    def inc(self, k=1):
        self.n += k
        return self.n


def emit_sigma(nc, block, es, wT, scale_out, consts, sems, tag):
    s_mm, s_ve, s_sc, s_ld = sems["mm"], sems["ve"], sems["sc"], sems["ld"]
    ones_col, ones_row, ones_bf = consts["ones_col"], consts["ones_row"], consts["ones_bf"]

    wt = es.enter_context(nc.sbuf_tensor(f"sg_wt{tag}", [128, 8, 1024], F32))
    Bf = es.enter_context(nc.sbuf_tensor(f"sg_Bf{tag}", [128, 8, 1024], F32))
    M0 = es.enter_context(nc.sbuf_tensor(f"sg_M0{tag}", [128, 8, 1024], BF16))
    M1 = es.enter_context(nc.sbuf_tensor(f"sg_M1{tag}", [128, 8, 1024], BF16))
    junk = es.enter_context(nc.sbuf_tensor(f"sg_junk{tag}", [128, 512], BF16))
    acc = es.enter_context(nc.sbuf_tensor(f"sg_acc{tag}", [128, 16], F32))
    r1 = es.enter_context(nc.sbuf_tensor(f"sg_r1{tag}", [128, 1], F32))
    inv1 = es.enter_context(nc.sbuf_tensor(f"sg_inv1{tag}", [128, 1], F32))
    ssc = es.enter_context(nc.sbuf_tensor(f"sg_ssc{tag}", [1, 1], F32))
    vv = es.enter_context(nc.sbuf_tensor(f"sg_vv{tag}", [128, 8], F32))
    tt = es.enter_context(nc.sbuf_tensor(f"sg_tt{tag}", [128, 8], F32))
    prod = es.enter_context(nc.sbuf_tensor(f"sg_prod{tag}", [128, 8], F32))
    den_sb = es.enter_context(nc.sbuf_tensor(f"sg_den{tag}", [128, 8], F32))
    n2 = es.enter_context(nc.sbuf_tensor(f"sg_n2{tag}", [1, 2], F32))
    ps0 = es.enter_context(nc.psum_tensor(f"sg_ps0{tag}", [128, 512], F32))
    ps1 = es.enter_context(nc.psum_tensor(f"sg_ps1{tag}", [128, 512], F32))
    pv = es.enter_context(nc.psum_tensor(f"sg_pv{tag}", [128, 1], F32))
    pv2 = es.enter_context(nc.psum_tensor(f"sg_pv2{tag}", [128, 1], F32))

    ld_gate = s_ld.n + 16
    ve_prev = s_ve.n
    sc_prev = s_sc.n
    mm_prev = s_mm.n

    def _(s):
        s.wait_ge(s_ve.h, ve_prev)
        s.wait_ge(s_sc.h, sc_prev)
        s.wait_ge(s_mm.h, mm_prev)
        s.dma_start(out=wt[:, :, :], in_=wT.rearrange("(c p) n -> p c n", p=128)).then_inc(s_ld.h, 16)
    block.sync(_)
    s_ld.inc(16)

    def sq_pass(src, dst, is_b_build):
        mm_base = s_mm.n
        ve_base = s_ve.n
        sc_base = s_sc.n

        def _t(t):
            if is_b_build:
                t.wait_ge(s_ld.h, ld_gate)
            t.wait_ge(s_ve.h, ve_base)
            t.wait_ge(s_sc.h, sc_base)
            for k in range(16):
                iblk, jg = k // 2, k % 2
                ps = ps0 if jg == 0 else ps1
                if k >= 2:
                    t.wait_ge(s_ve.h, ve_base + k - 1)
                    t.wait_ge(s_sc.h, sc_base + k - 1)
                for d in range(8):
                    r = t.matmul(ps[:, :], src[:, d, iblk * 128:(iblk + 1) * 128],
                                 src[:, d, jg * 512:(jg + 1) * 512],
                                 start=(d == 0), stop=(d == 7))
                r.then_inc(s_mm.h, 1)
        block.tensor(_t)
        s_mm.inc(16)

        def _v(v):
            for k in range(16):
                iblk, jg = k // 2, k % 2
                ps = ps0 if jg == 0 else ps1
                v.wait_ge(s_mm.h, mm_base + k + 1)
                tgt = Bf if is_b_build else dst
                r = v.tensor_copy(out=tgt[:, iblk, jg * 512:(jg + 1) * 512], in_=ps[:, :])
                r.then_inc(s_ve.h, 1)
        block.vector(_v)
        s_ve.inc(16)

        def _s(s):
            for k in range(16):
                iblk, jg = k // 2, k % 2
                s.wait_ge(s_ve.h, ve_base + k + 1)
                src_sb = (Bf if is_b_build else dst)[:, iblk, jg * 512:(jg + 1) * 512]
                if is_b_build:
                    r = s.activation(out=M0[:, iblk, jg * 512:(jg + 1) * 512], in_=src_sb,
                                     func=AF.Copy, bias=0.0)
                else:
                    r = s.activation(out=junk[:, :], in_=src_sb, func=AF.Abs,
                                     accum_out=acc[:, k:k + 1])
                r.then_inc(s_sc.h, 1)
        block.scalar(_s)
        s_sc.inc(16)

    sq_pass(wt, None, True)

    if SIGMA_STAGE < 2:
        return
    bufs = [M0, M1]
    for it in range(NSQ):
        src, dst = bufs[it % 2], bufs[(it + 1) % 2]
        sq_pass(src, dst, False)

        sc_gate = s_sc.n
        def _v2(v, sc_gate=sc_gate):
            v.wait_ge(s_sc.h, sc_gate)
            r = v.tensor_reduce(out=r1[:, :], in_=acc[:, :], axis=AX.X, op=OP.add)
            r.then_inc(s_ve.h, 1)
        block.vector(_v2)
        s_ve.inc(1)
        ve_g = s_ve.n
        def _t2(t, ve_g=ve_g):
            t.wait_ge(s_ve.h, ve_g)
            r = t.matmul(pv[:1, :], ones_col[:, :], r1[:, :], start=True, stop=True)
            r.then_inc(s_mm.h, 1)
        block.tensor(_t2)
        s_mm.inc(1)
        mm_g = s_mm.n
        def _v3(v, mm_g=mm_g):
            v.wait_ge(s_mm.h, mm_g)
            r = v.tensor_copy(out=ssc[:, :], in_=pv[:1, :])
            r.then_inc(s_ve.h, 1)
        block.vector(_v3)
        s_ve.inc(1)
        ve_g2 = s_ve.n
        def _t3(t, ve_g2=ve_g2):
            t.wait_ge(s_ve.h, ve_g2)
            r = t.matmul(pv2[:, :], ones_row[:1, :], ssc[:1, :], start=True, stop=True)
            r.then_inc(s_mm.h, 1)
        block.tensor(_t3)
        s_mm.inc(1)
        mm_g2 = s_mm.n
        def _v4(v, dst=dst, mm_g2=mm_g2):
            v.wait_ge(s_mm.h, mm_g2)
            v.reciprocal(out=inv1[:, :], in_=pv2[:, :])
            v.drain()
            for c in range(8):
                r = v.tensor_scalar(out=dst[:, c, :], in0=dst[:, c, :],
                                    scalar1=inv1[:, :], scalar2=256.0,
                                    op0=OP.mult, op1=OP.mult)
            r.then_inc(s_ve.h, 1)
        block.vector(_v4)
        s_ve.inc(1)

    Mfin = bufs[NSQ % 2]
    if SIGMA_STAGE < 3:
        return

    for pblk in range(8):
        gate = s_ve.n
        def _t(t, pblk=pblk, gate=gate):
            t.wait_ge(s_ve.h, gate)
            for q in range(8):
                r = t.matmul(pv[:, :], Mfin[:, q, pblk * 128:(pblk + 1) * 128],
                             ones_bf[:, :], start=(q == 0), stop=(q == 7))
            r.then_inc(s_mm.h, 1)
        block.tensor(_t)
        s_mm.inc(1)
        mm_g = s_mm.n
        def _v(v, pblk=pblk, mm_g=mm_g):
            v.wait_ge(s_mm.h, mm_g)
            r = v.tensor_copy(out=vv[:, pblk:pblk + 1], in_=pv[:, :])
            r.then_inc(s_ve.h, 1)
        block.vector(_v)
        s_ve.inc(1)

    if SIGMA_STAGE < 4:
        return
    vbuf, tbuf = vv, tt
    for step in range(NPW + 1):
        for pblk in range(8):
            gate = s_ve.n
            def _t(t, pblk=pblk, vbuf=vbuf, gate=gate):
                t.wait_ge(s_ve.h, gate)
                for q in range(8):
                    r = t.matmul(pv[:, :], Bf[:, q, pblk * 128:(pblk + 1) * 128],
                                 vbuf[:, q:q + 1], start=(q == 0), stop=(q == 7))
                r.then_inc(s_mm.h, 1)
            block.tensor(_t)
            s_mm.inc(1)
            mm_g = s_mm.n
            def _v(v, pblk=pblk, tbuf=tbuf, mm_g=mm_g):
                v.wait_ge(s_mm.h, mm_g)
                r = v.tensor_copy(out=tbuf[:, pblk:pblk + 1], in_=pv[:, :])
                r.then_inc(s_ve.h, 1)
            block.vector(_v)
            s_ve.inc(1)
        vbuf, tbuf = tbuf, vbuf
    vlast, tlast = tbuf, vbuf

    if SIGMA_STAGE < 5:
        return
    def _v5(v):
        v.tensor_tensor(out=prod[:, :], in0=vlast[:, :], in1=tlast[:, :], op=OP.mult)
        r = v.tensor_tensor(out=den_sb[:, :], in0=vlast[:, :], in1=vlast[:, :], op=OP.mult)
        r.then_inc(s_ve.h, 1)
    block.vector(_v5)
    s_ve.inc(1)

    ve_g = s_ve.n
    def _t4(t, ve_g=ve_g):
        t.wait_ge(s_ve.h, ve_g)
        for q in range(8):
            t.matmul(pv[:1, :], ones_col[:, :], prod[:, q:q + 1], start=(q == 0), stop=(q == 7))
        for q in range(8):
            r = t.matmul(pv2[:1, :], ones_col[:, :], den_sb[:, q:q + 1], start=(q == 0), stop=(q == 7))
        r.then_inc(s_mm.h, 1)
    block.tensor(_t4)
    s_mm.inc(1)

    mm_g = s_mm.n
    def _v6(v, mm_g=mm_g):
        v.wait_ge(s_mm.h, mm_g)
        v.tensor_copy(out=n2[:, 0:1], in_=pv[:1, :])
        v.tensor_copy(out=n2[:, 1:2], in_=pv2[:1, :])
        v.drain()
        v.reciprocal(out=n2[:, 1:2], in_=n2[:, 1:2])
        v.drain()
        r = v.tensor_tensor(out=scale_out[:, :], in0=n2[:, 0:1], in1=n2[:, 1:2], op=OP.mult)
        r.then_inc(s_ve.h, 1)
    block.vector(_v6)
    s_ve.inc(1)

    ve_g = s_ve.n
    def _s2(s, ve_g=ve_g):
        s.wait_ge(s_ve.h, ve_g)
        r = s.activation(out=scale_out[:, :], in_=scale_out[:, :], func=AF.Sqrt)
        r.then_inc(s_sc.h, 1)
    block.scalar(_s2)
    s_sc.inc(1)
    sc_g = s_sc.n
    def _v7(v, sc_g=sc_g):
        v.wait_ge(s_sc.h, sc_g)
        r = v.tensor_scalar(out=scale_out[:, :], in0=scale_out[:, :], scalar1=1.0,
                            scalar2=None, op0=OP.max)
        r.then_inc(s_ve.h, 1)
    block.vector(_v7)
    s_ve.inc(1)


U32 = mybir.dt.uint32
TT = 512
NTILE_B = TC // TT
CEXP = 2048
NDUMP = 2048
DEBUG_DUMPS = False
TOKP = TC + NDUMP
BIGS = float(2 ** 24)


def build_program():
    nc = bass.Bass("TRN2", target_bir_lowering=False, detect_race_conditions=False,
                   dynamic_dma_scratch_size=16384)

    xT = nc.dram_tensor("xT", [D, TC], F32, kind="ExternalInput").ap()
    out_tok = nc.dram_tensor("out_tok", [TOKP, D], F32, kind="ExternalOutput").ap()
    _dk = dict(kind="ExternalOutput") if DEBUG_DUMPS else {}
    xn_tok = nc.dram_tensor("xn_tok", [TC, 2 * D], BF16, **_dk).ap()
    comb_tok = nc.dram_tensor("comb_tok", [TC, NE], F32, **_dk).ap()
    idx_tab = nc.dram_tensor("idx_tab", [NE * CEXP, 1], U32, **_dk).ap()
    w_inT = nc.dram_tensor("w_inT", [D, D], F32, kind="ExternalInput").ap()
    w_outT = nc.dram_tensor("w_outT", [D, D], F32, kind="ExternalInput").ap()
    wpT = nc.dram_tensor("wpT", [D, 3 * D], F32, kind="ExternalInput").ap()
    gate_wT = nc.dram_tensor("gate_wT", [D, NE], F32, kind="ExternalInput").ap()
    gate_b = nc.dram_tensor("gate_b", [NE, 1], F32, kind="ExternalInput").ap()
    w13T = nc.dram_tensor("w13T", [NE, D, 2 * FI], F32, kind="ExternalInput").ap()
    w2T = nc.dram_tensor("w2T", [NE, FI, D], F32, kind="ExternalInput").ap()
    w13sT = nc.dram_tensor("w13sT", [D, 2 * FI], F32, kind="ExternalInput").ap()
    w2sT = nc.dram_tensor("w2sT", [FI, D], F32, kind="ExternalInput").ap()
    alpha_log = nc.dram_tensor("alpha_log", [128, 8], F32, kind="ExternalInput").ap()
    delta_bias = nc.dram_tensor("delta_bias", [128, 8], F32, kind="ExternalInput").ap()
    bp = nc.dram_tensor("bp", [128, 24], F32, kind="ExternalInput").ap()
    log_gamma = nc.dram_tensor("log_gamma", [1, 1], F32, kind="ExternalInput").ap()
    eye = nc.dram_tensor("eye", [128, 128], F32, kind="ExternalInput").ap()
    ebase = nc.dram_tensor("ebase", [NE, 1], F32, kind="ExternalInput").ap()

    es0 = ExitStack()
    eye_sb = es0.enter_context(nc.sbuf_tensor("eye_sb", [128, 128], F32))
    eye_bf = es0.enter_context(nc.sbuf_tensor("eye_bf", [128, 128], BF16))
    ones_col = es0.enter_context(nc.sbuf_tensor("ones_col", [128, 1], F32))
    ones_row = es0.enter_context(nc.sbuf_tensor("ones_row", [1, 128], F32))
    ones_bf = es0.enter_context(nc.sbuf_tensor("ones_bf", [128, 1], BF16))
    ones16 = es0.enter_context(nc.sbuf_tensor("ones16", [NE, TT], F32))
    scale1 = es0.enter_context(nc.sbuf_tensor("scale1", [1, 1], F32))
    scale2 = es0.enter_context(nc.sbuf_tensor("scale2", [1, 1], F32))
    attn_scale = es0.enter_context(nc.sbuf_tensor("attn_scale", [128, 1], F32))
    lg_sb = es0.enter_context(nc.sbuf_tensor("lg_sb", [1, 1], F32))
    alpha_n = es0.enter_context(nc.sbuf_tensor("alpha_n", [128, 8], F32))
    db_comb = es0.enter_context(nc.sbuf_tensor("db_comb", [128, 8], F32))
    bp_b = es0.enter_context(nc.sbuf_tensor("bp_b", [128, 8], F32))
    bp_c = es0.enter_context(nc.sbuf_tensor("bp_c", [128, 8], F32))
    gate_b_sb = es0.enter_context(nc.sbuf_tensor("gate_b_sb", [NE, 1], F32))
    gb_bcast = es0.enter_context(nc.sbuf_tensor("gb_bcast", [128, NE], F32))
    ebase_sb = es0.enter_context(nc.sbuf_tensor("ebase_sb", [NE, 1], F32))
    gwt = es0.enter_context(nc.sbuf_tensor("gwt", [128, 8, NE], F32))
    carry = es0.enter_context(nc.sbuf_tensor("carry", [128, 8], F32))
    cnt_vec = es0.enter_context(nc.sbuf_tensor("cnt_vec", [NE, 1], F32))
    tmp8 = es0.enter_context(nc.sbuf_tensor("tmp8", [128, 8], F32))
    cpool = es0.enter_context(nc.sbuf_tensor("cpool", [128, 2], F32))
    Mcol = es0.enter_context(nc.sbuf_tensor("Mcol", [128, 1], F32))
    negM = es0.enter_context(nc.sbuf_tensor("negM", [128, 1], F32))

    sems = {}
    for name in ["ld", "mm", "ve", "sc", "xin", "otok", "isc", "gat", "acc", "wa", "we", "xe",
                 "gs0", "gs1", "gs2", "gs3", "gs4", "gs5", "gs6", "gs7"]:
        sems[name] = Sem(es0, nc, "s_" + name)
    s_ld, s_mm, s_ve, s_sc = sems["ld"], sems["mm"], sems["ve"], sems["sc"]
    s_xin, s_otok, s_isc = sems["xin"], sems["otok"], sems["isc"]
    s_gat, s_acc, s_wa, s_we, s_xe = sems["gat"], sems["acc"], sems["wa"], sems["we"], sems["xe"]
    gslot = [sems[f"gs{j}"] for j in range(8)]

    consts = {"ones_col": ones_col, "ones_row": ones_row, "ones_bf": ones_bf}

    with ExitStack() as esA:
        ztile = esA.enter_context(nc.sbuf_tensor("ztile", [128, 1024], F32))
        idxfill = esA.enter_context(nc.sbuf_tensor("idxfill", [128, 256], U32))
        block = esA.enter_context(nc.Block())

        def _(s):
            s.dma_start(out=eye_sb[:, :], in_=eye).then_inc(s_ld.h, 16)
            s.dma_start(out=lg_sb[:, :], in_=log_gamma).then_inc(s_ld.h, 16)
            s.dma_start(out=alpha_n[:, :], in_=alpha_log).then_inc(s_ld.h, 16)
            s.dma_start(out=db_comb[:, :], in_=delta_bias).then_inc(s_ld.h, 16)
            s.dma_start(out=tmp8[:, :], in_=bp[:, 0:8]).then_inc(s_ld.h, 16)
            s.dma_start(out=bp_b[:, :], in_=bp[:, 8:16]).then_inc(s_ld.h, 16)
            s.dma_start(out=bp_c[:, :], in_=bp[:, 16:24]).then_inc(s_ld.h, 16)
            s.dma_start(out=gate_b_sb[:, :], in_=gate_b).then_inc(s_ld.h, 16)
            s.dma_start(out=gwt[:, :, :], in_=gate_wT.rearrange("(c p) e -> p c e", p=128)).then_inc(s_ld.h, 16)
            s.dma_start(out=ebase_sb[:, :], in_=ebase).then_inc(s_ld.h, 16)
        block.sync(_)
        s_ld.inc(16 * 10)

        ld_g = s_ld.n
        def _(v, ld_g=ld_g):
            v.wait_ge(s_ld.h, ld_g)
            v.memset(ones_col[:, :], 1.0)
            v.memset(ones_row[:, :], 1.0)
            v.memset(ones_bf[:, :], 1.0)
            v.memset(ones16[:, :], 1.0)
            v.memset(carry[:, :], 0.0)
            v.memset(cnt_vec[:, :], 0.0)
            v.memset(ztile[:, :], 0.0)
            v.memset(cpool[:, 0:1], 100000.0)
            v.drain()
            v.reciprocal(out=cpool[:, 0:1], in_=cpool[:, 0:1])
            v.drain()
            v.memset(cpool[:, 1:2], 1e-12)
            v.memset(Mcol[:, :], 1.0)
            v.drain()
            v.tensor_scalar(out=Mcol[:, :], in0=Mcol[:, :], scalar1=float(2.0 ** -29),
                            scalar2=1.0, op0=OP.add, op1=OP.subtract)
            v.drain()
            v.tensor_scalar(out=Mcol[:, :], in0=Mcol[:, :], scalar1=float(2.0 ** 58),
                            scalar2=None, op0=OP.mult)
            v.drain()
            v.tensor_scalar(out=negM[:, :], in0=Mcol[:, :], scalar1=-1.0, scalar2=None, op0=OP.mult)
            v.drain()
            r = v.tensor_copy(out=eye_bf[:, :], in_=eye_sb[:, :])
            r.then_inc(s_ve.h, 1)
        block.vector(_)
        s_ve.inc(1)
        def _(s, ld_g=ld_g):
            s.wait_ge(s_ld.h, ld_g)
            s.activation(out=alpha_n[:, :], in_=alpha_n[:, :], func=AF.Exp)
            s.add(alpha_n[:, :], alpha_n[:, :], 1.0)
            s.activation(out=alpha_n[:, :], in_=alpha_n[:, :], func=AF.Ln)
            r = s.mul(alpha_n[:, :], alpha_n[:, :], -1.0)
            r.then_inc(s_sc.h, 1)
        block.scalar(_)
        s_sc.inc(1)
        def _(v):
            r = v.tensor_tensor(out=db_comb[:, :], in0=db_comb[:, :], in1=tmp8[:, :], op=OP.add)
            r.then_inc(s_ve.h, 1)
        block.vector(_)
        s_ve.inc(1)

        ve_g = s_ve.n
        regbox = {}
        def _(g, ve_g=ve_g):
            regbox["slotbnd"] = g.alloc_register("slotbnd")
            g.reg_mov(regbox["slotbnd"], NE * CEXP - 1)
            regbox["tokbnd"] = g.alloc_register("tokbnd")
            g.reg_mov(regbox["tokbnd"], TC - 1)
            g.wait_ge(s_ve.h, ve_g)
            for r0 in range(TC, TOKP, 128):
                g.dma_start(out=out_tok[r0:r0 + 128, :], in_=ztile[:, :]).then_inc(s_otok.h, 16)
            g.iota(idxfill[:, :], [[1, 16], [0, 16]], base=TC, channel_multiplier=16, allow_small_or_imprecise_dtypes=True)
            g.dma_start(out=idx_tab.rearrange("(p c) x -> p (c x)", p=128),
                        in_=idxfill[:, :]).then_inc(s_isc.h, 16)
        block.gpsimd(_)
        s_otok.inc(16 * (NDUMP // 128))
        s_isc.inc(16)

        with ExitStack() as esS:
            emit_sigma(nc, block, esS, w_inT, scale1, consts, sems, "a")
        with ExitStack() as esS:
            emit_sigma(nc, block, esS, w_outT, scale2, consts, sems, "b")

        with ExitStack() as esS:
            pvb = esS.enter_context(nc.psum_tensor("pvb", [128, 1], F32))
            n2gb = esS.enter_context(nc.sbuf_tensor("n2gb", [1, NE], F32))
            pgb = esS.enter_context(nc.psum_tensor("pgb", [128, NE], F32))
            ve_g = s_ve.n
            def _(s, ve_g=ve_g):
                s.wait_ge(s_ve.h, ve_g)
                r = s.activation(out=lg_sb[:, :], in_=lg_sb[:, :], func=AF.Exp)
                r.then_inc(s_sc.h, 1)
            block.scalar(_)
            s_sc.inc(1)
            sc_g = s_sc.n
            def _(v, sc_g=sc_g):
                v.wait_ge(s_sc.h, sc_g)
                v.tensor_tensor(out=scale1[:, :], in0=scale1[:, :], in1=scale2[:, :], op=OP.mult)
                v.drain()
                v.reciprocal(out=scale1[:, :], in_=scale1[:, :])
                v.drain()
                r = v.tensor_tensor(out=scale1[:, :], in0=scale1[:, :], in1=lg_sb[:, :], op=OP.mult)
                r.then_inc(s_ve.h, 1)
            block.vector(_)
            s_ve.inc(1)
            ve_g = s_ve.n
            def _(t, ve_g=ve_g):
                t.wait_ge(s_ve.h, ve_g)
                t.matmul(pvb[:, :], ones_row[:1, :], scale1[:1, :], start=True, stop=True).then_inc(s_mm.h, 1)
                r = t.matmul(pgb[:1, :NE], gate_b_sb[:, :], eye_sb[:NE, :NE], start=True, stop=True)
                r.then_inc(s_mm.h, 1)
            block.tensor(_)
            s_mm.inc(2)
            mm_g = s_mm.n
            def _(v, mm_g=mm_g):
                v.wait_ge(s_mm.h, mm_g)
                v.tensor_copy(out=attn_scale[:, :], in_=pvb[:, :])
                r = v.tensor_copy(out=n2gb[:1, :], in_=pgb[:1, :NE])
                r.then_inc(s_ve.h, 1)
            block.vector(_)
            s_ve.inc(1)
            ve_g = s_ve.n
            def _(t, ve_g=ve_g):
                t.wait_ge(s_ve.h, ve_g)
                r = t.matmul(pgb[:, :], ones_row[:1, :], n2gb[:1, :], start=True, stop=True)
                r.then_inc(s_mm.h, 1)
            block.tensor(_)
            s_mm.inc(1)
            mm_g = s_mm.n
            def _(v, mm_g=mm_g):
                v.wait_ge(s_mm.h, mm_g)
                r = v.tensor_copy(out=gb_bcast[:, :], in_=pgb[:, :])
                r.then_inc(s_ve.h, 1)
            block.vector(_)
            s_ve.inc(1)

    esB = ExitStack()
    x_sb = esB.enter_context(nc.sbuf_tensor("x_sb", [128, 8, TT], F32))
    xnh = esB.enter_context(nc.sbuf_tensor("xnh", [128, 8, TT], BF16))
    xnl = esB.enter_context(nc.sbuf_tensor("xnl", [128, 8, TT], BF16))
    Ra = esB.enter_context(nc.sbuf_tensor("Ra", [128, 8, TT], F32))
    Rb = esB.enter_context(nc.sbuf_tensor("Rb", [128, 8, TT], F32))
    Rc = esB.enter_context(nc.sbuf_tensor("Rc", [128, 8, TT], F32))
    scr = esB.enter_context(nc.sbuf_tensor("scr", [128, 8 * TT], F32))
    sgad = esB.enter_context(nc.sbuf_tensor("sgad", [128, 2, TT], F32))
    pland = [esB.enter_context(nc.sbuf_tensor(f"pland{i}", [128, 8, 256], F32)) for i in range(2)]
    phl = [(esB.enter_context(nc.sbuf_tensor(f"ph{i}", [128, 8, 256], BF16)),
            esB.enter_context(nc.sbuf_tensor(f"pl{i}", [128, 8, 256], BF16))) for i in range(2)]
    stok = esB.enter_context(nc.sbuf_tensor("stok", [128, 4, NE], F32))
    btok = esB.enter_context(nc.sbuf_tensor("btok", [128, 4, NE], F32))
    rtok = esB.enter_context(nc.sbuf_tensor("rtok", [128, 4, NE], F32))
    thr8 = esB.enter_context(nc.sbuf_tensor("thr8", [128, 4], F32))
    invd8 = esB.enter_context(nc.sbuf_tensor("invd8", [128, 4], F32))
    mske = esB.enter_context(nc.sbuf_tensor("mske", [128, 4, NE], F32))
    slots_f = esB.enter_context(nc.sbuf_tensor("slots_f", [128, 4, NE], F32))
    slots_u = esB.enter_context(nc.sbuf_tensor("slots_u", [128, 4 * NE], U32))
    ids_u = esB.enter_context(nc.sbuf_tensor("ids_u", [128, 4], U32))
    tth = [esB.enter_context(nc.sbuf_tensor(f"tth{i}", [128, 1024], BF16)) for i in range(2)]
    ttl = [esB.enter_context(nc.sbuf_tensor(f"ttl{i}", [128, 1024], BF16)) for i in range(2)]
    otok = [esB.enter_context(nc.sbuf_tensor(f"otok{i}", [128, 1024], F32)) for i in range(2)]
    P = [esB.enter_context(nc.psum_tensor(f"P{i}", [128, 512], F32)) for i in range(8)]
    blockB = esB.enter_context(nc.Block())
    block = blockB

    stats = [Rc[:1, 0, 0:512], Rc[:1, 1, 0:512], Rc[:1, 2, 0:512], Rc[:1, 3, 0:512], Rc[:1, 4, 0:512]]
    scores_em = Rc[0:NE, 5, :]
    comb_em = Rc[0:NE, 6, :]
    m01 = Rc[0:NE, 7, :]
    incl_t = Rc[0:NE, 0, :]
    srel_t = Rc[0:NE, 1, :]
    tmp_t = Rc[0:NE, 2, :]

    def mm3(t, ps, wh, wl, xh, xl, first, last, inc=None):
        t.matmul(ps, wh, xh, start=first, stop=False)
        t.matmul(ps, wh, xl, start=False, stop=False)
        r = t.matmul(ps, wl, xh, start=False, stop=last)
        if inc:
            r.then_inc(s_mm.h, 1)
        return r

    for j in range(NTILE_B):
        t0 = j * TT

        mm_pre = s_mm.n
        def _(g, t0=t0, mm_pre=mm_pre):
            g.wait_ge(s_mm.h, mm_pre)
            g.dma_start(out=x_sb[:, :, :],
                        in_=xT[:, t0:t0 + TT].rearrange("(c p) t -> p c t", p=128)).then_inc(s_xin.h, 16)
        block.gpsimd(_)
        s_xin.inc(16)

        panels = ([wpT[:, p * 256:(p + 1) * 256] for p in range(12)]
                  + [w_inT[:, p * 256:(p + 1) * 256] for p in range(4)]
                  + [w_outT[:, p * 256:(p + 1) * 256] for p in range(4)])
        wa_base = s_wa.n
        panel_conv_ve = [None] * 20

        def emit_panel(pi, gate_mm):
            slot = pi % 2
            def _(s, pi=pi, gate_mm=gate_mm, slot=slot):
                if gate_mm is not None:
                    s.wait_ge(s_mm.h, gate_mm)
                s.wait_ge(s_ve.h, s_ve.n)
                s.dma_start(out=pland[slot][:, :, :],
                            in_=panels[pi].rearrange("(c p) n -> p c n", p=128)).then_inc(s_wa.h, 16)
            block.sync(_)
            s_wa.inc(16)
            wa_g = s_wa.n
            def _(v, pi=pi, slot=slot, wa_g=wa_g):
                v.wait_ge(s_wa.h, wa_g)
                v.tensor_copy(out=phl[slot][0][:, :, :], in_=pland[slot][:, :, :])
                r = v.tensor_tensor(out=phl[slot][1][:, :, :], in0=pland[slot][:, :, :],
                                    in1=phl[slot][0][:, :, :], op=OP.subtract)
                r.then_inc(s_ve.h, 1)
            block.vector(_)
            s_ve.inc(1)
            panel_conv_ve[pi] = s_ve.n

        emit_panel(0, s_mm.n)
        emit_panel(1, s_mm.n)
        pi_loaded = 2

        xin_gate = s_xin.n
        ve_pre = s_ve.n
        def _(s, xin_gate=xin_gate, ve_pre=ve_pre):
            s.wait_ge(s_xin.h, xin_gate)
            s.wait_ge(s_ve.h, ve_pre)
            for c in range(8):
                r = s.activation(out=scr[:, c * TT:(c + 1) * TT], in_=x_sb[:, c, :], func=AF.Square)
            r.then_inc(s_sc.h, 1)
        block.scalar(_)
        s_sc.inc(1)
        sc_sq = s_sc.n
        ve_pre = s_ve.n
        def _(t, sc_sq=sc_sq, ve_pre=ve_pre, xin_gate=xin_gate):
            t.wait_ge(s_xin.h, xin_gate)
            t.wait_ge(s_sc.h, sc_sq)
            t.wait_ge(s_ve.h, ve_pre)
            for c in range(8):
                t.matmul(P[0][:1, :], ones_col[:, :], x_sb[:, c, :], start=(c == 0), stop=(c == 7))
            for c in range(8):
                r = t.matmul(P[1][:1, :], ones_col[:, :], scr[:, c * TT:(c + 1) * TT],
                             start=(c == 0), stop=(c == 7))
            r.then_inc(s_mm.h, 1)
        block.tensor(_)
        s_mm.inc(1)
        mm_g = s_mm.n
        def _(v, mm_g=mm_g):
            v.wait_ge(s_mm.h, mm_g)
            v.tensor_scalar(out=stats[0], in0=P[0][:1, :], scalar1=1.0 / D, scalar2=None, op0=OP.mult)
            v.tensor_scalar(out=stats[1], in0=P[1][:1, :], scalar1=1.0 / D, scalar2=None, op0=OP.mult)
            v.drain()
            v.tensor_tensor(out=stats[2], in0=stats[0], in1=stats[0], op=OP.mult)
            v.drain()
            r = v.tensor_tensor(out=stats[1], in0=stats[1], in1=stats[2], op=OP.subtract)
            r.then_inc(s_ve.h, 1)
        block.vector(_)
        s_ve.inc(1)
        ve_g = s_ve.n
        def _(s, ve_g=ve_g):
            s.wait_ge(s_ve.h, ve_g)
            r = s.activation(out=stats[2], in_=stats[1], func=AF.Sqrt, bias=cpool[:1, 0:1])
            r.then_inc(s_sc.h, 1)
        block.scalar(_)
        s_sc.inc(1)
        sc_g = s_sc.n
        def _(v, sc_g=sc_g):
            v.wait_ge(s_sc.h, sc_g)
            v.reciprocal(out=stats[3], in_=stats[2])
            v.drain()
            r = v.tensor_tensor(out=stats[4], in0=stats[0], in1=stats[3], op=OP.mult)
            r.then_inc(s_ve.h, 1)
        block.vector(_)
        s_ve.inc(1)
        ve_g = s_ve.n
        def _(t, ve_g=ve_g):
            t.wait_ge(s_ve.h, ve_g)
            t.matmul(P[2][:, :], ones_row[:1, :], stats[3], start=True, stop=True)
            r = t.matmul(P[3][:, :], ones_row[:1, :], stats[4], start=True, stop=True)
            r.then_inc(s_mm.h, 1)
        block.tensor(_)
        s_mm.inc(1)
        mm_g = s_mm.n
        def _(v, mm_g=mm_g):
            v.wait_ge(s_mm.h, mm_g)
            for c in range(8):
                sl = slice(c * TT, (c + 1) * TT)
                v.tensor_tensor(out=scr[:, sl], in0=x_sb[:, c, :], in1=P[2][:, :], op=OP.mult)
                v.tensor_tensor(out=scr[:, sl], in0=scr[:, sl], in1=P[3][:, :], op=OP.subtract)
            for c in range(8):
                sl = slice(c * TT, (c + 1) * TT)
                v.tensor_copy(out=xnh[:, c, :], in_=scr[:, sl])
                r = v.tensor_tensor(out=xnl[:, c, :], in0=scr[:, sl], in1=xnh[:, c, :], op=OP.subtract)
            r.then_inc(s_ve.h, 1)
        block.vector(_)
        s_ve.inc(1)
        xn_ready = s_ve.n

        gate_mm_base = s_mm.n
        for c in range(8):
            mm_pre = s_mm.n
            def _(v, c=c, mm_pre=mm_pre, gate_mm_base=gate_mm_base):
                if c < 2:
                    v.wait_ge(s_mm.h, mm_pre)
                else:
                    v.wait_ge(s_mm.h, gate_mm_base + c - 1)
                sl = slice(c * TT, (c + 1) * TT)
                v.scalar_tensor_tensor(out=sgad[:, c % 2, :], in0=scr[:, sl], scalar=Mcol[:, :],
                                       in1=scr[:, sl], op0=OP.mult, op1=OP.add)
                r = v.scalar_tensor_tensor(out=sgad[:, c % 2, :], in0=scr[:, sl], scalar=negM[:, :],
                                           in1=sgad[:, c % 2, :], op0=OP.mult, op1=OP.add)
                r.then_inc(s_ve.h, 1)
            block.vector(_)
            s_ve.inc(1)
            ve_g = s_ve.n
            sc_pre = s_sc.n
            def _(t, c=c, ve_g=ve_g, sc_pre=sc_pre):
                t.wait_ge(s_ve.h, ve_g)
                if c == 0:
                    t.wait_ge(s_sc.h, sc_pre)
                r = t.matmul(P[4][:NE, :], gwt[:, c, :], sgad[:, c % 2, :],
                             start=(c == 0), stop=(c == 7))
                r.then_inc(s_mm.h, 1)
            block.tensor(_)
            s_mm.inc(1)
        mm_g = s_mm.n
        def _(s, mm_g=mm_g):
            s.wait_ge(s_mm.h, mm_g)
            r = s.activation(out=scores_em, in_=P[4][:NE, :], func=AF.Sigmoid)
            r.then_inc(s_sc.h, 1)
        block.scalar(_)
        s_sc.inc(1)

        sc_scores = s_sc.n
        for sub in range(4):
            sc_g = s_sc.n
            def _(t, sub=sub, sc_g=sc_g):
                t.wait_ge(s_sc.h, sc_scores)
                if sub > 0:
                    t.wait_ge(s_sc.h, sc_g)
                r = t.matmul(P[5][:, :NE], scores_em[:, sub * 128:(sub + 1) * 128], eye_sb[:NE, :NE], start=True, stop=True)
                r.then_inc(s_mm.h, 1)
            block.tensor(_)
            s_mm.inc(1)
            mm_g = s_mm.n
            def _(s, sub=sub, mm_g=mm_g):
                s.wait_ge(s_mm.h, mm_g)
                r = s.copy(out=stok[:, sub, :], in_=P[5][:, :NE])
                r.then_inc(s_sc.h, 1)
            block.scalar(_)
            s_sc.inc(1)
        sc_routing = s_sc.n
        xe_pre_comb = s_xe.n
        def _routing(v, sc_routing=sc_routing, xe_pre_comb=xe_pre_comb):
            v.wait_ge(s_sc.h, sc_routing)
            v.wait_ge(s_xe.h, xe_pre_comb)
            for sub in range(4):
                v.tensor_tensor(out=btok[:, sub, :], in0=stok[:, sub, :], in1=gb_bcast[:, :], op=OP.add)
            for sub in range(4):
                v.tensor_copy(out=rtok[:, sub, :], in_=btok[:, sub, :])
            for rnd in range(4):
                for sub in range(4):
                    v.tensor_reduce(out=thr8[:, sub:sub + 1], in_=rtok[:, sub, :], axis=AX.X, op=OP.max)
                if rnd == 1:
                    v.drain()
                if rnd < 3:
                    for sub in range(4):
                        v.tensor_scalar(out=mske[:, sub, :], in0=rtok[:, sub, :],
                                        scalar1=thr8[:, sub:sub + 1], scalar2=None, op0=OP.is_ge)
                    for sub in range(4):
                        v.scalar_tensor_tensor(out=rtok[:, sub, :], in0=mske[:, sub, :], scalar=-1e30,
                                               in1=rtok[:, sub, :], op0=OP.mult, op1=OP.add)
                else:
                    v.drain()
            for sub in range(4):
                v.tensor_scalar(out=mske[:, sub, :], in0=btok[:, sub, :],
                                scalar1=thr8[:, sub:sub + 1], scalar2=None, op0=OP.is_ge)
            for sub in range(4):
                v.tensor_tensor(out=rtok[:, sub, :], in0=stok[:, sub, :], in1=mske[:, sub, :], op=OP.mult)
            for sub in range(4):
                v.tensor_reduce(out=invd8[:, sub:sub + 1], in_=rtok[:, sub, :], axis=AX.X, op=OP.add)
            v.drain()
            v.reciprocal(out=invd8[:, :], in_=invd8[:, :])
            v.drain()
            for sub in range(4):
                v.tensor_scalar(out=btok[:, sub, :], in0=rtok[:, sub, :], scalar1=invd8[:, sub:sub + 1],
                                scalar2=2.5, op0=OP.mult, op1=OP.mult)
            v.drain()
            v.tensor_copy(out=rtok[:, :, :], in_=btok[:, :, :])
            v.drain()
            v.scalar_tensor_tensor(out=btok[:, :, :], in0=rtok[:, :, :], scalar=Mcol[:, :],
                                   in1=rtok[:, :, :], op0=OP.mult, op1=OP.add)
            v.drain()
            r = v.scalar_tensor_tensor(out=btok[:, :, :], in0=rtok[:, :, :], scalar=negM[:, :],
                                       in1=btok[:, :, :], op0=OP.mult, op1=OP.add)
            r.then_inc(s_ve.h, 1)
        block.vector(_routing)
        s_ve.inc(1)
        comb_ready_ve = s_ve.n

        def _(s, t0=t0, comb_ready_ve=comb_ready_ve):
            s.wait_ge(s_ve.h, comb_ready_ve)
            s.dma_start(out=comb_tok[t0:t0 + TT, :].rearrange("(s p) e -> p s e", p=128),
                        in_=btok[:, :, :]).then_inc(s_xe.h, 16)
        block.sync(_)
        s_xe.inc(16)

        for sub in range(4):
            sc_g = s_sc.n
            def _(t, sub=sub, sc_g=sc_g):
                t.wait_ge(s_ve.h, comb_ready_ve)
                if sub > 0:
                    t.wait_ge(s_sc.h, sc_g)
                r = t.matmul(P[6][:NE, :128], btok[:, sub, :], eye_sb[:, :], start=True, stop=True)
                r.then_inc(s_mm.h, 1)
            block.tensor(_)
            s_mm.inc(1)
            mm_g = s_mm.n
            def _(s, sub=sub, mm_g=mm_g):
                s.wait_ge(s_mm.h, mm_g)
                r = s.copy(out=comb_em[:, sub * 128:(sub + 1) * 128], in_=P[6][:NE, :128])
                r.then_inc(s_sc.h, 1)
            block.scalar(_)
            s_sc.inc(1)
        sc_g = s_sc.n
        def _(v, sc_g=sc_g):
            v.wait_ge(s_sc.h, sc_g)
            v.tensor_scalar(out=m01, in0=comb_em, scalar1=0.0, scalar2=None, op0=OP.is_gt)
            v.tensor_tensor_scan(out=incl_t, data0=ones16[:, :], data1=m01,
                                 initial=cnt_vec[:, :], op0=OP.mult, op1=OP.add)
            v.tensor_scalar(out=srel_t, in0=incl_t, scalar1=-1.0, scalar2=None, op0=OP.add)
            v.drain()
            v.tensor_copy(out=cnt_vec[:, :], in_=incl_t[:, TT - 1:TT])
            v.tensor_scalar(out=tmp_t, in0=srel_t, scalar1=float(CEXP), scalar2=None, op0=OP.is_lt)
            v.tensor_tensor(out=m01, in0=m01, in1=tmp_t, op=OP.mult)
            v.tensor_scalar(out=srel_t, in0=srel_t, scalar1=ebase_sb[:NE, :], scalar2=-BIGS,
                            op0=OP.add, op1=OP.add)
            v.tensor_tensor(out=srel_t, in0=srel_t, in1=m01, op=OP.mult)
            r = v.tensor_scalar(out=srel_t, in0=srel_t, scalar1=BIGS, scalar2=None, op0=OP.add)
            r.then_inc(s_ve.h, 1)
        block.vector(_)
        s_ve.inc(1)
        for sub in range(4):
            ve_g = s_ve.n
            sc_g = s_sc.n
            def _(t, sub=sub, ve_g=ve_g, sc_g=sc_g):
                t.wait_ge(s_ve.h, ve_g)
                if sub > 0:
                    t.wait_ge(s_sc.h, sc_g)
                r = t.matmul(P[7][:, :NE], srel_t[:, sub * 128:(sub + 1) * 128], eye_sb[:NE, :NE], start=True, stop=True)
                r.then_inc(s_mm.h, 1)
            block.tensor(_)
            s_mm.inc(1)
            mm_g = s_mm.n
            def _(s, sub=sub, mm_g=mm_g):
                s.wait_ge(s_mm.h, mm_g)
                r = s.copy(out=slots_f[:, sub, :], in_=P[7][:, :NE])
                r.then_inc(s_sc.h, 1)
            block.scalar(_)
            s_sc.inc(1)
        sc_g = s_sc.n
        isc_pre = s_isc.n
        def _(v, sc_g=sc_g, isc_pre=isc_pre):
            v.wait_ge(s_sc.h, sc_g)
            v.wait_ge(s_isc.h, isc_pre)
            r = v.tensor_copy(out=slots_u[:, :], in_=slots_f[:, :, :])
            r.then_inc(s_ve.h, 1)
        block.vector(_)
        s_ve.inc(1)
        ve_slots = s_ve.n
        isc_base = s_isc.n
        def _(g, t0=t0, ve_slots=ve_slots, isc_base=isc_base):
            g.wait_ge(s_ve.h, ve_slots)
            g.wait_ge(s_isc.h, isc_base)
            g.iota(ids_u[:, :], [[128, 4]], base=t0, channel_multiplier=1,
                   allow_small_or_imprecise_dtypes=True)
            n = 0
            for sub in range(4):
                for e in range(NE):
                    if n and n % 8 == 0:
                        g.wait_ge(s_isc.h, isc_base + 16 * (n - 8))
                    g.indirect_dma_start(
                        out=idx_tab,
                        out_offset=bass.IndirectOffsetOnAxis(ap=slots_u[:, sub * NE + e:sub * NE + e + 1], axis=0),
                        in_=ids_u[:, sub:sub + 1], in_offset=None,
                        bounds_check=regbox["slotbnd"], oob_is_err=False,
                    ).then_inc(s_isc.h, 16)
                    n += 1
        block.gpsimd(_)
        s_isc.inc(16 * 64)

        for sub in range(4):
            par = sub % 2
            xe_pre = s_xe.n
            for half in range(2):
                cs = range(half * 4, half * 4 + 4)
                sc_g = s_sc.n
                def _(t, sub=sub, half=half, cs=cs, sc_g=sc_g):
                    t.wait_ge(s_ve.h, xn_ready)
                    t.wait_ge(s_sc.h, sc_g)
                    for i, c in enumerate(cs):
                        t.matmul(P[5][:, i * 128:(i + 1) * 128], xnh[:, c, sub * 128:(sub + 1) * 128],
                                 eye_bf[:, :], start=(i == 0), stop=(i == 3))
                        r = t.matmul(P[6][:, i * 128:(i + 1) * 128], xnl[:, c, sub * 128:(sub + 1) * 128],
                                     eye_bf[:, :], start=(i == 0), stop=(i == 3))
                    r.then_inc(s_mm.h, 1)
                block.tensor(_)
                s_mm.inc(1)
                mm_g = s_mm.n
                def _(s, sub=sub, half=half, par=par, mm_g=mm_g, xe_pre=xe_pre):
                    s.wait_ge(s_mm.h, mm_g)
                    if half == 0:
                        s.wait_ge(s_xe.h, xe_pre)
                    s.copy(out=tth[par][:, half * 512:(half + 1) * 512], in_=P[5][:, :])
                    r = s.copy(out=ttl[par][:, half * 512:(half + 1) * 512], in_=P[6][:, :])
                    r.then_inc(s_sc.h, 1)
                block.scalar(_)
                s_sc.inc(1)
            sc_g = s_sc.n
            def _(s, sub=sub, par=par, t0=t0, sc_g=sc_g):
                s.wait_ge(s_sc.h, sc_g)
                r0 = t0 + sub * 128
                s.dma_start(out=xn_tok[r0:r0 + 128, 0:D], in_=tth[par][:, :]).then_inc(s_xe.h, 16)
                s.dma_start(out=xn_tok[r0:r0 + 128, D:2 * D], in_=ttl[par][:, :]).then_inc(s_xe.h, 16)
            block.sync(_)
            s_xe.inc(32)

        for pi in range(12):
            wa_gate = panel_conv_ve[pi]
            slot = pi % 2
            for blk in range(2):
                kblk = pi * 2 + blk
                sect = kblk // 8
                loc = kblk % 8
                pband = P[2 * (kblk % 2)]
                ve_g = s_ve.n
                sc_g = s_sc.n
                def _(t, pi=pi, blk=blk, slot=slot, wa_gate=wa_gate, pband=pband, ve_g=ve_g, sc_g=sc_g):
                    t.wait_ge(s_ve.h, max(wa_gate, ve_g))
                    t.wait_ge(s_sc.h, sc_g)
                    for c in range(8):
                        mm3(t, pband[:, :],
                            phl[slot][0][:, c, blk * 128:(blk + 1) * 128],
                            phl[slot][1][:, c, blk * 128:(blk + 1) * 128],
                            xnh[:, c, :], xnl[:, c, :],
                            first=(c == 0), last=(c == 7), inc=(c == 7))
                block.tensor(_)
                s_mm.inc(1)
                mm_g = s_mm.n
                if sect == 0:
                    def _(s, loc=loc, pband=pband, mm_g=mm_g):
                        s.wait_ge(s_mm.h, mm_g)
                        s.activation(out=scr[:, 0:TT], in_=pband[:, :], func=AF.Exp,
                                     bias=db_comb[:, loc:loc + 1])
                        s.add(scr[:, 0:TT], scr[:, 0:TT], 1.0)
                        s.activation(out=scr[:, 0:TT], in_=scr[:, 0:TT], func=AF.Ln)
                        r = s.activation(out=Ra[:, loc, :], in_=scr[:, 0:TT],
                                         func=AF.Exp, scale=alpha_n[:, loc:loc + 1])
                        r.then_inc(s_sc.h, 1)
                    block.scalar(_)
                    s_sc.inc(1)
                    sc_g = s_sc.n
                    def _(v, loc=loc, sc_g=sc_g):
                        v.wait_ge(s_sc.h, sc_g)
                        r = v.tensor_scalar(out=Ra[:, loc, :], in0=Ra[:, loc, :],
                                            scalar1=1.0 - EPS_A, scalar2=None, op0=OP.min)
                        r.then_inc(s_ve.h, 1)
                    block.vector(_)
                    s_ve.inc(1)
                else:
                    dst = Rb if sect == 1 else Rc
                    bias_t = bp_b if sect == 1 else bp_c
                    def _(s, loc=loc, pband=pband, dst=dst, bias_t=bias_t, mm_g=mm_g):
                        s.wait_ge(s_mm.h, mm_g)
                        r = s.activation(out=dst[:, loc, :], in_=pband[:, :],
                                         func=AF.Tanh, bias=bias_t[:, loc:loc + 1])
                        r.then_inc(s_sc.h, 1)
                    block.scalar(_)
                    s_sc.inc(1)
            if pi_loaded < 20:
                emit_panel(pi_loaded, s_mm.n)
                pi_loaded += 1

        for c in range(8):
            sc_g = s_sc.n
            def _(v, c=c, sc_g=sc_g):
                v.wait_ge(s_sc.h, sc_g)
                a_, b_, c_ = Ra[:, c, :], Rb[:, c, :], Rc[:, c, :]
                p_, r_, q_, d_ = scr[:, 512:1024], scr[:, 1024:1536], scr[:, 1536:2048], scr[:, 2048:2560]
                v.tensor_tensor(out=p_, in0=a_, in1=a_, op=OP.mult)
                v.tensor_tensor(out=q_, in0=c_, in1=c_, op=OP.mult)
                v.tensor_tensor(out=p_, in0=p_, in1=q_, op=OP.add)
                v.tensor_tensor(out=r_, in0=b_, in1=b_, op=OP.mult)
                v.tensor_tensor(out=q_, in0=a_, in1=b_, op=OP.mult)
                v.tensor_tensor(out=d_, in0=p_, in1=r_, op=OP.subtract)
                v.tensor_tensor(out=d_, in0=d_, in1=d_, op=OP.mult)
                v.tensor_tensor(out=q_, in0=q_, in1=q_, op=OP.mult)
                v.tensor_scalar(out=q_, in0=q_, scalar1=4.0, scalar2=None, op0=OP.mult)
                v.tensor_tensor(out=d_, in0=d_, in1=q_, op=OP.add)
                r2 = v.tensor_tensor(out=p_, in0=p_, in1=r_, op=OP.add)
                r2.then_inc(s_ve.h, 1)
            block.vector(_)
            s_ve.inc(1)
            ve_g = s_ve.n
            def _(s, ve_g=ve_g):
                s.wait_ge(s_ve.h, ve_g)
                r2 = s.activation(out=scr[:, 2560:3072], in_=scr[:, 2048:2560], func=AF.Sqrt,
                                  bias=cpool[:, 1:2])
                r2.then_inc(s_sc.h, 1)
            block.scalar(_)
            s_sc.inc(1)
            sc_g = s_sc.n
            def _(v, sc_g=sc_g):
                v.wait_ge(s_sc.h, sc_g)
                r2 = v.tensor_tensor(out=scr[:, 512:1024], in0=scr[:, 512:1024],
                                     in1=scr[:, 2560:3072], op=OP.add)
                r2.then_inc(s_ve.h, 1)
            block.vector(_)
            s_ve.inc(1)
            ve_g = s_ve.n
            def _(s, ve_g=ve_g):
                s.wait_ge(s_ve.h, ve_g)
                r2 = s.activation(out=scr[:, 512:1024], in_=scr[:, 512:1024], func=AF.Sqrt,
                                  scale=0.5, bias=cpool[:, 1:2])
                r2.then_inc(s_sc.h, 1)
            block.scalar(_)
            s_sc.inc(1)
            sc_g = s_sc.n
            def _(v, c=c, sc_g=sc_g):
                v.wait_ge(s_sc.h, sc_g)
                a_, b_, c_ = Ra[:, c, :], Rb[:, c, :], Rc[:, c, :]
                v.tensor_scalar(out=scr[:, 512:1024], in0=scr[:, 512:1024], scalar1=1.0,
                                scalar2=None, op0=OP.max)
                v.drain()
                v.reciprocal(out=scr[:, 1024:1536], in_=scr[:, 512:1024])
                v.drain()
                v.tensor_tensor(out=a_, in0=a_, in1=scr[:, 1024:1536], op=OP.mult)
                v.tensor_tensor(out=b_, in0=b_, in1=scr[:, 1024:1536], op=OP.mult)
                r2 = v.tensor_tensor(out=c_, in0=c_, in1=scr[:, 1024:1536], op=OP.mult)
                r2.then_inc(s_ve.h, 1)
            block.vector(_)
            s_ve.inc(1)

        for pp in range(4):
            pi = 12 + pp
            slot = pi % 2
            wa_gate = panel_conv_ve[pi]
            for blk in range(2):
                nblk = pp * 2 + blk
                pband = P[4 + (nblk % 2)]
                ve_g = s_ve.n
                def _(t, pi=pi, blk=blk, slot=slot, wa_gate=wa_gate, pband=pband, ve_g=ve_g):
                    t.wait_ge(s_ve.h, max(wa_gate, ve_g))
                    for c in range(8):
                        mm3(t, pband[:, :],
                            phl[slot][0][:, c, blk * 128:(blk + 1) * 128],
                            phl[slot][1][:, c, blk * 128:(blk + 1) * 128],
                            xnh[:, c, :], xnl[:, c, :],
                            first=(c == 0), last=(c == 7), inc=(c == 7))
                block.tensor(_)
                s_mm.inc(1)
                mm_g = s_mm.n
                def _(v, nblk=nblk, pband=pband, mm_g=mm_g):
                    v.wait_ge(s_mm.h, mm_g)
                    r = v.tensor_tensor(out=Rb[:, nblk, :], in0=Rb[:, nblk, :],
                                        in1=pband[:, :], op=OP.mult)
                    r.then_inc(s_ve.h, 1)
                block.vector(_)
                s_ve.inc(1)
            if pi_loaded < 20:
                emit_panel(pi_loaded, s_mm.n)
                pi_loaded += 1

        for c in range(8):
            def _(v, c=c):
                v.tensor_tensor_scan(out=scr[:, 0:TT], data0=Ra[:, c, :], data1=Rb[:, c, :],
                                     initial=carry[:, c:c + 1], op0=OP.mult, op1=OP.add)
                v.tensor_tensor(out=Ra[:, c, 0:1], in0=Rc[:, c, 0:1], in1=carry[:, c:c + 1], op=OP.mult)
                v.tensor_tensor(out=Ra[:, c, 1:], in0=Rc[:, c, 1:], in1=scr[:, 0:TT - 1], op=OP.mult)
                v.tensor_copy(out=carry[:, c:c + 1], in_=scr[:, TT - 1:TT])
                v.tensor_copy(out=xnh[:, c, :], in_=Ra[:, c, :])
                r = v.tensor_tensor(out=xnl[:, c, :], in0=Ra[:, c, :], in1=xnh[:, c, :], op=OP.subtract)
                r.then_inc(s_ve.h, 1)
            block.vector(_)
            s_ve.inc(1)

        for pp in range(4):
            pi = 16 + pp
            slot = pi % 2
            wa_gate = panel_conv_ve[pi]
            for blk in range(2):
                oblk = pp * 2 + blk
                pband = P[4 + (oblk % 2)]
                ve_g = s_ve.n
                def _(t, pi=pi, blk=blk, slot=slot, wa_gate=wa_gate, pband=pband, ve_g=ve_g):
                    t.wait_ge(s_ve.h, max(wa_gate, ve_g))
                    for c in range(8):
                        mm3(t, pband[:, :],
                            phl[slot][0][:, c, blk * 128:(blk + 1) * 128],
                            phl[slot][1][:, c, blk * 128:(blk + 1) * 128],
                            xnh[:, c, :], xnl[:, c, :],
                            first=(c == 0), last=(c == 7), inc=(c == 7))
                block.tensor(_)
                s_mm.inc(1)
                mm_g = s_mm.n
                def _(v, oblk=oblk, pband=pband, mm_g=mm_g):
                    v.wait_ge(s_mm.h, mm_g)
                    r = v.scalar_tensor_tensor(out=x_sb[:, oblk, :], in0=pband[:, :],
                                               scalar=attn_scale[:, :], in1=x_sb[:, oblk, :],
                                               op0=OP.mult, op1=OP.add)
                    r.then_inc(s_ve.h, 1)
                block.vector(_)
                s_ve.inc(1)
            if pi_loaded < 20:
                emit_panel(pi_loaded, s_mm.n)
                pi_loaded += 1

        for sub in range(4):
            par = sub % 2
            ot_pre = s_otok.n
            for half in range(2):
                cs = range(half * 4, half * 4 + 4)
                ve_g = s_ve.n
                sc_g = s_sc.n
                def _(t, sub=sub, half=half, cs=cs, ve_g=ve_g, sc_g=sc_g):
                    t.wait_ge(s_ve.h, ve_g)
                    t.wait_ge(s_sc.h, sc_g)
                    for i, c in enumerate(cs):
                        r = t.matmul(P[6][:, i * 128:(i + 1) * 128], x_sb[:, c, sub * 128:(sub + 1) * 128],
                                     eye_sb[:, :], start=(i == 0), stop=(i == 3))
                    r.then_inc(s_mm.h, 1)
                block.tensor(_)
                s_mm.inc(1)
                mm_g = s_mm.n
                def _(s, sub=sub, half=half, par=par, mm_g=mm_g, ot_pre=ot_pre):
                    s.wait_ge(s_mm.h, mm_g)
                    if half == 0:
                        s.wait_ge(s_otok.h, ot_pre)
                    r = s.copy(out=otok[par][:, half * 512:(half + 1) * 512], in_=P[6][:, :])
                    r.then_inc(s_sc.h, 1)
                block.scalar(_)
                s_sc.inc(1)
            sc_g = s_sc.n
            def _(g, sub=sub, par=par, t0=t0, sc_g=sc_g):
                g.wait_ge(s_sc.h, sc_g)
                r0 = t0 + sub * 128
                g.dma_start(out=out_tok[r0:r0 + 128, :], in_=otok[par][:, :]).then_inc(s_otok.h, 16)
            block.gpsimd(_)
            s_otok.inc(16)

    esB.close()

    esC = ExitStack()
    land13 = esC.enter_context(nc.sbuf_tensor("land13", [128, 8, 1024], F32))
    land2 = esC.enter_context(nc.sbuf_tensor("land2", [128, 4, 1024], F32))
    w13h = esC.enter_context(nc.sbuf_tensor("w13h", [128, 8, 1024], BF16))
    w13l = esC.enter_context(nc.sbuf_tensor("w13l", [128, 8, 1024], BF16))
    w2h = esC.enter_context(nc.sbuf_tensor("w2h", [128, 4, 1024], BF16))
    w2l = esC.enter_context(nc.sbuf_tensor("w2l", [128, 4, 1024], BF16))
    xg = esC.enter_context(nc.sbuf_tensor("xg", [128, 8, 2 * D], BF16))
    cg = esC.enter_context(nc.sbuf_tensor("cg", [128, 8, NE], F32))
    cekeep = esC.enter_context(nc.sbuf_tensor("cekeep", [128, 32], F32))
    xfh = esC.enter_context(nc.sbuf_tensor("xfh", [128, 8, 512], BF16))
    xfl = esC.enter_context(nc.sbuf_tensor("xfl", [128, 8, 512], BF16))
    s_sb = esC.enter_context(nc.sbuf_tensor("s_sb", [128, 4, 512], F32))
    hh = esC.enter_context(nc.sbuf_tensor("hh", [128, 4, 512], BF16))
    hl = esC.enter_context(nc.sbuf_tensor("hl", [128, 4, 512], BF16))
    y_fm = esC.enter_context(nc.sbuf_tensor("y_fm", [128, 8, 512], F32))
    y_tok = [esC.enter_context(nc.sbuf_tensor(f"y_tok{i}", [128, 1024], F32)) for i in range(4)]
    idx_sb = esC.enter_context(nc.sbuf_tensor("idx_sb", [128, 64], U32))
    PC = [esC.enter_context(nc.psum_tensor(f"PC{i}", [128, 512], F32)) for i in range(8)]
    blockC = esC.enter_context(nc.Block())
    block = blockC

    xe_total = s_xe.n
    isc_total = s_isc.n
    otok_total = s_otok.n

    def _(g):
        g.memset(xg[:, :, :], 0.0)
        g.memset(cg[:, :, :], 0.0)
        g.wait_ge(s_xe.h, xe_total)
        g.wait_ge(s_isc.h, isc_total)
        g.wait_ge(s_otok.h, otok_total)
    block.gpsimd(_)

    pslot_mm = {}
    pslot_ve = {}
    prev_yout_ve = s_ve.n
    prev_w2_mm = s_mm.n
    ehist = []
    for e in range(NE + 1):
        shared = (e == NE)
        NK = 32 if shared else NE
        epar = e % 2
        acc_expert_base = s_acc.n
        add_cnt = {}

        ehist.append(([gslot[j].n for j in range(8)], s_acc.n))
        gat_pre, acc_pre_e = ehist[e - 1] if e >= 1 else ehist[e]
        ve_conv_pre = s_ve.n
        def _(s, e=e, shared=shared, epar=epar, gat_pre=gat_pre, ve_conv_pre=ve_conv_pre, acc_pre_e=acc_pre_e):
            s.wait_ge(s_ve.h, ve_conv_pre)
            if shared:
                s.dma_start(out=land13[:, :, :],
                            in_=w13sT.rearrange("(c p) f -> p c f", p=128)).then_inc(s_we.h, 16)
                s.dma_start(out=land2[:, :, :],
                            in_=w2sT.rearrange("(c p) d2 -> p c d2", p=128)).then_inc(s_we.h, 16)
            else:
                s.dma_start(out=land13[:, :, :],
                            in_=w13T[e].rearrange("(c p) f -> p c f", p=128)).then_inc(s_we.h, 16)
                s.dma_start(out=land2[:, :, :],
                            in_=w2T[e].rearrange("(c p) d2 -> p c d2", p=128)).then_inc(s_we.h, 16)
                for j in range(8):
                    s.wait_ge(gslot[j].h, gat_pre[j])
                s.wait_ge(s_isc.h, isc_total)
                s.wait_ge(s_acc.h, acc_pre_e)
                s.dma_start(out=idx_sb[:, epar * 32:epar * 32 + 16],
                            in_=idx_tab[e * CEXP:(e + 1) * CEXP, :].rearrange("(p k) x -> p (k x)", p=128)
                            ).then_inc(s_we.h, 16)
        block.sync(_)
        s_we.inc(32 if shared else 48)
        we_ready = s_we.n
        we_w_ready = we_ready if shared else we_ready - 16
        conv13_c = []
        conv2_c = []
        for c in range(8):
            def _(v, c=c, we_w_ready=we_w_ready, prev_w2_mm=prev_w2_mm):
                if c == 0:
                    v.wait_ge(s_we.h, we_w_ready)
                    v.wait_ge(s_mm.h, prev_w2_mm)
                v.tensor_copy(out=w13h[:, c, :], in_=land13[:, c, :])
                r = v.tensor_tensor(out=w13l[:, c, :], in0=land13[:, c, :], in1=w13h[:, c, :], op=OP.subtract)
                r.then_inc(s_ve.h, 1)
            block.vector(_)
            s_ve.inc(1)
            conv13_c.append(s_ve.n)
        for c in range(4):
            def _(v, c=c):
                v.tensor_copy(out=w2h[:, c, :], in_=land2[:, c, :])
                r = v.tensor_tensor(out=w2l[:, c, :], in0=land2[:, c, :], in1=w2h[:, c, :], op=OP.subtract)
                r.then_inc(s_ve.h, 1)
            block.vector(_)
            s_ve.inc(1)
            conv2_c.append(s_ve.n)
        w_ready = conv13_c[0]

        idx_ready = 0 if shared else we_ready
        trans_mm_cnt = {}
        gat_cnt = {}
        ce_ve_cnt = {}

        def emit_gather(k, mm_pre, ve_pre_g, we_pre, e=e, epar=epar, shared=shared):
            kk8 = k % 8
            def _(g, k=k, kk8=kk8, mm_pre=mm_pre, ve_pre_g=ve_pre_g, we_pre=we_pre):
                g.wait_ge(s_mm.h, mm_pre)
                g.wait_ge(s_ve.h, ve_pre_g)
                if we_pre:
                    g.wait_ge(s_we.h, we_pre)
                if shared:
                    g.dma_start(out=xg[:, kk8, :], in_=xn_tok[k * 128:(k + 1) * 128, :]
                                ).then_inc(gslot[kk8].h, 16)
                else:
                    g.indirect_dma_start(
                        out=xg[:, kk8, :], out_offset=None,
                        in_=xn_tok,
                        in_offset=bass.IndirectOffsetOnAxis(ap=idx_sb[:, epar * 32 + k:epar * 32 + k + 1], axis=0),
                        bounds_check=regbox["tokbnd"], oob_is_err=False,
                    ).then_inc(gslot[kk8].h, 16)
                    g.indirect_dma_start(
                        out=cg[:, kk8, :], out_offset=None,
                        in_=comb_tok,
                        in_offset=bass.IndirectOffsetOnAxis(ap=idx_sb[:, epar * 32 + k:epar * 32 + k + 1], axis=0),
                        bounds_check=regbox["tokbnd"], oob_is_err=False,
                    ).then_inc(gslot[kk8].h, 16)
            block.gpsimd(_)
            gslot[kk8].inc(16 if shared else 32)
            gat_cnt[k] = gslot[kk8].n

        def emit_cecopy(k, e=e, shared=shared):
            if shared:
                ce_ve_cnt[k] = s_ve.n
                return
            kk8 = k % 8
            def _(v, k=k, kk8=kk8, gat_g=gat_cnt[k]):
                v.wait_ge(gslot[kk8].h, gat_g)
                r = v.tensor_copy(out=cekeep[:, k:k + 1], in_=cg[:, kk8, e:e + 1])
                r.then_inc(s_ve.h, 1)
            block.vector(_)
            s_ve.inc(1)
            ce_ve_cnt[k] = s_ve.n

        for k in range(min(8, NK)):
            emit_gather(k, pslot_mm.get(k % 8, 0), pslot_ve.get(k % 8, 0), idx_ready)
            emit_cecopy(k)

        prev_silu_sc = s_sc.n
        prev_yfm_ve = s_ve.n
        for grp in range(NK // 4):
            for kk in range(4):
                k = grp * 4 + kk
                kk8 = k % 8
                gat_g = gat_cnt[k]
                for half in range(2):
                    cs = range(half * 4, half * 4 + 4)
                    sc_g = s_sc.n
                    ve_g2 = s_ve.n
                    def _(t, k=k, kk8=kk8, half=half, cs=cs, gat_g=gat_g, sc_g=sc_g, ve_g2=ve_g2):
                        t.wait_ge(gslot[kk8].h, gat_g)
                        t.wait_ge(s_sc.h, sc_g)
                        t.wait_ge(s_ve.h, ve_g2)
                        for i, c in enumerate(cs):
                            t.matmul(PC[6][:, i * 128:(i + 1) * 128], xg[:, kk8, c * 128:(c + 1) * 128],
                                     eye_bf[:, :], start=(i == 0), stop=(i == 3))
                            r = t.matmul(PC[7][:, i * 128:(i + 1) * 128], xg[:, kk8, D + c * 128:D + (c + 1) * 128],
                                         eye_bf[:, :], start=(i == 0), stop=(i == 3))
                        r.then_inc(s_mm.h, 1)
                    block.tensor(_)
                    s_mm.inc(1)
                    mm_g = s_mm.n
                    def _(s, kk=kk, half=half, mm_g=mm_g):
                        s.wait_ge(s_mm.h, mm_g)
                        r = s.copy(out=xfh[:, half * 4:(half + 1) * 4, kk * 128:(kk + 1) * 128], in_=PC[6][:, :])
                        r.then_inc(s_sc.h, 1)
                    block.scalar(_)
                    s_sc.inc(1)
                    def _(v, kk=kk, half=half, mm_g=mm_g):
                        v.wait_ge(s_mm.h, mm_g)
                        r = v.tensor_copy(out=xfl[:, half * 4:(half + 1) * 4, kk * 128:(kk + 1) * 128], in_=PC[7][:, :])
                        r.then_inc(s_ve.h, 1)
                    block.vector(_)
                    s_ve.inc(1)
                trans_mm_cnt[k] = s_mm.n
                if k + 8 < NK:
                    emit_gather(k + 8, trans_mm_cnt[k], ce_ve_cnt[k], 0)
            xf_ready_sc = s_sc.n
            xf_ready_ve = s_ve.n

            def _(t, xf_ready_sc=xf_ready_sc, xf_ready_ve=xf_ready_ve, conv13_c=conv13_c,
                  prev_silu_sc=prev_silu_sc, prev_yfm_ve=prev_yfm_ve):
                t.wait_ge(s_sc.h, max(xf_ready_sc, prev_silu_sc))
                t.wait_ge(s_ve.h, max(xf_ready_ve, prev_yfm_ve))
                for c in range(8):
                    t.wait_ge(s_ve.h, conv13_c[c])
                    for fb in range(4):
                        mm3(t, PC[fb][:, :],
                            w13h[:, c, fb * 128:(fb + 1) * 128], w13l[:, c, fb * 128:(fb + 1) * 128],
                            xfh[:, c, :], xfl[:, c, :], first=(c == 0), last=(c == 7), inc=(c == 7))
            block.tensor(_)
            s_mm.inc(4)
            mm_gate = s_mm.n
            def _(s, mm_gate=mm_gate):
                for fb in range(4):
                    s.wait_ge(s_mm.h, mm_gate - 3 + fb)
                    r = s.activation(out=s_sb[:, fb, :], in_=PC[fb][:, :], func=AF.Silu)
                r.then_inc(s_sc.h, 1)
            block.scalar(_)
            s_sc.inc(1)
            prev_silu_sc = s_sc.n
            def _(t, prev_yout_ve=prev_yout_ve):
                t.wait_ge(s_ve.h, prev_yout_ve)
                for c in range(8):
                    for fb in range(4):
                        mm3(t, PC[4 + fb][:, :],
                            w13h[:, c, 512 + fb * 128:512 + (fb + 1) * 128],
                            w13l[:, c, 512 + fb * 128:512 + (fb + 1) * 128],
                            xfh[:, c, :], xfl[:, c, :], first=(c == 0), last=(c == 7), inc=(c == 7))
            block.tensor(_)
            s_mm.inc(4)
            mm_up = s_mm.n
            sc_silu = s_sc.n
            def _(v, mm_up=mm_up, sc_silu=sc_silu):
                v.wait_ge(s_sc.h, sc_silu)
                for fb in range(4):
                    v.wait_ge(s_mm.h, mm_up - 3 + fb)
                    v.tensor_tensor(out=s_sb[:, fb, :], in0=s_sb[:, fb, :], in1=PC[4 + fb][:, :], op=OP.mult)
                for fb in range(4):
                    v.tensor_copy(out=hh[:, fb, :], in_=s_sb[:, fb, :])
                    r = v.tensor_tensor(out=hl[:, fb, :], in0=s_sb[:, fb, :], in1=hh[:, fb, :], op=OP.subtract)
                r.then_inc(s_ve.h, 1)
            block.vector(_)
            s_ve.inc(1)
            h_ready = s_ve.n
            def _(t, h_ready=h_ready, conv2_c=conv2_c):
                t.wait_ge(s_ve.h, h_ready)
                for fb in range(4):
                    t.wait_ge(s_ve.h, conv2_c[fb])
                    for db in range(8):
                        mm3(t, PC[db][:, :],
                            w2h[:, fb, db * 128:(db + 1) * 128], w2l[:, fb, db * 128:(db + 1) * 128],
                            hh[:, fb, :], hl[:, fb, :], first=(fb == 0), last=(fb == 3), inc=(fb == 3))
            block.tensor(_)
            s_mm.inc(8)
            mm_y = s_mm.n
            prev_w2_mm = mm_y
            def _(v, mm_y=mm_y):
                for db in range(8):
                    v.wait_ge(s_mm.h, mm_y - 7 + db)
                    r = v.tensor_copy(out=y_fm[:, db, :], in_=PC[db][:, :])
                r.then_inc(s_ve.h, 1)
            block.vector(_)
            s_ve.inc(1)
            yfm_ready = s_ve.n
            prev_yfm_ve = yfm_ready

            for kk in range(4):
                k = grp * 4 + kk
                ypar = k % 4
                acc_pre = add_cnt.get(k - 4, acc_expert_base)
                for half in range(2):
                    dbs = range(half * 4, half * 4 + 4)
                    ve_g = s_ve.n
                    def _(t, kk=kk, half=half, dbs=dbs, yfm_ready=yfm_ready, ve_g=ve_g):
                        t.wait_ge(s_ve.h, max(yfm_ready, ve_g))
                        for i, db in enumerate(dbs):
                            r = t.matmul(PC[4 + half][:, i * 128:(i + 1) * 128], y_fm[:, db, kk * 128:(kk + 1) * 128],
                                         eye_sb[:, :], start=(i == 0), stop=(i == 3))
                        r.then_inc(s_mm.h, 1)
                    block.tensor(_)
                    s_mm.inc(1)
                    mm_g = s_mm.n
                    def _(v, k=k, kk=kk, half=half, ypar=ypar, mm_g=mm_g, shared=shared, acc_pre=acc_pre):
                        v.wait_ge(s_mm.h, mm_g)
                        if half == 0:
                            v.wait_ge(s_acc.h, acc_pre)
                        if shared:
                            r = v.tensor_copy(out=y_tok[ypar][:, half * 512:(half + 1) * 512],
                                              in_=PC[4 + half][:, :])
                        else:
                            r = v.tensor_scalar(out=y_tok[ypar][:, half * 512:(half + 1) * 512],
                                                in0=PC[4 + half][:, :],
                                                scalar1=cekeep[:, k:k + 1], scalar2=None, op0=OP.mult)
                        r.then_inc(s_ve.h, 1)
                    block.vector(_)
                    s_ve.inc(1)
                ve_g = s_ve.n
                def _(g, e=e, k=k, ypar=ypar, epar=epar, shared=shared, ve_g=ve_g,
                      acc_expert_base=acc_expert_base, first=(k == 0)):
                    g.wait_ge(s_ve.h, ve_g)
                    if first:
                        g.wait_ge(s_acc.h, acc_expert_base)
                    if shared:
                        g.dma_start(out=out_tok[k * 128:(k + 1) * 128, :], in_=y_tok[ypar][:, :],
                                    accum_op=OP.add).then_inc(s_acc.h, 16)
                    else:
                        g.indirect_dma_start(
                            out=out_tok,
                            out_offset=bass.IndirectOffsetOnAxis(ap=idx_sb[:, epar * 32 + k:epar * 32 + k + 1], axis=0),
                            in_=y_tok[ypar][:, :], in_offset=None,
                            compute_op=OP.add,
                        ).then_inc(s_acc.h, 16)
                block.gpsimd(_)
                s_acc.inc(16)
                add_cnt[k] = s_acc.n
            prev_yout_ve = s_ve.n

            for kk in range(4):
                k2 = (grp + 2) * 4 + kk
                if k2 < NK:
                    emit_cecopy(k2)

        for j in range(min(8, NK)):
            last = NK - 8 + j
            pslot_mm[j] = trans_mm_cnt[last]
            pslot_ve[j] = ce_ve_cnt[last]

    acc_total = s_acc.n
    def _(g, acc_total=acc_total):
        g.wait_ge(s_acc.h, acc_total)
    block.gpsimd(_)
    esC.close()

    es0.close()
    return nc


_NC_CACHE = None


def kernel(L0_attn_u, L0_attn_W_in, L0_attn_W_out, L0_attn_Wp, L0_attn_bp,
           L0_attn_alpha_log, L0_attn_delta_bias, L0_attn_log_gamma,
           L0_ffn_input_ids, L0_ffn_gate_w, L0_ffn_gate_b, L0_ffn_w13, L0_ffn_w2,
           L0_ffn_w13_s, L0_ffn_w2_s):
    global _NC_CACHE
    u = np.asarray(L0_attn_u, dtype=np.float32)

    c = np.ascontiguousarray
    w_inT = c(np.asarray(L0_attn_W_in, np.float32).T)
    w_outT = c(np.asarray(L0_attn_W_out, np.float32).T)
    wpT = c(np.asarray(L0_attn_Wp, np.float32).T)
    gate_wT = c(np.asarray(L0_ffn_gate_w, np.float32).T)
    gate_b = c(np.asarray(L0_ffn_gate_b, np.float32).reshape(NE, 1))
    w13T = c(np.asarray(L0_ffn_w13, np.float32).transpose(0, 2, 1))
    w2T = c(np.asarray(L0_ffn_w2, np.float32).transpose(0, 2, 1))
    w13sT = c(np.asarray(L0_ffn_w13_s, np.float32).T)
    w2sT = c(np.asarray(L0_ffn_w2_s, np.float32).T)
    alpha_log = c(np.asarray(L0_attn_alpha_log, np.float32).reshape(8, 128).T)
    delta_bias = c(np.asarray(L0_attn_delta_bias, np.float32).reshape(8, 128).T)
    bp = c(np.asarray(L0_attn_bp, np.float32).reshape(3, 8, 128).transpose(2, 0, 1).reshape(128, 24))
    log_gamma = c(np.asarray(L0_attn_log_gamma, np.float32).reshape(1, 1))
    eye = np.eye(128, dtype=np.float32)
    ebase = (np.arange(NE, dtype=np.float32) * CEXP).reshape(NE, 1)

    shared = dict(w_inT=w_inT, w_outT=w_outT, wpT=wpT, gate_wT=gate_wT, gate_b=gate_b,
                  w13T=w13T, w2T=w2T, w13sT=w13sT, w2sT=w2sT, alpha_log=alpha_log,
                  delta_bias=delta_bias, bp=bp, log_gamma=log_gamma, eye=eye, ebase=ebase)

    in_maps = []
    for b in range(N_CORES):
        m = dict(shared)
        m["xT"] = c(u[b].T)
        in_maps.append(m)

    if _NC_CACHE is None:
        _NC_CACHE = build_program()
    nc = _NC_CACHE

    res = run_bass_kernel_spmd(nc, in_maps, core_ids=list(range(N_CORES)))
    out = np.stack([np.asarray(res.results[b]["out_tok"])[:TC] for b in range(N_CORES)], axis=0)
    return out


# version 16
# speedup vs baseline: 1.0410x; 1.0381x over previous
import numpy as np
from contextlib import ExitStack

import concourse.bass as bass
import concourse.mybir as mybir
from concourse.bass_utils import run_bass_kernel_spmd

F32 = mybir.dt.float32
BF16 = mybir.dt.bfloat16
AF = mybir.ActivationFunctionType
OP = mybir.AluOpType
AX = mybir.AxisListType

N_CORES = 8
D = 1024
TC = 4096
TT = 1024
NTILE = TC // TT
NG = TT // 512
NE = 16
FI = 512
NSQ = 14
NPW = 6
SIGMA_STAGE = 99
DEBUG_SIGMA = False
EPS_A = 1e-4
LN_EPS = 1e-5


class Sem:
    def __init__(self, es, nc, name):
        self.h = es.enter_context(nc.semaphore(name))
        self.n = 0

    def inc(self, k=1):
        self.n += k
        return self.n


def emit_sigma(nc, block, es, wT, scale_out, consts, sems, tag):
    s_mm, s_ve, s_sc, s_ld = sems["mm"], sems["ve"], sems["sc"], sems["ld"]
    ones_col, ones_row, ones_bf = consts["ones_col"], consts["ones_row"], consts["ones_bf"]

    wt = es.enter_context(nc.sbuf_tensor(f"sg_wt{tag}", [128, 8, 1024], F32))
    Bf = es.enter_context(nc.sbuf_tensor(f"sg_Bf{tag}", [128, 8, 1024], F32))
    M0 = es.enter_context(nc.sbuf_tensor(f"sg_M0{tag}", [128, 8, 1024], BF16))
    M1 = es.enter_context(nc.sbuf_tensor(f"sg_M1{tag}", [128, 8, 1024], BF16))
    junk = es.enter_context(nc.sbuf_tensor(f"sg_junk{tag}", [128, 512], BF16))
    acc = es.enter_context(nc.sbuf_tensor(f"sg_acc{tag}", [128, 16], F32))
    r1 = es.enter_context(nc.sbuf_tensor(f"sg_r1{tag}", [128, 1], F32))
    inv1 = es.enter_context(nc.sbuf_tensor(f"sg_inv1{tag}", [128, 1], F32))
    ssc = es.enter_context(nc.sbuf_tensor(f"sg_ssc{tag}", [1, 1], F32))
    vv = es.enter_context(nc.sbuf_tensor(f"sg_vv{tag}", [128, 8], F32))
    tt = es.enter_context(nc.sbuf_tensor(f"sg_tt{tag}", [128, 8], F32))
    prod = es.enter_context(nc.sbuf_tensor(f"sg_prod{tag}", [128, 8], F32))
    den_sb = es.enter_context(nc.sbuf_tensor(f"sg_den{tag}", [128, 8], F32))
    n2 = es.enter_context(nc.sbuf_tensor(f"sg_n2{tag}", [1, 2], F32))
    ps0 = es.enter_context(nc.psum_tensor(f"sg_ps0{tag}", [128, 512], F32))
    ps1 = es.enter_context(nc.psum_tensor(f"sg_ps1{tag}", [128, 512], F32))
    pv = es.enter_context(nc.psum_tensor(f"sg_pv{tag}", [128, 1], F32))
    pv2 = es.enter_context(nc.psum_tensor(f"sg_pv2{tag}", [128, 1], F32))

    ld_gate = s_ld.n + 16
    ve_prev = s_ve.n
    sc_prev = s_sc.n
    mm_prev = s_mm.n

    def _(s):
        s.wait_ge(s_ve.h, ve_prev)
        s.wait_ge(s_sc.h, sc_prev)
        s.wait_ge(s_mm.h, mm_prev)
        s.dma_start(out=wt[:, :, :], in_=wT.rearrange("(c p) n -> p c n", p=128)).then_inc(s_ld.h, 16)
    block.sync(_)
    s_ld.inc(16)

    def sq_pass(src, dst, is_b_build):
        mm_base = s_mm.n
        ve_base = s_ve.n
        sc_base = s_sc.n

        def _t(t):
            if is_b_build:
                t.wait_ge(s_ld.h, ld_gate)
            t.wait_ge(s_ve.h, ve_base)
            t.wait_ge(s_sc.h, sc_base)
            for k in range(16):
                iblk, jg = k // 2, k % 2
                ps = ps0 if jg == 0 else ps1
                if k >= 2:
                    t.wait_ge(s_ve.h, ve_base + k - 1)
                    t.wait_ge(s_sc.h, sc_base + k - 1)
                for d in range(8):
                    r = t.matmul(ps[:, :], src[:, d, iblk * 128:(iblk + 1) * 128],
                                 src[:, d, jg * 512:(jg + 1) * 512],
                                 start=(d == 0), stop=(d == 7))
                r.then_inc(s_mm.h, 1)
        block.tensor(_t)
        s_mm.inc(16)

        def _v(v):
            for k in range(16):
                iblk, jg = k // 2, k % 2
                ps = ps0 if jg == 0 else ps1
                v.wait_ge(s_mm.h, mm_base + k + 1)
                tgt = Bf if is_b_build else dst
                r = v.tensor_copy(out=tgt[:, iblk, jg * 512:(jg + 1) * 512], in_=ps[:, :])
                r.then_inc(s_ve.h, 1)
        block.vector(_v)
        s_ve.inc(16)

        def _s(s):
            for k in range(16):
                iblk, jg = k // 2, k % 2
                s.wait_ge(s_ve.h, ve_base + k + 1)
                src_sb = (Bf if is_b_build else dst)[:, iblk, jg * 512:(jg + 1) * 512]
                if is_b_build:
                    r = s.activation(out=M0[:, iblk, jg * 512:(jg + 1) * 512], in_=src_sb,
                                     func=AF.Copy, bias=0.0)
                else:
                    r = s.activation(out=junk[:, :], in_=src_sb, func=AF.Abs,
                                     accum_out=acc[:, k:k + 1])
                r.then_inc(s_sc.h, 1)
        block.scalar(_s)
        s_sc.inc(16)

    sq_pass(wt, None, True)

    if SIGMA_STAGE < 2:
        return
    bufs = [M0, M1]
    for it in range(NSQ):
        src, dst = bufs[it % 2], bufs[(it + 1) % 2]
        sq_pass(src, dst, False)

        sc_gate = s_sc.n
        def _v2(v, sc_gate=sc_gate):
            v.wait_ge(s_sc.h, sc_gate)
            r = v.tensor_reduce(out=r1[:, :], in_=acc[:, :], axis=AX.X, op=OP.add)
            r.then_inc(s_ve.h, 1)
        block.vector(_v2)
        s_ve.inc(1)
        ve_g = s_ve.n
        def _t2(t, ve_g=ve_g):
            t.wait_ge(s_ve.h, ve_g)
            r = t.matmul(pv[:1, :], ones_col[:, :], r1[:, :], start=True, stop=True)
            r.then_inc(s_mm.h, 1)
        block.tensor(_t2)
        s_mm.inc(1)
        mm_g = s_mm.n
        def _v3(v, mm_g=mm_g):
            v.wait_ge(s_mm.h, mm_g)
            r = v.tensor_copy(out=ssc[:, :], in_=pv[:1, :])
            r.then_inc(s_ve.h, 1)
        block.vector(_v3)
        s_ve.inc(1)
        ve_g2 = s_ve.n
        def _t3(t, ve_g2=ve_g2):
            t.wait_ge(s_ve.h, ve_g2)
            r = t.matmul(pv2[:, :], ones_row[:1, :], ssc[:1, :], start=True, stop=True)
            r.then_inc(s_mm.h, 1)
        block.tensor(_t3)
        s_mm.inc(1)
        mm_g2 = s_mm.n
        def _v4(v, dst=dst, mm_g2=mm_g2):
            v.wait_ge(s_mm.h, mm_g2)
            v.reciprocal(out=inv1[:, :], in_=pv2[:, :])
            v.drain()
            for c in range(8):
                r = v.tensor_scalar(out=dst[:, c, :], in0=dst[:, c, :],
                                    scalar1=inv1[:, :], scalar2=256.0,
                                    op0=OP.mult, op1=OP.mult)
            r.then_inc(s_ve.h, 1)
        block.vector(_v4)
        s_ve.inc(1)

    Mfin = bufs[NSQ % 2]
    if SIGMA_STAGE < 3:
        return

    for pblk in range(8):
        gate = s_ve.n
        def _t(t, pblk=pblk, gate=gate):
            t.wait_ge(s_ve.h, gate)
            for q in range(8):
                r = t.matmul(pv[:, :], Mfin[:, q, pblk * 128:(pblk + 1) * 128],
                             ones_bf[:, :], start=(q == 0), stop=(q == 7))
            r.then_inc(s_mm.h, 1)
        block.tensor(_t)
        s_mm.inc(1)
        mm_g = s_mm.n
        def _v(v, pblk=pblk, mm_g=mm_g):
            v.wait_ge(s_mm.h, mm_g)
            r = v.tensor_copy(out=vv[:, pblk:pblk + 1], in_=pv[:, :])
            r.then_inc(s_ve.h, 1)
        block.vector(_v)
        s_ve.inc(1)

    if SIGMA_STAGE < 4:
        return
    vbuf, tbuf = vv, tt
    for step in range(NPW + 1):
        for pblk in range(8):
            gate = s_ve.n
            def _t(t, pblk=pblk, vbuf=vbuf, gate=gate):
                t.wait_ge(s_ve.h, gate)
                for q in range(8):
                    r = t.matmul(pv[:, :], Bf[:, q, pblk * 128:(pblk + 1) * 128],
                                 vbuf[:, q:q + 1], start=(q == 0), stop=(q == 7))
                r.then_inc(s_mm.h, 1)
            block.tensor(_t)
            s_mm.inc(1)
            mm_g = s_mm.n
            def _v(v, pblk=pblk, tbuf=tbuf, mm_g=mm_g):
                v.wait_ge(s_mm.h, mm_g)
                r = v.tensor_copy(out=tbuf[:, pblk:pblk + 1], in_=pv[:, :])
                r.then_inc(s_ve.h, 1)
            block.vector(_v)
            s_ve.inc(1)
        vbuf, tbuf = tbuf, vbuf
    vlast, tlast = tbuf, vbuf

    if SIGMA_STAGE < 5:
        return
    def _v5(v):
        v.tensor_tensor(out=prod[:, :], in0=vlast[:, :], in1=tlast[:, :], op=OP.mult)
        r = v.tensor_tensor(out=den_sb[:, :], in0=vlast[:, :], in1=vlast[:, :], op=OP.mult)
        r.then_inc(s_ve.h, 1)
    block.vector(_v5)
    s_ve.inc(1)

    ve_g = s_ve.n
    def _t4(t, ve_g=ve_g):
        t.wait_ge(s_ve.h, ve_g)
        for q in range(8):
            t.matmul(pv[:1, :], ones_col[:, :], prod[:, q:q + 1], start=(q == 0), stop=(q == 7))
        for q in range(8):
            r = t.matmul(pv2[:1, :], ones_col[:, :], den_sb[:, q:q + 1], start=(q == 0), stop=(q == 7))
        r.then_inc(s_mm.h, 1)
    block.tensor(_t4)
    s_mm.inc(1)

    mm_g = s_mm.n
    def _v6(v, mm_g=mm_g):
        v.wait_ge(s_mm.h, mm_g)
        v.tensor_copy(out=n2[:, 0:1], in_=pv[:1, :])
        v.tensor_copy(out=n2[:, 1:2], in_=pv2[:1, :])
        v.drain()
        v.reciprocal(out=n2[:, 1:2], in_=n2[:, 1:2])
        v.drain()
        r = v.tensor_tensor(out=scale_out[:, :], in0=n2[:, 0:1], in1=n2[:, 1:2], op=OP.mult)
        r.then_inc(s_ve.h, 1)
    block.vector(_v6)
    s_ve.inc(1)

    ve_g = s_ve.n
    def _s2(s, ve_g=ve_g):
        s.wait_ge(s_ve.h, ve_g)
        r = s.activation(out=scale_out[:, :], in_=scale_out[:, :], func=AF.Sqrt)
        r.then_inc(s_sc.h, 1)
    block.scalar(_s2)
    s_sc.inc(1)
    sc_g = s_sc.n
    def _v7(v, sc_g=sc_g):
        v.wait_ge(s_sc.h, sc_g)
        r = v.tensor_scalar(out=scale_out[:, :], in0=scale_out[:, :], scalar1=1.0,
                            scalar2=None, op0=OP.max)
        r.then_inc(s_ve.h, 1)
    block.vector(_v7)
    s_ve.inc(1)


U32 = mybir.dt.uint32
TT = 512
NTILE_B = TC // TT
CEXP = 2048
NDUMP = 2048
DEBUG_DUMPS = False
TOKP = TC + NDUMP
BIGS = float(2 ** 24)


def build_program():
    nc = bass.Bass("TRN2", target_bir_lowering=False, detect_race_conditions=False,
                   dynamic_dma_scratch_size=16384)

    xT = nc.dram_tensor("xT", [D, TC], F32, kind="ExternalInput").ap()
    out_tok = nc.dram_tensor("out_tok", [TOKP, D], F32, kind="ExternalOutput").ap()
    _dk = dict(kind="ExternalOutput") if DEBUG_DUMPS else {}
    xn_tok = nc.dram_tensor("xn_tok", [TC, 2 * D], BF16, **_dk).ap()
    comb_tok = nc.dram_tensor("comb_tok", [TC, NE], F32, **_dk).ap()
    idx_tab = nc.dram_tensor("idx_tab", [NE * CEXP, 1], U32, **_dk).ap()
    w_inT = nc.dram_tensor("w_inT", [D, D], F32, kind="ExternalInput").ap()
    w_outT = nc.dram_tensor("w_outT", [D, D], F32, kind="ExternalInput").ap()
    wpT = nc.dram_tensor("wpT", [D, 3 * D], F32, kind="ExternalInput").ap()
    gate_wT = nc.dram_tensor("gate_wT", [D, NE], F32, kind="ExternalInput").ap()
    gate_b = nc.dram_tensor("gate_b", [NE, 1], F32, kind="ExternalInput").ap()
    w13T = nc.dram_tensor("w13T", [NE, D, 2 * FI], F32, kind="ExternalInput").ap()
    w2T = nc.dram_tensor("w2T", [NE, FI, D], F32, kind="ExternalInput").ap()
    w13sT = nc.dram_tensor("w13sT", [D, 2 * FI], F32, kind="ExternalInput").ap()
    w2sT = nc.dram_tensor("w2sT", [FI, D], F32, kind="ExternalInput").ap()
    alpha_log = nc.dram_tensor("alpha_log", [128, 8], F32, kind="ExternalInput").ap()
    delta_bias = nc.dram_tensor("delta_bias", [128, 8], F32, kind="ExternalInput").ap()
    bp = nc.dram_tensor("bp", [128, 24], F32, kind="ExternalInput").ap()
    log_gamma = nc.dram_tensor("log_gamma", [1, 1], F32, kind="ExternalInput").ap()
    eye = nc.dram_tensor("eye", [128, 128], F32, kind="ExternalInput").ap()
    ebase = nc.dram_tensor("ebase", [NE, 1], F32, kind="ExternalInput").ap()

    es0 = ExitStack()
    eye_sb = es0.enter_context(nc.sbuf_tensor("eye_sb", [128, 128], F32))
    eye_bf = es0.enter_context(nc.sbuf_tensor("eye_bf", [128, 128], BF16))
    ones_col = es0.enter_context(nc.sbuf_tensor("ones_col", [128, 1], F32))
    ones_row = es0.enter_context(nc.sbuf_tensor("ones_row", [1, 128], F32))
    ones_bf = es0.enter_context(nc.sbuf_tensor("ones_bf", [128, 1], BF16))
    ones16 = es0.enter_context(nc.sbuf_tensor("ones16", [NE, TT], F32))
    scale1 = es0.enter_context(nc.sbuf_tensor("scale1", [1, 1], F32))
    scale2 = es0.enter_context(nc.sbuf_tensor("scale2", [1, 1], F32))
    attn_scale = es0.enter_context(nc.sbuf_tensor("attn_scale", [128, 1], F32))
    lg_sb = es0.enter_context(nc.sbuf_tensor("lg_sb", [1, 1], F32))
    alpha_n = es0.enter_context(nc.sbuf_tensor("alpha_n", [128, 8], F32))
    db_comb = es0.enter_context(nc.sbuf_tensor("db_comb", [128, 8], F32))
    bp_b = es0.enter_context(nc.sbuf_tensor("bp_b", [128, 8], F32))
    bp_c = es0.enter_context(nc.sbuf_tensor("bp_c", [128, 8], F32))
    gate_b_sb = es0.enter_context(nc.sbuf_tensor("gate_b_sb", [NE, 1], F32))
    gb_bcast = es0.enter_context(nc.sbuf_tensor("gb_bcast", [128, NE], F32))
    ebase_sb = es0.enter_context(nc.sbuf_tensor("ebase_sb", [NE, 1], F32))
    gwt = es0.enter_context(nc.sbuf_tensor("gwt", [128, 8, NE], F32))
    carry = es0.enter_context(nc.sbuf_tensor("carry", [128, 8], F32))
    cnt_vec = es0.enter_context(nc.sbuf_tensor("cnt_vec", [NE, 1], F32))
    tmp8 = es0.enter_context(nc.sbuf_tensor("tmp8", [128, 8], F32))
    cpool = es0.enter_context(nc.sbuf_tensor("cpool", [128, 2], F32))
    Mcol = es0.enter_context(nc.sbuf_tensor("Mcol", [128, 1], F32))
    negM = es0.enter_context(nc.sbuf_tensor("negM", [128, 1], F32))

    sems = {}
    for name in ["ld", "mm", "ve", "sc", "xin", "otok", "isc", "gat", "acc", "wa", "we", "xe",
                 "gs0", "gs1", "gs2", "gs3", "gs4", "gs5", "gs6", "gs7"]:
        sems[name] = Sem(es0, nc, "s_" + name)
    s_ld, s_mm, s_ve, s_sc = sems["ld"], sems["mm"], sems["ve"], sems["sc"]
    s_xin, s_otok, s_isc = sems["xin"], sems["otok"], sems["isc"]
    s_gat, s_acc, s_wa, s_we, s_xe = sems["gat"], sems["acc"], sems["wa"], sems["we"], sems["xe"]
    gslot = [sems[f"gs{j}"] for j in range(8)]

    consts = {"ones_col": ones_col, "ones_row": ones_row, "ones_bf": ones_bf}

    with ExitStack() as esA:
        ztile = esA.enter_context(nc.sbuf_tensor("ztile", [128, 1024], F32))
        idxfill = esA.enter_context(nc.sbuf_tensor("idxfill", [128, 256], U32))
        block = esA.enter_context(nc.Block())

        def _(s):
            s.dma_start(out=eye_sb[:, :], in_=eye).then_inc(s_ld.h, 16)
            s.dma_start(out=lg_sb[:, :], in_=log_gamma).then_inc(s_ld.h, 16)
            s.dma_start(out=alpha_n[:, :], in_=alpha_log).then_inc(s_ld.h, 16)
            s.dma_start(out=db_comb[:, :], in_=delta_bias).then_inc(s_ld.h, 16)
            s.dma_start(out=tmp8[:, :], in_=bp[:, 0:8]).then_inc(s_ld.h, 16)
            s.dma_start(out=bp_b[:, :], in_=bp[:, 8:16]).then_inc(s_ld.h, 16)
            s.dma_start(out=bp_c[:, :], in_=bp[:, 16:24]).then_inc(s_ld.h, 16)
            s.dma_start(out=gate_b_sb[:, :], in_=gate_b).then_inc(s_ld.h, 16)
            s.dma_start(out=gwt[:, :, :], in_=gate_wT.rearrange("(c p) e -> p c e", p=128)).then_inc(s_ld.h, 16)
            s.dma_start(out=ebase_sb[:, :], in_=ebase).then_inc(s_ld.h, 16)
        block.sync(_)
        s_ld.inc(16 * 10)

        ld_g = s_ld.n
        def _(v, ld_g=ld_g):
            v.wait_ge(s_ld.h, ld_g)
            v.memset(ones_col[:, :], 1.0)
            v.memset(ones_row[:, :], 1.0)
            v.memset(ones_bf[:, :], 1.0)
            v.memset(ones16[:, :], 1.0)
            v.memset(carry[:, :], 0.0)
            v.memset(cnt_vec[:, :], 0.0)
            v.memset(ztile[:, :], 0.0)
            v.memset(cpool[:, 0:1], 100000.0)
            v.drain()
            v.reciprocal(out=cpool[:, 0:1], in_=cpool[:, 0:1])
            v.drain()
            v.memset(cpool[:, 1:2], 1e-12)
            v.memset(Mcol[:, :], 1.0)
            v.drain()
            v.tensor_scalar(out=Mcol[:, :], in0=Mcol[:, :], scalar1=float(2.0 ** -29),
                            scalar2=1.0, op0=OP.add, op1=OP.subtract)
            v.drain()
            v.tensor_scalar(out=Mcol[:, :], in0=Mcol[:, :], scalar1=float(2.0 ** 58),
                            scalar2=None, op0=OP.mult)
            v.drain()
            v.tensor_scalar(out=negM[:, :], in0=Mcol[:, :], scalar1=-1.0, scalar2=None, op0=OP.mult)
            v.drain()
            r = v.tensor_copy(out=eye_bf[:, :], in_=eye_sb[:, :])
            r.then_inc(s_ve.h, 1)
        block.vector(_)
        s_ve.inc(1)
        def _(s, ld_g=ld_g):
            s.wait_ge(s_ld.h, ld_g)
            s.activation(out=alpha_n[:, :], in_=alpha_n[:, :], func=AF.Exp)
            s.add(alpha_n[:, :], alpha_n[:, :], 1.0)
            s.activation(out=alpha_n[:, :], in_=alpha_n[:, :], func=AF.Ln)
            r = s.mul(alpha_n[:, :], alpha_n[:, :], -1.0)
            r.then_inc(s_sc.h, 1)
        block.scalar(_)
        s_sc.inc(1)
        def _(v):
            r = v.tensor_tensor(out=db_comb[:, :], in0=db_comb[:, :], in1=tmp8[:, :], op=OP.add)
            r.then_inc(s_ve.h, 1)
        block.vector(_)
        s_ve.inc(1)

        ve_g = s_ve.n
        regbox = {}
        def _(g, ve_g=ve_g):
            regbox["slotbnd"] = g.alloc_register("slotbnd")
            g.reg_mov(regbox["slotbnd"], NE * CEXP - 1)
            regbox["tokbnd"] = g.alloc_register("tokbnd")
            g.reg_mov(regbox["tokbnd"], TC - 1)
            g.wait_ge(s_ve.h, ve_g)
            for r0 in range(TC, TOKP, 128):
                g.dma_start(out=out_tok[r0:r0 + 128, :], in_=ztile[:, :]).then_inc(s_otok.h, 16)
            g.iota(idxfill[:, :], [[1, 16], [0, 16]], base=TC, channel_multiplier=16, allow_small_or_imprecise_dtypes=True)
            g.dma_start(out=idx_tab.rearrange("(p c) x -> p (c x)", p=128),
                        in_=idxfill[:, :]).then_inc(s_isc.h, 16)
        block.gpsimd(_)
        s_otok.inc(16 * (NDUMP // 128))
        s_isc.inc(16)

        with ExitStack() as esS:
            emit_sigma(nc, block, esS, w_inT, scale1, consts, sems, "a")
        with ExitStack() as esS:
            emit_sigma(nc, block, esS, w_outT, scale2, consts, sems, "b")

        with ExitStack() as esS:
            pvb = esS.enter_context(nc.psum_tensor("pvb", [128, 1], F32))
            n2gb = esS.enter_context(nc.sbuf_tensor("n2gb", [1, NE], F32))
            pgb = esS.enter_context(nc.psum_tensor("pgb", [128, NE], F32))
            ve_g = s_ve.n
            def _(s, ve_g=ve_g):
                s.wait_ge(s_ve.h, ve_g)
                r = s.activation(out=lg_sb[:, :], in_=lg_sb[:, :], func=AF.Exp)
                r.then_inc(s_sc.h, 1)
            block.scalar(_)
            s_sc.inc(1)
            sc_g = s_sc.n
            def _(v, sc_g=sc_g):
                v.wait_ge(s_sc.h, sc_g)
                v.tensor_tensor(out=scale1[:, :], in0=scale1[:, :], in1=scale2[:, :], op=OP.mult)
                v.drain()
                v.reciprocal(out=scale1[:, :], in_=scale1[:, :])
                v.drain()
                r = v.tensor_tensor(out=scale1[:, :], in0=scale1[:, :], in1=lg_sb[:, :], op=OP.mult)
                r.then_inc(s_ve.h, 1)
            block.vector(_)
            s_ve.inc(1)
            ve_g = s_ve.n
            def _(t, ve_g=ve_g):
                t.wait_ge(s_ve.h, ve_g)
                t.matmul(pvb[:, :], ones_row[:1, :], scale1[:1, :], start=True, stop=True).then_inc(s_mm.h, 1)
                r = t.matmul(pgb[:1, :NE], gate_b_sb[:, :], eye_sb[:NE, :NE], start=True, stop=True)
                r.then_inc(s_mm.h, 1)
            block.tensor(_)
            s_mm.inc(2)
            mm_g = s_mm.n
            def _(v, mm_g=mm_g):
                v.wait_ge(s_mm.h, mm_g)
                v.tensor_copy(out=attn_scale[:, :], in_=pvb[:, :])
                r = v.tensor_copy(out=n2gb[:1, :], in_=pgb[:1, :NE])
                r.then_inc(s_ve.h, 1)
            block.vector(_)
            s_ve.inc(1)
            ve_g = s_ve.n
            def _(t, ve_g=ve_g):
                t.wait_ge(s_ve.h, ve_g)
                r = t.matmul(pgb[:, :], ones_row[:1, :], n2gb[:1, :], start=True, stop=True)
                r.then_inc(s_mm.h, 1)
            block.tensor(_)
            s_mm.inc(1)
            mm_g = s_mm.n
            def _(v, mm_g=mm_g):
                v.wait_ge(s_mm.h, mm_g)
                r = v.tensor_copy(out=gb_bcast[:, :], in_=pgb[:, :])
                r.then_inc(s_ve.h, 1)
            block.vector(_)
            s_ve.inc(1)

    esB = ExitStack()
    x_sb = esB.enter_context(nc.sbuf_tensor("x_sb", [128, 8, TT], F32))
    xnh = esB.enter_context(nc.sbuf_tensor("xnh", [128, 8, TT], BF16))
    xnl = esB.enter_context(nc.sbuf_tensor("xnl", [128, 8, TT], BF16))
    Ra = esB.enter_context(nc.sbuf_tensor("Ra", [128, 8, TT], F32))
    Rb = esB.enter_context(nc.sbuf_tensor("Rb", [128, 8, TT], F32))
    Rc = esB.enter_context(nc.sbuf_tensor("Rc", [128, 8, TT], F32))
    scr = esB.enter_context(nc.sbuf_tensor("scr", [128, 8 * TT], F32))
    sgad = esB.enter_context(nc.sbuf_tensor("sgad", [128, 2, TT], F32))
    pland = [esB.enter_context(nc.sbuf_tensor(f"pland{i}", [128, 8, 256], F32)) for i in range(2)]
    phl = [(esB.enter_context(nc.sbuf_tensor(f"ph{i}", [128, 8, 256], BF16)),
            esB.enter_context(nc.sbuf_tensor(f"pl{i}", [128, 8, 256], BF16))) for i in range(2)]
    stok = esB.enter_context(nc.sbuf_tensor("stok", [128, 4, NE], F32))
    btok = esB.enter_context(nc.sbuf_tensor("btok", [128, 4, NE], F32))
    rtok = esB.enter_context(nc.sbuf_tensor("rtok", [128, 4, NE], F32))
    thr8 = esB.enter_context(nc.sbuf_tensor("thr8", [128, 4], F32))
    invd8 = esB.enter_context(nc.sbuf_tensor("invd8", [128, 4], F32))
    mske = esB.enter_context(nc.sbuf_tensor("mske", [128, 4, NE], F32))
    slots_f = esB.enter_context(nc.sbuf_tensor("slots_f", [128, 4, NE], F32))
    slots_u = esB.enter_context(nc.sbuf_tensor("slots_u", [128, 4 * NE], U32))
    ids_u = esB.enter_context(nc.sbuf_tensor("ids_u", [128, 4], U32))
    tth = [esB.enter_context(nc.sbuf_tensor(f"tth{i}", [128, 1024], BF16)) for i in range(2)]
    ttl = [esB.enter_context(nc.sbuf_tensor(f"ttl{i}", [128, 1024], BF16)) for i in range(2)]
    otok = [esB.enter_context(nc.sbuf_tensor(f"otok{i}", [128, 1024], F32)) for i in range(2)]
    P = [esB.enter_context(nc.psum_tensor(f"P{i}", [128, 512], F32)) for i in range(8)]
    blockB = esB.enter_context(nc.Block())
    block = blockB

    stats = [Rc[:1, 0, 0:512], Rc[:1, 1, 0:512], Rc[:1, 2, 0:512], Rc[:1, 3, 0:512], Rc[:1, 4, 0:512]]
    scores_em = Rc[0:NE, 5, :]
    comb_em = Rc[0:NE, 6, :]
    m01 = Rc[0:NE, 7, :]
    incl_t = Rc[0:NE, 0, :]
    srel_t = Rc[0:NE, 1, :]
    tmp_t = Rc[0:NE, 2, :]

    def mm3(t, ps, wh, wl, xh, xl, first, last, inc=None):
        t.matmul(ps, wh, xh, start=first, stop=False)
        t.matmul(ps, wh, xl, start=False, stop=False)
        r = t.matmul(ps, wl, xh, start=False, stop=last)
        if inc:
            r.then_inc(s_mm.h, 1)
        return r

    for j in range(NTILE_B):
        t0 = j * TT

        mm_pre = s_mm.n
        def _(g, t0=t0, mm_pre=mm_pre):
            g.wait_ge(s_mm.h, mm_pre)
            g.dma_start(out=x_sb[:, :, :],
                        in_=xT[:, t0:t0 + TT].rearrange("(c p) t -> p c t", p=128)).then_inc(s_xin.h, 16)
        block.gpsimd(_)
        s_xin.inc(16)

        panels = ([wpT[:, p * 256:(p + 1) * 256] for p in range(12)]
                  + [w_inT[:, p * 256:(p + 1) * 256] for p in range(4)]
                  + [w_outT[:, p * 256:(p + 1) * 256] for p in range(4)])
        wa_base = s_wa.n
        panel_conv_ve = [None] * 20

        def emit_panel(pi, gate_mm):
            slot = pi % 2
            def _(s, pi=pi, gate_mm=gate_mm, slot=slot):
                if gate_mm is not None:
                    s.wait_ge(s_mm.h, gate_mm)
                s.wait_ge(s_ve.h, s_ve.n)
                s.dma_start(out=pland[slot][:, :, :],
                            in_=panels[pi].rearrange("(c p) n -> p c n", p=128)).then_inc(s_wa.h, 16)
            block.sync(_)
            s_wa.inc(16)
            wa_g = s_wa.n
            def _(v, pi=pi, slot=slot, wa_g=wa_g):
                v.wait_ge(s_wa.h, wa_g)
                v.tensor_copy(out=phl[slot][0][:, :, :], in_=pland[slot][:, :, :])
                r = v.tensor_tensor(out=phl[slot][1][:, :, :], in0=pland[slot][:, :, :],
                                    in1=phl[slot][0][:, :, :], op=OP.subtract)
                r.then_inc(s_ve.h, 1)
            block.vector(_)
            s_ve.inc(1)
            panel_conv_ve[pi] = s_ve.n

        emit_panel(0, s_mm.n)
        emit_panel(1, s_mm.n)
        pi_loaded = 2

        xin_gate = s_xin.n
        ve_pre = s_ve.n
        def _(s, xin_gate=xin_gate, ve_pre=ve_pre):
            s.wait_ge(s_xin.h, xin_gate)
            s.wait_ge(s_ve.h, ve_pre)
            for c in range(8):
                r = s.activation(out=scr[:, c * TT:(c + 1) * TT], in_=x_sb[:, c, :], func=AF.Square)
            r.then_inc(s_sc.h, 1)
        block.scalar(_)
        s_sc.inc(1)
        sc_sq = s_sc.n
        ve_pre = s_ve.n
        def _(t, sc_sq=sc_sq, ve_pre=ve_pre, xin_gate=xin_gate):
            t.wait_ge(s_xin.h, xin_gate)
            t.wait_ge(s_sc.h, sc_sq)
            t.wait_ge(s_ve.h, ve_pre)
            for c in range(8):
                t.matmul(P[0][:1, :], ones_col[:, :], x_sb[:, c, :], start=(c == 0), stop=(c == 7))
            for c in range(8):
                r = t.matmul(P[1][:1, :], ones_col[:, :], scr[:, c * TT:(c + 1) * TT],
                             start=(c == 0), stop=(c == 7))
            r.then_inc(s_mm.h, 1)
        block.tensor(_)
        s_mm.inc(1)
        mm_g = s_mm.n
        def _(v, mm_g=mm_g):
            v.wait_ge(s_mm.h, mm_g)
            v.tensor_scalar(out=stats[0], in0=P[0][:1, :], scalar1=1.0 / D, scalar2=None, op0=OP.mult)
            v.tensor_scalar(out=stats[1], in0=P[1][:1, :], scalar1=1.0 / D, scalar2=None, op0=OP.mult)
            v.drain()
            v.tensor_tensor(out=stats[2], in0=stats[0], in1=stats[0], op=OP.mult)
            v.drain()
            r = v.tensor_tensor(out=stats[1], in0=stats[1], in1=stats[2], op=OP.subtract)
            r.then_inc(s_ve.h, 1)
        block.vector(_)
        s_ve.inc(1)
        ve_g = s_ve.n
        def _(s, ve_g=ve_g):
            s.wait_ge(s_ve.h, ve_g)
            r = s.activation(out=stats[2], in_=stats[1], func=AF.Sqrt, bias=cpool[:1, 0:1])
            r.then_inc(s_sc.h, 1)
        block.scalar(_)
        s_sc.inc(1)
        sc_g = s_sc.n
        def _(v, sc_g=sc_g):
            v.wait_ge(s_sc.h, sc_g)
            v.reciprocal(out=stats[3], in_=stats[2])
            v.drain()
            r = v.tensor_tensor(out=stats[4], in0=stats[0], in1=stats[3], op=OP.mult)
            r.then_inc(s_ve.h, 1)
        block.vector(_)
        s_ve.inc(1)
        ve_g = s_ve.n
        def _(t, ve_g=ve_g):
            t.wait_ge(s_ve.h, ve_g)
            t.matmul(P[2][:, :], ones_row[:1, :], stats[3], start=True, stop=True)
            r = t.matmul(P[3][:, :], ones_row[:1, :], stats[4], start=True, stop=True)
            r.then_inc(s_mm.h, 1)
        block.tensor(_)
        s_mm.inc(1)
        mm_g = s_mm.n
        def _(v, mm_g=mm_g):
            v.wait_ge(s_mm.h, mm_g)
            for c in range(8):
                sl = slice(c * TT, (c + 1) * TT)
                v.tensor_tensor(out=scr[:, sl], in0=x_sb[:, c, :], in1=P[2][:, :], op=OP.mult)
                v.tensor_tensor(out=scr[:, sl], in0=scr[:, sl], in1=P[3][:, :], op=OP.subtract)
            for c in range(8):
                sl = slice(c * TT, (c + 1) * TT)
                v.tensor_copy(out=xnh[:, c, :], in_=scr[:, sl])
                r = v.tensor_tensor(out=xnl[:, c, :], in0=scr[:, sl], in1=xnh[:, c, :], op=OP.subtract)
            r.then_inc(s_ve.h, 1)
        block.vector(_)
        s_ve.inc(1)
        xn_ready = s_ve.n

        gate_mm_base = s_mm.n
        for c in range(8):
            mm_pre = s_mm.n
            def _(v, c=c, mm_pre=mm_pre, gate_mm_base=gate_mm_base):
                if c < 2:
                    v.wait_ge(s_mm.h, mm_pre)
                else:
                    v.wait_ge(s_mm.h, gate_mm_base + c - 1)
                sl = slice(c * TT, (c + 1) * TT)
                v.scalar_tensor_tensor(out=sgad[:, c % 2, :], in0=scr[:, sl], scalar=Mcol[:, :],
                                       in1=scr[:, sl], op0=OP.mult, op1=OP.add)
                r = v.scalar_tensor_tensor(out=sgad[:, c % 2, :], in0=scr[:, sl], scalar=negM[:, :],
                                           in1=sgad[:, c % 2, :], op0=OP.mult, op1=OP.add)
                r.then_inc(s_ve.h, 1)
            block.vector(_)
            s_ve.inc(1)
            ve_g = s_ve.n
            sc_pre = s_sc.n
            def _(t, c=c, ve_g=ve_g, sc_pre=sc_pre):
                t.wait_ge(s_ve.h, ve_g)
                if c == 0:
                    t.wait_ge(s_sc.h, sc_pre)
                r = t.matmul(P[4][:NE, :], gwt[:, c, :], sgad[:, c % 2, :],
                             start=(c == 0), stop=(c == 7))
                r.then_inc(s_mm.h, 1)
            block.tensor(_)
            s_mm.inc(1)
        mm_g = s_mm.n
        def _(s, mm_g=mm_g):
            s.wait_ge(s_mm.h, mm_g)
            r = s.activation(out=scores_em, in_=P[4][:NE, :], func=AF.Sigmoid)
            r.then_inc(s_sc.h, 1)
        block.scalar(_)
        s_sc.inc(1)

        sc_scores = s_sc.n
        for sub in range(4):
            sc_g = s_sc.n
            def _(t, sub=sub, sc_g=sc_g):
                t.wait_ge(s_sc.h, sc_scores)
                if sub > 0:
                    t.wait_ge(s_sc.h, sc_g)
                r = t.matmul(P[5][:, :NE], scores_em[:, sub * 128:(sub + 1) * 128], eye_sb[:NE, :NE], start=True, stop=True)
                r.then_inc(s_mm.h, 1)
            block.tensor(_)
            s_mm.inc(1)
            mm_g = s_mm.n
            def _(s, sub=sub, mm_g=mm_g):
                s.wait_ge(s_mm.h, mm_g)
                r = s.copy(out=stok[:, sub, :], in_=P[5][:, :NE])
                r.then_inc(s_sc.h, 1)
            block.scalar(_)
            s_sc.inc(1)
        sc_routing = s_sc.n
        xe_pre_comb = s_xe.n
        def _routing(v, sc_routing=sc_routing, xe_pre_comb=xe_pre_comb):
            v.wait_ge(s_sc.h, sc_routing)
            v.wait_ge(s_xe.h, xe_pre_comb)
            for sub in range(4):
                v.tensor_tensor(out=btok[:, sub, :], in0=stok[:, sub, :], in1=gb_bcast[:, :], op=OP.add)
            for sub in range(4):
                v.tensor_copy(out=rtok[:, sub, :], in_=btok[:, sub, :])
            for rnd in range(4):
                for sub in range(4):
                    v.tensor_reduce(out=thr8[:, sub:sub + 1], in_=rtok[:, sub, :], axis=AX.X, op=OP.max)
                if rnd == 1:
                    v.drain()
                if rnd < 3:
                    for sub in range(4):
                        v.tensor_scalar(out=mske[:, sub, :], in0=rtok[:, sub, :],
                                        scalar1=thr8[:, sub:sub + 1], scalar2=None, op0=OP.is_ge)
                    for sub in range(4):
                        v.scalar_tensor_tensor(out=rtok[:, sub, :], in0=mske[:, sub, :], scalar=-1e30,
                                               in1=rtok[:, sub, :], op0=OP.mult, op1=OP.add)
                else:
                    v.drain()
            for sub in range(4):
                v.tensor_scalar(out=mske[:, sub, :], in0=btok[:, sub, :],
                                scalar1=thr8[:, sub:sub + 1], scalar2=None, op0=OP.is_ge)
            for sub in range(4):
                v.tensor_tensor(out=rtok[:, sub, :], in0=stok[:, sub, :], in1=mske[:, sub, :], op=OP.mult)
            for sub in range(4):
                v.tensor_reduce(out=invd8[:, sub:sub + 1], in_=rtok[:, sub, :], axis=AX.X, op=OP.add)
            v.drain()
            v.reciprocal(out=invd8[:, :], in_=invd8[:, :])
            v.drain()
            for sub in range(4):
                v.tensor_scalar(out=btok[:, sub, :], in0=rtok[:, sub, :], scalar1=invd8[:, sub:sub + 1],
                                scalar2=2.5, op0=OP.mult, op1=OP.mult)
            v.drain()
            v.tensor_copy(out=rtok[:, :, :], in_=btok[:, :, :])
            v.drain()
            v.scalar_tensor_tensor(out=btok[:, :, :], in0=rtok[:, :, :], scalar=Mcol[:, :],
                                   in1=rtok[:, :, :], op0=OP.mult, op1=OP.add)
            v.drain()
            r = v.scalar_tensor_tensor(out=btok[:, :, :], in0=rtok[:, :, :], scalar=negM[:, :],
                                       in1=btok[:, :, :], op0=OP.mult, op1=OP.add)
            r.then_inc(s_ve.h, 1)
        block.vector(_routing)
        s_ve.inc(1)
        comb_ready_ve = s_ve.n

        def _(s, t0=t0, comb_ready_ve=comb_ready_ve):
            s.wait_ge(s_ve.h, comb_ready_ve)
            s.dma_start(out=comb_tok[t0:t0 + TT, :].rearrange("(s p) e -> p s e", p=128),
                        in_=btok[:, :, :]).then_inc(s_xe.h, 16)
        block.sync(_)
        s_xe.inc(16)

        for sub in range(4):
            sc_g = s_sc.n
            def _(t, sub=sub, sc_g=sc_g):
                t.wait_ge(s_ve.h, comb_ready_ve)
                if sub > 0:
                    t.wait_ge(s_sc.h, sc_g)
                r = t.matmul(P[6][:NE, :128], btok[:, sub, :], eye_sb[:, :], start=True, stop=True)
                r.then_inc(s_mm.h, 1)
            block.tensor(_)
            s_mm.inc(1)
            mm_g = s_mm.n
            def _(s, sub=sub, mm_g=mm_g):
                s.wait_ge(s_mm.h, mm_g)
                r = s.copy(out=comb_em[:, sub * 128:(sub + 1) * 128], in_=P[6][:NE, :128])
                r.then_inc(s_sc.h, 1)
            block.scalar(_)
            s_sc.inc(1)
        sc_g = s_sc.n
        def _(v, sc_g=sc_g):
            v.wait_ge(s_sc.h, sc_g)
            v.tensor_scalar(out=m01, in0=comb_em, scalar1=0.0, scalar2=None, op0=OP.is_gt)
            v.tensor_tensor_scan(out=incl_t, data0=ones16[:, :], data1=m01,
                                 initial=cnt_vec[:, :], op0=OP.mult, op1=OP.add)
            v.tensor_scalar(out=srel_t, in0=incl_t, scalar1=-1.0, scalar2=None, op0=OP.add)
            v.drain()
            v.tensor_copy(out=cnt_vec[:, :], in_=incl_t[:, TT - 1:TT])
            v.tensor_scalar(out=tmp_t, in0=srel_t, scalar1=float(CEXP), scalar2=None, op0=OP.is_lt)
            v.tensor_tensor(out=m01, in0=m01, in1=tmp_t, op=OP.mult)
            v.tensor_scalar(out=srel_t, in0=srel_t, scalar1=ebase_sb[:NE, :], scalar2=-BIGS,
                            op0=OP.add, op1=OP.add)
            v.tensor_tensor(out=srel_t, in0=srel_t, in1=m01, op=OP.mult)
            r = v.tensor_scalar(out=srel_t, in0=srel_t, scalar1=BIGS, scalar2=None, op0=OP.add)
            r.then_inc(s_ve.h, 1)
        block.vector(_)
        s_ve.inc(1)
        for sub in range(4):
            ve_g = s_ve.n
            sc_g = s_sc.n
            def _(t, sub=sub, ve_g=ve_g, sc_g=sc_g):
                t.wait_ge(s_ve.h, ve_g)
                if sub > 0:
                    t.wait_ge(s_sc.h, sc_g)
                r = t.matmul(P[7][:, :NE], srel_t[:, sub * 128:(sub + 1) * 128], eye_sb[:NE, :NE], start=True, stop=True)
                r.then_inc(s_mm.h, 1)
            block.tensor(_)
            s_mm.inc(1)
            mm_g = s_mm.n
            def _(s, sub=sub, mm_g=mm_g):
                s.wait_ge(s_mm.h, mm_g)
                r = s.copy(out=slots_f[:, sub, :], in_=P[7][:, :NE])
                r.then_inc(s_sc.h, 1)
            block.scalar(_)
            s_sc.inc(1)
        sc_g = s_sc.n
        isc_pre = s_isc.n
        def _(v, sc_g=sc_g, isc_pre=isc_pre):
            v.wait_ge(s_sc.h, sc_g)
            v.wait_ge(s_isc.h, isc_pre)
            r = v.tensor_copy(out=slots_u[:, :], in_=slots_f[:, :, :])
            r.then_inc(s_ve.h, 1)
        block.vector(_)
        s_ve.inc(1)
        ve_slots = s_ve.n
        isc_base = s_isc.n
        def _(g, t0=t0, ve_slots=ve_slots, isc_base=isc_base):
            g.wait_ge(s_ve.h, ve_slots)
            g.wait_ge(s_isc.h, isc_base)
            g.iota(ids_u[:, :], [[128, 4]], base=t0, channel_multiplier=1,
                   allow_small_or_imprecise_dtypes=True)
            n = 0
            for sub in range(4):
                for e in range(NE):
                    if n and n % 8 == 0:
                        g.wait_ge(s_isc.h, isc_base + 16 * (n - 8))
                    g.indirect_dma_start(
                        out=idx_tab,
                        out_offset=bass.IndirectOffsetOnAxis(ap=slots_u[:, sub * NE + e:sub * NE + e + 1], axis=0),
                        in_=ids_u[:, sub:sub + 1], in_offset=None,
                        bounds_check=regbox["slotbnd"], oob_is_err=False,
                    ).then_inc(s_isc.h, 16)
                    n += 1
        block.gpsimd(_)
        s_isc.inc(16 * 64)

        for sub in range(4):
            par = sub % 2
            xe_pre = s_xe.n
            for half in range(2):
                cs = range(half * 4, half * 4 + 4)
                sc_g = s_sc.n
                def _(t, sub=sub, half=half, cs=cs, sc_g=sc_g):
                    t.wait_ge(s_ve.h, xn_ready)
                    t.wait_ge(s_sc.h, sc_g)
                    for i, c in enumerate(cs):
                        t.matmul(P[5][:, i * 128:(i + 1) * 128], xnh[:, c, sub * 128:(sub + 1) * 128],
                                 eye_bf[:, :], start=(i == 0), stop=(i == 3))
                        r = t.matmul(P[6][:, i * 128:(i + 1) * 128], xnl[:, c, sub * 128:(sub + 1) * 128],
                                     eye_bf[:, :], start=(i == 0), stop=(i == 3))
                    r.then_inc(s_mm.h, 1)
                block.tensor(_)
                s_mm.inc(1)
                mm_g = s_mm.n
                def _(s, sub=sub, half=half, par=par, mm_g=mm_g, xe_pre=xe_pre):
                    s.wait_ge(s_mm.h, mm_g)
                    if half == 0:
                        s.wait_ge(s_xe.h, xe_pre)
                    s.copy(out=tth[par][:, half * 512:(half + 1) * 512], in_=P[5][:, :])
                    r = s.copy(out=ttl[par][:, half * 512:(half + 1) * 512], in_=P[6][:, :])
                    r.then_inc(s_sc.h, 1)
                block.scalar(_)
                s_sc.inc(1)
            sc_g = s_sc.n
            def _(s, sub=sub, par=par, t0=t0, sc_g=sc_g):
                s.wait_ge(s_sc.h, sc_g)
                r0 = t0 + sub * 128
                s.dma_start(out=xn_tok[r0:r0 + 128, 0:D], in_=tth[par][:, :]).then_inc(s_xe.h, 16)
                s.dma_start(out=xn_tok[r0:r0 + 128, D:2 * D], in_=ttl[par][:, :]).then_inc(s_xe.h, 16)
            block.sync(_)
            s_xe.inc(32)

        for pi in range(12):
            wa_gate = panel_conv_ve[pi]
            slot = pi % 2
            for blk in range(2):
                kblk = pi * 2 + blk
                sect = kblk // 8
                loc = kblk % 8
                pband = P[2 * (kblk % 2)]
                ve_g = s_ve.n
                sc_g = s_sc.n
                def _(t, pi=pi, blk=blk, slot=slot, wa_gate=wa_gate, pband=pband, ve_g=ve_g, sc_g=sc_g):
                    t.wait_ge(s_ve.h, max(wa_gate, ve_g))
                    t.wait_ge(s_sc.h, sc_g)
                    for c in range(8):
                        mm3(t, pband[:, :],
                            phl[slot][0][:, c, blk * 128:(blk + 1) * 128],
                            phl[slot][1][:, c, blk * 128:(blk + 1) * 128],
                            xnh[:, c, :], xnl[:, c, :],
                            first=(c == 0), last=(c == 7), inc=(c == 7))
                block.tensor(_)
                s_mm.inc(1)
                mm_g = s_mm.n
                if sect == 0:
                    def _(s, loc=loc, pband=pband, mm_g=mm_g):
                        s.wait_ge(s_mm.h, mm_g)
                        s.activation(out=scr[:, 0:TT], in_=pband[:, :], func=AF.Exp,
                                     bias=db_comb[:, loc:loc + 1])
                        s.add(scr[:, 0:TT], scr[:, 0:TT], 1.0)
                        s.activation(out=scr[:, 0:TT], in_=scr[:, 0:TT], func=AF.Ln)
                        r = s.activation(out=Ra[:, loc, :], in_=scr[:, 0:TT],
                                         func=AF.Exp, scale=alpha_n[:, loc:loc + 1])
                        r.then_inc(s_sc.h, 1)
                    block.scalar(_)
                    s_sc.inc(1)
                    sc_g = s_sc.n
                    def _(v, loc=loc, sc_g=sc_g):
                        v.wait_ge(s_sc.h, sc_g)
                        r = v.tensor_scalar(out=Ra[:, loc, :], in0=Ra[:, loc, :],
                                            scalar1=1.0 - EPS_A, scalar2=None, op0=OP.min)
                        r.then_inc(s_ve.h, 1)
                    block.vector(_)
                    s_ve.inc(1)
                else:
                    dst = Rb if sect == 1 else Rc
                    bias_t = bp_b if sect == 1 else bp_c
                    def _(s, loc=loc, pband=pband, dst=dst, bias_t=bias_t, mm_g=mm_g):
                        s.wait_ge(s_mm.h, mm_g)
                        r = s.activation(out=dst[:, loc, :], in_=pband[:, :],
                                         func=AF.Tanh, bias=bias_t[:, loc:loc + 1])
                        r.then_inc(s_sc.h, 1)
                    block.scalar(_)
                    s_sc.inc(1)
            if pi_loaded < 20:
                emit_panel(pi_loaded, s_mm.n)
                pi_loaded += 1

        for c in range(8):
            sc_g = s_sc.n
            def _(v, c=c, sc_g=sc_g):
                v.wait_ge(s_sc.h, sc_g)
                a_, b_, c_ = Ra[:, c, :], Rb[:, c, :], Rc[:, c, :]
                p_, r_, q_, d_ = scr[:, 512:1024], scr[:, 1024:1536], scr[:, 1536:2048], scr[:, 2048:2560]
                v.tensor_tensor(out=p_, in0=a_, in1=a_, op=OP.mult)
                v.tensor_tensor(out=q_, in0=c_, in1=c_, op=OP.mult)
                v.tensor_tensor(out=p_, in0=p_, in1=q_, op=OP.add)
                v.tensor_tensor(out=r_, in0=b_, in1=b_, op=OP.mult)
                v.tensor_tensor(out=q_, in0=a_, in1=b_, op=OP.mult)
                v.tensor_tensor(out=d_, in0=p_, in1=r_, op=OP.subtract)
                v.tensor_tensor(out=d_, in0=d_, in1=d_, op=OP.mult)
                v.tensor_tensor(out=q_, in0=q_, in1=q_, op=OP.mult)
                v.tensor_scalar(out=q_, in0=q_, scalar1=4.0, scalar2=None, op0=OP.mult)
                v.tensor_tensor(out=d_, in0=d_, in1=q_, op=OP.add)
                r2 = v.tensor_tensor(out=p_, in0=p_, in1=r_, op=OP.add)
                r2.then_inc(s_ve.h, 1)
            block.vector(_)
            s_ve.inc(1)
            ve_g = s_ve.n
            def _(s, ve_g=ve_g):
                s.wait_ge(s_ve.h, ve_g)
                r2 = s.activation(out=scr[:, 2560:3072], in_=scr[:, 2048:2560], func=AF.Sqrt,
                                  bias=cpool[:, 1:2])
                r2.then_inc(s_sc.h, 1)
            block.scalar(_)
            s_sc.inc(1)
            sc_g = s_sc.n
            def _(v, sc_g=sc_g):
                v.wait_ge(s_sc.h, sc_g)
                r2 = v.tensor_tensor(out=scr[:, 512:1024], in0=scr[:, 512:1024],
                                     in1=scr[:, 2560:3072], op=OP.add)
                r2.then_inc(s_ve.h, 1)
            block.vector(_)
            s_ve.inc(1)
            ve_g = s_ve.n
            def _(s, ve_g=ve_g):
                s.wait_ge(s_ve.h, ve_g)
                r2 = s.activation(out=scr[:, 512:1024], in_=scr[:, 512:1024], func=AF.Sqrt,
                                  scale=0.5, bias=cpool[:, 1:2])
                r2.then_inc(s_sc.h, 1)
            block.scalar(_)
            s_sc.inc(1)
            sc_g = s_sc.n
            def _(v, c=c, sc_g=sc_g):
                v.wait_ge(s_sc.h, sc_g)
                a_, b_, c_ = Ra[:, c, :], Rb[:, c, :], Rc[:, c, :]
                v.tensor_scalar(out=scr[:, 512:1024], in0=scr[:, 512:1024], scalar1=1.0,
                                scalar2=None, op0=OP.max)
                v.drain()
                v.reciprocal(out=scr[:, 1024:1536], in_=scr[:, 512:1024])
                v.drain()
                v.tensor_tensor(out=a_, in0=a_, in1=scr[:, 1024:1536], op=OP.mult)
                v.tensor_tensor(out=b_, in0=b_, in1=scr[:, 1024:1536], op=OP.mult)
                r2 = v.tensor_tensor(out=c_, in0=c_, in1=scr[:, 1024:1536], op=OP.mult)
                r2.then_inc(s_ve.h, 1)
            block.vector(_)
            s_ve.inc(1)

        for pp in range(4):
            pi = 12 + pp
            slot = pi % 2
            wa_gate = panel_conv_ve[pi]
            for blk in range(2):
                nblk = pp * 2 + blk
                pband = P[4 + (nblk % 2)]
                ve_g = s_ve.n
                def _(t, pi=pi, blk=blk, slot=slot, wa_gate=wa_gate, pband=pband, ve_g=ve_g):
                    t.wait_ge(s_ve.h, max(wa_gate, ve_g))
                    for c in range(8):
                        mm3(t, pband[:, :],
                            phl[slot][0][:, c, blk * 128:(blk + 1) * 128],
                            phl[slot][1][:, c, blk * 128:(blk + 1) * 128],
                            xnh[:, c, :], xnl[:, c, :],
                            first=(c == 0), last=(c == 7), inc=(c == 7))
                block.tensor(_)
                s_mm.inc(1)
                mm_g = s_mm.n
                def _(v, nblk=nblk, pband=pband, mm_g=mm_g):
                    v.wait_ge(s_mm.h, mm_g)
                    r = v.tensor_tensor(out=Rb[:, nblk, :], in0=Rb[:, nblk, :],
                                        in1=pband[:, :], op=OP.mult)
                    r.then_inc(s_ve.h, 1)
                block.vector(_)
                s_ve.inc(1)
            if pi_loaded < 20:
                emit_panel(pi_loaded, s_mm.n)
                pi_loaded += 1

        for c in range(8):
            def _(v, c=c):
                v.tensor_tensor_scan(out=scr[:, 0:TT], data0=Ra[:, c, :], data1=Rb[:, c, :],
                                     initial=carry[:, c:c + 1], op0=OP.mult, op1=OP.add)
                v.tensor_tensor(out=Ra[:, c, 0:1], in0=Rc[:, c, 0:1], in1=carry[:, c:c + 1], op=OP.mult)
                v.tensor_tensor(out=Ra[:, c, 1:], in0=Rc[:, c, 1:], in1=scr[:, 0:TT - 1], op=OP.mult)
                v.tensor_copy(out=carry[:, c:c + 1], in_=scr[:, TT - 1:TT])
                v.tensor_copy(out=xnh[:, c, :], in_=Ra[:, c, :])
                r = v.tensor_tensor(out=xnl[:, c, :], in0=Ra[:, c, :], in1=xnh[:, c, :], op=OP.subtract)
                r.then_inc(s_ve.h, 1)
            block.vector(_)
            s_ve.inc(1)

        for pp in range(4):
            pi = 16 + pp
            slot = pi % 2
            wa_gate = panel_conv_ve[pi]
            for blk in range(2):
                oblk = pp * 2 + blk
                pband = P[4 + (oblk % 2)]
                ve_g = s_ve.n
                def _(t, pi=pi, blk=blk, slot=slot, wa_gate=wa_gate, pband=pband, ve_g=ve_g):
                    t.wait_ge(s_ve.h, max(wa_gate, ve_g))
                    for c in range(8):
                        mm3(t, pband[:, :],
                            phl[slot][0][:, c, blk * 128:(blk + 1) * 128],
                            phl[slot][1][:, c, blk * 128:(blk + 1) * 128],
                            xnh[:, c, :], xnl[:, c, :],
                            first=(c == 0), last=(c == 7), inc=(c == 7))
                block.tensor(_)
                s_mm.inc(1)
                mm_g = s_mm.n
                def _(v, oblk=oblk, pband=pband, mm_g=mm_g):
                    v.wait_ge(s_mm.h, mm_g)
                    r = v.scalar_tensor_tensor(out=x_sb[:, oblk, :], in0=pband[:, :],
                                               scalar=attn_scale[:, :], in1=x_sb[:, oblk, :],
                                               op0=OP.mult, op1=OP.add)
                    r.then_inc(s_ve.h, 1)
                block.vector(_)
                s_ve.inc(1)
            if pi_loaded < 20:
                emit_panel(pi_loaded, s_mm.n)
                pi_loaded += 1

        for sub in range(4):
            par = sub % 2
            ot_pre = s_otok.n
            for half in range(2):
                cs = range(half * 4, half * 4 + 4)
                ve_g = s_ve.n
                sc_g = s_sc.n
                def _(t, sub=sub, half=half, cs=cs, ve_g=ve_g, sc_g=sc_g):
                    t.wait_ge(s_ve.h, ve_g)
                    t.wait_ge(s_sc.h, sc_g)
                    for i, c in enumerate(cs):
                        r = t.matmul(P[6][:, i * 128:(i + 1) * 128], x_sb[:, c, sub * 128:(sub + 1) * 128],
                                     eye_sb[:, :], start=(i == 0), stop=(i == 3))
                    r.then_inc(s_mm.h, 1)
                block.tensor(_)
                s_mm.inc(1)
                mm_g = s_mm.n
                def _(s, sub=sub, half=half, par=par, mm_g=mm_g, ot_pre=ot_pre):
                    s.wait_ge(s_mm.h, mm_g)
                    if half == 0:
                        s.wait_ge(s_otok.h, ot_pre)
                    r = s.copy(out=otok[par][:, half * 512:(half + 1) * 512], in_=P[6][:, :])
                    r.then_inc(s_sc.h, 1)
                block.scalar(_)
                s_sc.inc(1)
            sc_g = s_sc.n
            def _(g, sub=sub, par=par, t0=t0, sc_g=sc_g):
                g.wait_ge(s_sc.h, sc_g)
                r0 = t0 + sub * 128
                g.dma_start(out=out_tok[r0:r0 + 128, :], in_=otok[par][:, :]).then_inc(s_otok.h, 16)
            block.gpsimd(_)
            s_otok.inc(16)

    esB.close()

    esC = ExitStack()
    land13 = esC.enter_context(nc.sbuf_tensor("land13", [128, 8, 1024], F32))
    land2 = esC.enter_context(nc.sbuf_tensor("land2", [128, 4, 1024], F32))
    w13h = esC.enter_context(nc.sbuf_tensor("w13h", [128, 8, 1024], BF16))
    w13l = esC.enter_context(nc.sbuf_tensor("w13l", [128, 8, 1024], BF16))
    w2h = esC.enter_context(nc.sbuf_tensor("w2h", [128, 4, 1024], BF16))
    w2l = esC.enter_context(nc.sbuf_tensor("w2l", [128, 4, 1024], BF16))
    xg = esC.enter_context(nc.sbuf_tensor("xg", [128, 8, 2 * D], BF16))
    cg = esC.enter_context(nc.sbuf_tensor("cg", [128, 8, NE], F32))
    cekeep = esC.enter_context(nc.sbuf_tensor("cekeep", [128, 32], F32))
    xfh = esC.enter_context(nc.sbuf_tensor("xfh", [128, 8, 512], BF16))
    xfl = esC.enter_context(nc.sbuf_tensor("xfl", [128, 8, 512], BF16))
    s_sb = esC.enter_context(nc.sbuf_tensor("s_sb", [128, 4, 512], F32))
    hh = esC.enter_context(nc.sbuf_tensor("hh", [128, 4, 512], BF16))
    hl = esC.enter_context(nc.sbuf_tensor("hl", [128, 4, 512], BF16))
    y_fm = esC.enter_context(nc.sbuf_tensor("y_fm", [128, 8, 512], F32))
    y_tok = [esC.enter_context(nc.sbuf_tensor(f"y_tok{i}", [128, 1024], F32)) for i in range(4)]
    idx_sb = esC.enter_context(nc.sbuf_tensor("idx_sb", [128, 64], U32))
    PC = [esC.enter_context(nc.psum_tensor(f"PC{i}", [128, 512], F32)) for i in range(8)]
    blockC = esC.enter_context(nc.Block())
    block = blockC

    xe_total = s_xe.n
    isc_total = s_isc.n
    otok_total = s_otok.n

    def _(g):
        g.memset(xg[:, :, :], 0.0)
        g.memset(cg[:, :, :], 0.0)
        g.wait_ge(s_xe.h, xe_total)
        g.wait_ge(s_isc.h, isc_total)
        g.wait_ge(s_otok.h, otok_total)
    block.gpsimd(_)

    prev_yout_ve = s_ve.n
    prev_w2_mm = s_mm.n
    prev_silu_sc = s_sc.n
    prev_yfm_ve = s_ve.n
    last_xfl_ve = s_ve.n
    pending = {}
    next_gat = {}

    def emit_loads(te):
        sh = (te == NE)
        ep = te % 2
        gat_pre = [gslot[j].n for j in range(8)]
        acc_pre_e = s_acc.n
        ve_conv_pre = s_ve.n
        def _(s, te=te, sh=sh, ep=ep, gat_pre=gat_pre, ve_conv_pre=ve_conv_pre, acc_pre_e=acc_pre_e):
            s.wait_ge(s_ve.h, ve_conv_pre)
            if sh:
                s.dma_start(out=land13[:, :, :],
                            in_=w13sT.rearrange("(c p) f -> p c f", p=128)).then_inc(s_we.h, 16)
                s.dma_start(out=land2[:, :, :],
                            in_=w2sT.rearrange("(c p) d2 -> p c d2", p=128)).then_inc(s_we.h, 16)
            else:
                s.dma_start(out=land13[:, :, :],
                            in_=w13T[te].rearrange("(c p) f -> p c f", p=128)).then_inc(s_we.h, 16)
                s.dma_start(out=land2[:, :, :],
                            in_=w2T[te].rearrange("(c p) d2 -> p c d2", p=128)).then_inc(s_we.h, 16)
                for j in range(8):
                    s.wait_ge(gslot[j].h, gat_pre[j])
                s.wait_ge(s_isc.h, isc_total)
                s.wait_ge(s_acc.h, acc_pre_e)
                s.dma_start(out=idx_sb[:, ep * 32:ep * 32 + 16],
                            in_=idx_tab[te * CEXP:(te + 1) * CEXP, :].rearrange("(p k) x -> p (k x)", p=128)
                            ).then_inc(s_we.h, 16)
        block.sync(_)
        s_we.inc(32 if sh else 48)
        wr = s_we.n
        return wr, (wr if sh else wr - 16), (0 if sh else wr)

    pending[0] = emit_loads(0)
    for e in range(NE + 1):
        shared = (e == NE)
        NK = 32 if shared else NE
        epar = e % 2
        acc_expert_base = s_acc.n
        add_cnt = {}
        we_ready, we_w_ready, idx_ready = pending[e]
        conv13_c = []
        conv2_c = []
        for c in range(8):
            def _(v, c=c, we_w_ready=we_w_ready, prev_w2_mm=prev_w2_mm):
                if c == 0:
                    v.wait_ge(s_we.h, we_w_ready)
                    v.wait_ge(s_mm.h, prev_w2_mm)
                v.tensor_copy(out=w13h[:, c, :], in_=land13[:, c, :])
                r = v.tensor_tensor(out=w13l[:, c, :], in0=land13[:, c, :], in1=w13h[:, c, :], op=OP.subtract)
                r.then_inc(s_ve.h, 1)
            block.vector(_)
            s_ve.inc(1)
            conv13_c.append(s_ve.n)
        for c in range(4):
            def _(v, c=c):
                v.tensor_copy(out=w2h[:, c, :], in_=land2[:, c, :])
                r = v.tensor_tensor(out=w2l[:, c, :], in0=land2[:, c, :], in1=w2h[:, c, :], op=OP.subtract)
                r.then_inc(s_ve.h, 1)
            block.vector(_)
            s_ve.inc(1)
            conv2_c.append(s_ve.n)
        w_ready = conv13_c[0]
        if e + 1 <= NE:
            pending[e + 1] = emit_loads(e + 1)

        trans_mm_cnt = {}
        ce_ve_cnt = {}

        def emit_gather(te, k, mm_pre, ve_pre_g, we_pre):
            sh = (te == NE)
            ep = te % 2
            kk8 = k % 8
            def _(g, te=te, sh=sh, ep=ep, k=k, kk8=kk8, mm_pre=mm_pre, ve_pre_g=ve_pre_g, we_pre=we_pre):
                g.wait_ge(s_mm.h, mm_pre)
                g.wait_ge(s_ve.h, ve_pre_g)
                if we_pre:
                    g.wait_ge(s_we.h, we_pre)
                if sh:
                    g.dma_start(out=xg[:, kk8, :], in_=xn_tok[k * 128:(k + 1) * 128, :]
                                ).then_inc(gslot[kk8].h, 16)
                else:
                    g.indirect_dma_start(
                        out=xg[:, kk8, :], out_offset=None,
                        in_=xn_tok,
                        in_offset=bass.IndirectOffsetOnAxis(ap=idx_sb[:, ep * 32 + k:ep * 32 + k + 1], axis=0),
                        bounds_check=regbox["tokbnd"], oob_is_err=False,
                    ).then_inc(gslot[kk8].h, 16)
                    g.indirect_dma_start(
                        out=cg[:, kk8, :], out_offset=None,
                        in_=comb_tok,
                        in_offset=bass.IndirectOffsetOnAxis(ap=idx_sb[:, ep * 32 + k:ep * 32 + k + 1], axis=0),
                        bounds_check=regbox["tokbnd"], oob_is_err=False,
                    ).then_inc(gslot[kk8].h, 16)
            block.gpsimd(_)
            gslot[kk8].inc(16 if sh else 32)
            return gslot[kk8].n

        def emit_cecopy(k, e=e, shared=shared):
            if shared:
                ce_ve_cnt[k] = s_ve.n
                return
            kk8 = k % 8
            def _(v, k=k, kk8=kk8, gat_g=gat_cnt[k]):
                v.wait_ge(gslot[kk8].h, gat_g)
                r = v.tensor_copy(out=cekeep[:, k:k + 1], in_=cg[:, kk8, e:e + 1])
                r.then_inc(s_ve.h, 1)
            block.vector(_)
            s_ve.inc(1)
            ce_ve_cnt[k] = s_ve.n

        if e == 0:
            gat_cnt = {}
            for k in range(8):
                gat_cnt[k] = emit_gather(0, k, 0, 0, idx_ready)
        else:
            gat_cnt = dict(next_gat)
        next_gat = {}
        for k in range(min(8, NK)):
            emit_cecopy(k)

        for grp in range(NK // 4):
            for kk in range(4):
                k = grp * 4 + kk
                kk8 = k % 8
                gat_g = gat_cnt[k]
                for half in range(2):
                    cs = range(half * 4, half * 4 + 4)
                    sc_g = s_sc.n
                    ve_tr = prev_yfm_ve if (kk == 0 and half == 0) else last_xfl_ve
                    def _(t, k=k, kk8=kk8, half=half, cs=cs, gat_g=gat_g, sc_g=sc_g, ve_tr=ve_tr):
                        t.wait_ge(gslot[kk8].h, gat_g)
                        t.wait_ge(s_sc.h, sc_g)
                        t.wait_ge(s_ve.h, ve_tr)
                        for i, c in enumerate(cs):
                            t.matmul(PC[6][:, i * 128:(i + 1) * 128], xg[:, kk8, c * 128:(c + 1) * 128],
                                     eye_bf[:, :], start=(i == 0), stop=(i == 3))
                            r = t.matmul(PC[7][:, i * 128:(i + 1) * 128], xg[:, kk8, D + c * 128:D + (c + 1) * 128],
                                         eye_bf[:, :], start=(i == 0), stop=(i == 3))
                        r.then_inc(s_mm.h, 1)
                    block.tensor(_)
                    s_mm.inc(1)
                    mm_g = s_mm.n
                    def _(s, kk=kk, half=half, mm_g=mm_g):
                        s.wait_ge(s_mm.h, mm_g)
                        r = s.copy(out=xfh[:, half * 4:(half + 1) * 4, kk * 128:(kk + 1) * 128], in_=PC[6][:, :])
                        r.then_inc(s_sc.h, 1)
                    block.scalar(_)
                    s_sc.inc(1)
                    def _(v, kk=kk, half=half, mm_g=mm_g):
                        v.wait_ge(s_mm.h, mm_g)
                        r = v.tensor_copy(out=xfl[:, half * 4:(half + 1) * 4, kk * 128:(kk + 1) * 128], in_=PC[7][:, :])
                        r.then_inc(s_ve.h, 1)
                    block.vector(_)
                    s_ve.inc(1)
                    last_xfl_ve = s_ve.n
                trans_mm_cnt[k] = s_mm.n
                if k + 8 < NK:
                    gat_cnt[k + 8] = emit_gather(e, k + 8, trans_mm_cnt[k], ce_ve_cnt[k], 0)
                elif e + 1 <= NE:
                    next_gat[k - 8] = emit_gather(e + 1, k - 8, trans_mm_cnt[k], ce_ve_cnt[k],
                                                  pending[e + 1][2])
            xf_ready_sc = s_sc.n
            xf_ready_ve = s_ve.n

            def _(t, xf_ready_sc=xf_ready_sc, xf_ready_ve=xf_ready_ve, conv13_c=conv13_c,
                  prev_silu_sc=prev_silu_sc, prev_yfm_ve=prev_yfm_ve):
                t.wait_ge(s_sc.h, max(xf_ready_sc, prev_silu_sc))
                t.wait_ge(s_ve.h, max(xf_ready_ve, prev_yfm_ve))
                for c in range(8):
                    t.wait_ge(s_ve.h, conv13_c[c])
                    for fb in range(4):
                        mm3(t, PC[fb][:, :],
                            w13h[:, c, fb * 128:(fb + 1) * 128], w13l[:, c, fb * 128:(fb + 1) * 128],
                            xfh[:, c, :], xfl[:, c, :], first=(c == 0), last=(c == 7), inc=(c == 7))
            block.tensor(_)
            s_mm.inc(4)
            mm_gate = s_mm.n
            def _(s, mm_gate=mm_gate):
                for fb in range(4):
                    s.wait_ge(s_mm.h, mm_gate - 3 + fb)
                    r = s.activation(out=s_sb[:, fb, :], in_=PC[fb][:, :], func=AF.Silu)
                r.then_inc(s_sc.h, 1)
            block.scalar(_)
            s_sc.inc(1)
            prev_silu_sc = s_sc.n
            def _(t, prev_yout_ve=prev_yout_ve):
                t.wait_ge(s_ve.h, prev_yout_ve)
                for c in range(8):
                    for fb in range(4):
                        mm3(t, PC[4 + fb][:, :],
                            w13h[:, c, 512 + fb * 128:512 + (fb + 1) * 128],
                            w13l[:, c, 512 + fb * 128:512 + (fb + 1) * 128],
                            xfh[:, c, :], xfl[:, c, :], first=(c == 0), last=(c == 7), inc=(c == 7))
            block.tensor(_)
            s_mm.inc(4)
            mm_up = s_mm.n
            sc_silu = s_sc.n
            def _(v, mm_up=mm_up, sc_silu=sc_silu):
                v.wait_ge(s_sc.h, sc_silu)
                for fb in range(4):
                    v.wait_ge(s_mm.h, mm_up - 3 + fb)
                    v.tensor_tensor(out=s_sb[:, fb, :], in0=s_sb[:, fb, :], in1=PC[4 + fb][:, :], op=OP.mult)
                for fb in range(4):
                    v.tensor_copy(out=hh[:, fb, :], in_=s_sb[:, fb, :])
                    r = v.tensor_tensor(out=hl[:, fb, :], in0=s_sb[:, fb, :], in1=hh[:, fb, :], op=OP.subtract)
                r.then_inc(s_ve.h, 1)
            block.vector(_)
            s_ve.inc(1)
            h_ready = s_ve.n
            def _(t, h_ready=h_ready, conv2_c=conv2_c):
                t.wait_ge(s_ve.h, h_ready)
                for fb in range(4):
                    t.wait_ge(s_ve.h, conv2_c[fb])
                    for db in range(8):
                        mm3(t, PC[db][:, :],
                            w2h[:, fb, db * 128:(db + 1) * 128], w2l[:, fb, db * 128:(db + 1) * 128],
                            hh[:, fb, :], hl[:, fb, :], first=(fb == 0), last=(fb == 3), inc=(fb == 3))
            block.tensor(_)
            s_mm.inc(8)
            mm_y = s_mm.n
            prev_w2_mm = mm_y
            def _(v, mm_y=mm_y):
                for db in range(8):
                    v.wait_ge(s_mm.h, mm_y - 7 + db)
                    r = v.tensor_copy(out=y_fm[:, db, :], in_=PC[db][:, :])
                r.then_inc(s_ve.h, 1)
            block.vector(_)
            s_ve.inc(1)
            yfm_ready = s_ve.n
            prev_yfm_ve = yfm_ready

            for kk in range(4):
                k = grp * 4 + kk
                ypar = k % 4
                acc_pre = add_cnt.get(k - 4, acc_expert_base)
                for half in range(2):
                    dbs = range(half * 4, half * 4 + 4)
                    ve_g = s_ve.n
                    def _(t, kk=kk, half=half, dbs=dbs, yfm_ready=yfm_ready, ve_g=ve_g):
                        t.wait_ge(s_ve.h, max(yfm_ready, ve_g))
                        for i, db in enumerate(dbs):
                            r = t.matmul(PC[4 + half][:, i * 128:(i + 1) * 128], y_fm[:, db, kk * 128:(kk + 1) * 128],
                                         eye_sb[:, :], start=(i == 0), stop=(i == 3))
                        r.then_inc(s_mm.h, 1)
                    block.tensor(_)
                    s_mm.inc(1)
                    mm_g = s_mm.n
                    def _(v, k=k, kk=kk, half=half, ypar=ypar, mm_g=mm_g, shared=shared, acc_pre=acc_pre):
                        v.wait_ge(s_mm.h, mm_g)
                        if half == 0:
                            v.wait_ge(s_acc.h, acc_pre)
                        if shared:
                            r = v.tensor_copy(out=y_tok[ypar][:, half * 512:(half + 1) * 512],
                                              in_=PC[4 + half][:, :])
                        else:
                            r = v.tensor_scalar(out=y_tok[ypar][:, half * 512:(half + 1) * 512],
                                                in0=PC[4 + half][:, :],
                                                scalar1=cekeep[:, k:k + 1], scalar2=None, op0=OP.mult)
                        r.then_inc(s_ve.h, 1)
                    block.vector(_)
                    s_ve.inc(1)
                ve_g = s_ve.n
                def _(g, e=e, k=k, ypar=ypar, epar=epar, shared=shared, ve_g=ve_g,
                      acc_expert_base=acc_expert_base, first=(k == 0)):
                    g.wait_ge(s_ve.h, ve_g)
                    if first:
                        g.wait_ge(s_acc.h, acc_expert_base)
                    if shared:
                        g.dma_start(out=out_tok[k * 128:(k + 1) * 128, :], in_=y_tok[ypar][:, :],
                                    accum_op=OP.add).then_inc(s_acc.h, 16)
                    else:
                        g.indirect_dma_start(
                            out=out_tok,
                            out_offset=bass.IndirectOffsetOnAxis(ap=idx_sb[:, epar * 32 + k:epar * 32 + k + 1], axis=0),
                            in_=y_tok[ypar][:, :], in_offset=None,
                            compute_op=OP.add,
                        ).then_inc(s_acc.h, 16)
                block.gpsimd(_)
                s_acc.inc(16)
                add_cnt[k] = s_acc.n
            prev_yout_ve = s_ve.n

            for kk in range(4):
                k2 = (grp + 2) * 4 + kk
                if k2 < NK:
                    emit_cecopy(k2)

    acc_total = s_acc.n
    def _(g, acc_total=acc_total):
        g.wait_ge(s_acc.h, acc_total)
    block.gpsimd(_)
    esC.close()

    es0.close()
    return nc


_NC_CACHE = None


def kernel(L0_attn_u, L0_attn_W_in, L0_attn_W_out, L0_attn_Wp, L0_attn_bp,
           L0_attn_alpha_log, L0_attn_delta_bias, L0_attn_log_gamma,
           L0_ffn_input_ids, L0_ffn_gate_w, L0_ffn_gate_b, L0_ffn_w13, L0_ffn_w2,
           L0_ffn_w13_s, L0_ffn_w2_s):
    global _NC_CACHE
    u = np.asarray(L0_attn_u, dtype=np.float32)

    c = np.ascontiguousarray
    w_inT = c(np.asarray(L0_attn_W_in, np.float32).T)
    w_outT = c(np.asarray(L0_attn_W_out, np.float32).T)
    wpT = c(np.asarray(L0_attn_Wp, np.float32).T)
    gate_wT = c(np.asarray(L0_ffn_gate_w, np.float32).T)
    gate_b = c(np.asarray(L0_ffn_gate_b, np.float32).reshape(NE, 1))
    w13T = c(np.asarray(L0_ffn_w13, np.float32).transpose(0, 2, 1))
    w2T = c(np.asarray(L0_ffn_w2, np.float32).transpose(0, 2, 1))
    w13sT = c(np.asarray(L0_ffn_w13_s, np.float32).T)
    w2sT = c(np.asarray(L0_ffn_w2_s, np.float32).T)
    alpha_log = c(np.asarray(L0_attn_alpha_log, np.float32).reshape(8, 128).T)
    delta_bias = c(np.asarray(L0_attn_delta_bias, np.float32).reshape(8, 128).T)
    bp = c(np.asarray(L0_attn_bp, np.float32).reshape(3, 8, 128).transpose(2, 0, 1).reshape(128, 24))
    log_gamma = c(np.asarray(L0_attn_log_gamma, np.float32).reshape(1, 1))
    eye = np.eye(128, dtype=np.float32)
    ebase = (np.arange(NE, dtype=np.float32) * CEXP).reshape(NE, 1)

    shared = dict(w_inT=w_inT, w_outT=w_outT, wpT=wpT, gate_wT=gate_wT, gate_b=gate_b,
                  w13T=w13T, w2T=w2T, w13sT=w13sT, w2sT=w2sT, alpha_log=alpha_log,
                  delta_bias=delta_bias, bp=bp, log_gamma=log_gamma, eye=eye, ebase=ebase)

    in_maps = []
    for b in range(N_CORES):
        m = dict(shared)
        m["xT"] = c(u[b].T)
        in_maps.append(m)

    if _NC_CACHE is None:
        _NC_CACHE = build_program()
    nc = _NC_CACHE

    res = run_bass_kernel_spmd(nc, in_maps, core_ids=list(range(N_CORES)))
    out = np.stack([np.asarray(res.results[b]["out_tok"])[:TC] for b in range(N_CORES)], axis=0)
    return out


# version 17
# speedup vs baseline: 1.0699x; 1.0278x over previous
import numpy as np
from contextlib import ExitStack

import concourse.bass as bass
import concourse.mybir as mybir
from concourse.bass_utils import run_bass_kernel_spmd

F32 = mybir.dt.float32
BF16 = mybir.dt.bfloat16
AF = mybir.ActivationFunctionType
OP = mybir.AluOpType
AX = mybir.AxisListType

N_CORES = 8
D = 1024
TC = 4096
TT = 1024
NTILE = TC // TT
NG = TT // 512
NE = 16
FI = 512
NSQ = 14
NPW = 6
SIGMA_STAGE = 99
DEBUG_SIGMA = False
EPS_A = 1e-4
LN_EPS = 1e-5


class Sem:
    def __init__(self, es, nc, name):
        self.h = es.enter_context(nc.semaphore(name))
        self.n = 0

    def inc(self, k=1):
        self.n += k
        return self.n


def emit_sigma(nc, block, es, wT, scale_out, consts, sems, tag):
    s_mm, s_ve, s_sc, s_ld = sems["mm"], sems["ve"], sems["sc"], sems["ld"]
    ones_col, ones_row, ones_bf = consts["ones_col"], consts["ones_row"], consts["ones_bf"]

    wt = es.enter_context(nc.sbuf_tensor(f"sg_wt{tag}", [128, 8, 1024], F32))
    Bf = es.enter_context(nc.sbuf_tensor(f"sg_Bf{tag}", [128, 8, 1024], F32))
    M0 = es.enter_context(nc.sbuf_tensor(f"sg_M0{tag}", [128, 8, 1024], BF16))
    M1 = es.enter_context(nc.sbuf_tensor(f"sg_M1{tag}", [128, 8, 1024], BF16))
    junk = es.enter_context(nc.sbuf_tensor(f"sg_junk{tag}", [128, 512], BF16))
    acc = es.enter_context(nc.sbuf_tensor(f"sg_acc{tag}", [128, 16], F32))
    r1 = es.enter_context(nc.sbuf_tensor(f"sg_r1{tag}", [128, 1], F32))
    inv1 = es.enter_context(nc.sbuf_tensor(f"sg_inv1{tag}", [128, 1], F32))
    ssc = es.enter_context(nc.sbuf_tensor(f"sg_ssc{tag}", [1, 1], F32))
    vv = es.enter_context(nc.sbuf_tensor(f"sg_vv{tag}", [128, 8], F32))
    tt = es.enter_context(nc.sbuf_tensor(f"sg_tt{tag}", [128, 8], F32))
    prod = es.enter_context(nc.sbuf_tensor(f"sg_prod{tag}", [128, 8], F32))
    den_sb = es.enter_context(nc.sbuf_tensor(f"sg_den{tag}", [128, 8], F32))
    n2 = es.enter_context(nc.sbuf_tensor(f"sg_n2{tag}", [1, 2], F32))
    ps0 = es.enter_context(nc.psum_tensor(f"sg_ps0{tag}", [128, 512], F32))
    ps1 = es.enter_context(nc.psum_tensor(f"sg_ps1{tag}", [128, 512], F32))
    pv = es.enter_context(nc.psum_tensor(f"sg_pv{tag}", [128, 1], F32))
    pv2 = es.enter_context(nc.psum_tensor(f"sg_pv2{tag}", [128, 1], F32))

    ld_gate = s_ld.n + 16
    ve_prev = s_ve.n
    sc_prev = s_sc.n
    mm_prev = s_mm.n

    def _(s):
        s.wait_ge(s_ve.h, ve_prev)
        s.wait_ge(s_sc.h, sc_prev)
        s.wait_ge(s_mm.h, mm_prev)
        s.dma_start(out=wt[:, :, :], in_=wT.rearrange("(c p) n -> p c n", p=128)).then_inc(s_ld.h, 16)
    block.sync(_)
    s_ld.inc(16)

    def sq_pass(src, dst, is_b_build):
        mm_base = s_mm.n
        ve_base = s_ve.n
        sc_base = s_sc.n

        def _t(t):
            if is_b_build:
                t.wait_ge(s_ld.h, ld_gate)
            t.wait_ge(s_ve.h, ve_base)
            t.wait_ge(s_sc.h, sc_base)
            for k in range(16):
                iblk, jg = k // 2, k % 2
                ps = ps0 if jg == 0 else ps1
                if k >= 2:
                    t.wait_ge(s_ve.h, ve_base + k - 1)
                    t.wait_ge(s_sc.h, sc_base + k - 1)
                for d in range(8):
                    r = t.matmul(ps[:, :], src[:, d, iblk * 128:(iblk + 1) * 128],
                                 src[:, d, jg * 512:(jg + 1) * 512],
                                 start=(d == 0), stop=(d == 7))
                r.then_inc(s_mm.h, 1)
        block.tensor(_t)
        s_mm.inc(16)

        def _v(v):
            for k in range(16):
                iblk, jg = k // 2, k % 2
                ps = ps0 if jg == 0 else ps1
                v.wait_ge(s_mm.h, mm_base + k + 1)
                tgt = Bf if is_b_build else dst
                r = v.tensor_copy(out=tgt[:, iblk, jg * 512:(jg + 1) * 512], in_=ps[:, :])
                r.then_inc(s_ve.h, 1)
        block.vector(_v)
        s_ve.inc(16)

        def _s(s):
            for k in range(16):
                iblk, jg = k // 2, k % 2
                s.wait_ge(s_ve.h, ve_base + k + 1)
                src_sb = (Bf if is_b_build else dst)[:, iblk, jg * 512:(jg + 1) * 512]
                if is_b_build:
                    r = s.activation(out=M0[:, iblk, jg * 512:(jg + 1) * 512], in_=src_sb,
                                     func=AF.Copy, bias=0.0)
                else:
                    r = s.activation(out=junk[:, :], in_=src_sb, func=AF.Abs,
                                     accum_out=acc[:, k:k + 1])
                r.then_inc(s_sc.h, 1)
        block.scalar(_s)
        s_sc.inc(16)

    sq_pass(wt, None, True)

    if SIGMA_STAGE < 2:
        return
    bufs = [M0, M1]
    for it in range(NSQ):
        src, dst = bufs[it % 2], bufs[(it + 1) % 2]
        sq_pass(src, dst, False)

        sc_gate = s_sc.n
        def _v2(v, sc_gate=sc_gate):
            v.wait_ge(s_sc.h, sc_gate)
            r = v.tensor_reduce(out=r1[:, :], in_=acc[:, :], axis=AX.X, op=OP.add)
            r.then_inc(s_ve.h, 1)
        block.vector(_v2)
        s_ve.inc(1)
        ve_g = s_ve.n
        def _t2(t, ve_g=ve_g):
            t.wait_ge(s_ve.h, ve_g)
            r = t.matmul(pv[:1, :], ones_col[:, :], r1[:, :], start=True, stop=True)
            r.then_inc(s_mm.h, 1)
        block.tensor(_t2)
        s_mm.inc(1)
        mm_g = s_mm.n
        def _v3(v, mm_g=mm_g):
            v.wait_ge(s_mm.h, mm_g)
            r = v.tensor_copy(out=ssc[:, :], in_=pv[:1, :])
            r.then_inc(s_ve.h, 1)
        block.vector(_v3)
        s_ve.inc(1)
        ve_g2 = s_ve.n
        def _t3(t, ve_g2=ve_g2):
            t.wait_ge(s_ve.h, ve_g2)
            r = t.matmul(pv2[:, :], ones_row[:1, :], ssc[:1, :], start=True, stop=True)
            r.then_inc(s_mm.h, 1)
        block.tensor(_t3)
        s_mm.inc(1)
        mm_g2 = s_mm.n
        def _v4(v, dst=dst, mm_g2=mm_g2):
            v.wait_ge(s_mm.h, mm_g2)
            v.reciprocal(out=inv1[:, :], in_=pv2[:, :])
            v.drain()
            for c in range(8):
                r = v.tensor_scalar(out=dst[:, c, :], in0=dst[:, c, :],
                                    scalar1=inv1[:, :], scalar2=256.0,
                                    op0=OP.mult, op1=OP.mult)
            r.then_inc(s_ve.h, 1)
        block.vector(_v4)
        s_ve.inc(1)

    Mfin = bufs[NSQ % 2]
    if SIGMA_STAGE < 3:
        return

    for pblk in range(8):
        gate = s_ve.n
        def _t(t, pblk=pblk, gate=gate):
            t.wait_ge(s_ve.h, gate)
            for q in range(8):
                r = t.matmul(pv[:, :], Mfin[:, q, pblk * 128:(pblk + 1) * 128],
                             ones_bf[:, :], start=(q == 0), stop=(q == 7))
            r.then_inc(s_mm.h, 1)
        block.tensor(_t)
        s_mm.inc(1)
        mm_g = s_mm.n
        def _v(v, pblk=pblk, mm_g=mm_g):
            v.wait_ge(s_mm.h, mm_g)
            r = v.tensor_copy(out=vv[:, pblk:pblk + 1], in_=pv[:, :])
            r.then_inc(s_ve.h, 1)
        block.vector(_v)
        s_ve.inc(1)

    if SIGMA_STAGE < 4:
        return
    vbuf, tbuf = vv, tt
    for step in range(NPW + 1):
        for pblk in range(8):
            gate = s_ve.n
            def _t(t, pblk=pblk, vbuf=vbuf, gate=gate):
                t.wait_ge(s_ve.h, gate)
                for q in range(8):
                    r = t.matmul(pv[:, :], Bf[:, q, pblk * 128:(pblk + 1) * 128],
                                 vbuf[:, q:q + 1], start=(q == 0), stop=(q == 7))
                r.then_inc(s_mm.h, 1)
            block.tensor(_t)
            s_mm.inc(1)
            mm_g = s_mm.n
            def _v(v, pblk=pblk, tbuf=tbuf, mm_g=mm_g):
                v.wait_ge(s_mm.h, mm_g)
                r = v.tensor_copy(out=tbuf[:, pblk:pblk + 1], in_=pv[:, :])
                r.then_inc(s_ve.h, 1)
            block.vector(_v)
            s_ve.inc(1)
        vbuf, tbuf = tbuf, vbuf
    vlast, tlast = tbuf, vbuf

    if SIGMA_STAGE < 5:
        return
    def _v5(v):
        v.tensor_tensor(out=prod[:, :], in0=vlast[:, :], in1=tlast[:, :], op=OP.mult)
        r = v.tensor_tensor(out=den_sb[:, :], in0=vlast[:, :], in1=vlast[:, :], op=OP.mult)
        r.then_inc(s_ve.h, 1)
    block.vector(_v5)
    s_ve.inc(1)

    ve_g = s_ve.n
    def _t4(t, ve_g=ve_g):
        t.wait_ge(s_ve.h, ve_g)
        for q in range(8):
            t.matmul(pv[:1, :], ones_col[:, :], prod[:, q:q + 1], start=(q == 0), stop=(q == 7))
        for q in range(8):
            r = t.matmul(pv2[:1, :], ones_col[:, :], den_sb[:, q:q + 1], start=(q == 0), stop=(q == 7))
        r.then_inc(s_mm.h, 1)
    block.tensor(_t4)
    s_mm.inc(1)

    mm_g = s_mm.n
    def _v6(v, mm_g=mm_g):
        v.wait_ge(s_mm.h, mm_g)
        v.tensor_copy(out=n2[:, 0:1], in_=pv[:1, :])
        v.tensor_copy(out=n2[:, 1:2], in_=pv2[:1, :])
        v.drain()
        v.reciprocal(out=n2[:, 1:2], in_=n2[:, 1:2])
        v.drain()
        r = v.tensor_tensor(out=scale_out[:, :], in0=n2[:, 0:1], in1=n2[:, 1:2], op=OP.mult)
        r.then_inc(s_ve.h, 1)
    block.vector(_v6)
    s_ve.inc(1)

    ve_g = s_ve.n
    def _s2(s, ve_g=ve_g):
        s.wait_ge(s_ve.h, ve_g)
        r = s.activation(out=scale_out[:, :], in_=scale_out[:, :], func=AF.Sqrt)
        r.then_inc(s_sc.h, 1)
    block.scalar(_s2)
    s_sc.inc(1)
    sc_g = s_sc.n
    def _v7(v, sc_g=sc_g):
        v.wait_ge(s_sc.h, sc_g)
        r = v.tensor_scalar(out=scale_out[:, :], in0=scale_out[:, :], scalar1=1.0,
                            scalar2=None, op0=OP.max)
        r.then_inc(s_ve.h, 1)
    block.vector(_v7)
    s_ve.inc(1)


U32 = mybir.dt.uint32
TT = 512
NTILE_B = TC // TT
CEXP = 2048
NDUMP = 2048
DEBUG_DUMPS = False
TOKP = TC + NDUMP
BIGS = float(2 ** 24)


def build_program():
    nc = bass.Bass("TRN2", target_bir_lowering=False, detect_race_conditions=False,
                   dynamic_dma_scratch_size=16384)

    xT = nc.dram_tensor("xT", [D, TC], F32, kind="ExternalInput").ap()
    out_tok = nc.dram_tensor("out_tok", [TOKP, D], F32, kind="ExternalOutput").ap()
    _dk = dict(kind="ExternalOutput") if DEBUG_DUMPS else {}
    xn_tok = nc.dram_tensor("xn_tok", [TC, 2 * D], BF16, **_dk).ap()
    comb_tok = nc.dram_tensor("comb_tok", [TC, NE], F32, **_dk).ap()
    idx_tab = nc.dram_tensor("idx_tab", [NE * CEXP, 1], U32, **_dk).ap()
    w_inT = nc.dram_tensor("w_inT", [D, D], F32, kind="ExternalInput").ap()
    w_outT = nc.dram_tensor("w_outT", [D, D], F32, kind="ExternalInput").ap()
    wpT = nc.dram_tensor("wpT", [D, 3 * D], F32, kind="ExternalInput").ap()
    gate_wT = nc.dram_tensor("gate_wT", [D, NE], F32, kind="ExternalInput").ap()
    gate_b = nc.dram_tensor("gate_b", [NE, 1], F32, kind="ExternalInput").ap()
    w13T = nc.dram_tensor("w13T", [NE, D, 2 * FI], F32, kind="ExternalInput").ap()
    w2T = nc.dram_tensor("w2T", [NE, FI, D], F32, kind="ExternalInput").ap()
    w13sT = nc.dram_tensor("w13sT", [D, 2 * FI], F32, kind="ExternalInput").ap()
    w2sT = nc.dram_tensor("w2sT", [FI, D], F32, kind="ExternalInput").ap()
    alpha_log = nc.dram_tensor("alpha_log", [128, 8], F32, kind="ExternalInput").ap()
    delta_bias = nc.dram_tensor("delta_bias", [128, 8], F32, kind="ExternalInput").ap()
    bp = nc.dram_tensor("bp", [128, 24], F32, kind="ExternalInput").ap()
    log_gamma = nc.dram_tensor("log_gamma", [1, 1], F32, kind="ExternalInput").ap()
    eye = nc.dram_tensor("eye", [128, 128], F32, kind="ExternalInput").ap()
    ebase = nc.dram_tensor("ebase", [NE, 1], F32, kind="ExternalInput").ap()

    es0 = ExitStack()
    eye_sb = es0.enter_context(nc.sbuf_tensor("eye_sb", [128, 128], F32))
    eye_bf = es0.enter_context(nc.sbuf_tensor("eye_bf", [128, 128], BF16))
    ones_col = es0.enter_context(nc.sbuf_tensor("ones_col", [128, 1], F32))
    ones_row = es0.enter_context(nc.sbuf_tensor("ones_row", [1, 128], F32))
    ones_bf = es0.enter_context(nc.sbuf_tensor("ones_bf", [128, 1], BF16))
    ones16 = es0.enter_context(nc.sbuf_tensor("ones16", [NE, TT], F32))
    scale1 = es0.enter_context(nc.sbuf_tensor("scale1", [1, 1], F32))
    scale2 = es0.enter_context(nc.sbuf_tensor("scale2", [1, 1], F32))
    attn_scale = es0.enter_context(nc.sbuf_tensor("attn_scale", [128, 1], F32))
    lg_sb = es0.enter_context(nc.sbuf_tensor("lg_sb", [1, 1], F32))
    alpha_n = es0.enter_context(nc.sbuf_tensor("alpha_n", [128, 8], F32))
    db_comb = es0.enter_context(nc.sbuf_tensor("db_comb", [128, 8], F32))
    bp_b = es0.enter_context(nc.sbuf_tensor("bp_b", [128, 8], F32))
    bp_c = es0.enter_context(nc.sbuf_tensor("bp_c", [128, 8], F32))
    gate_b_sb = es0.enter_context(nc.sbuf_tensor("gate_b_sb", [NE, 1], F32))
    gb_bcast = es0.enter_context(nc.sbuf_tensor("gb_bcast", [128, NE], F32))
    ebase_sb = es0.enter_context(nc.sbuf_tensor("ebase_sb", [NE, 1], F32))
    gwt = es0.enter_context(nc.sbuf_tensor("gwt", [128, 8, NE], F32))
    carry = es0.enter_context(nc.sbuf_tensor("carry", [128, 8], F32))
    cnt_vec = es0.enter_context(nc.sbuf_tensor("cnt_vec", [NE, 1], F32))
    tmp8 = es0.enter_context(nc.sbuf_tensor("tmp8", [128, 8], F32))
    cpool = es0.enter_context(nc.sbuf_tensor("cpool", [128, 4], F32))
    Mcol = es0.enter_context(nc.sbuf_tensor("Mcol", [128, 1], F32))
    negM = es0.enter_context(nc.sbuf_tensor("negM", [128, 1], F32))

    sems = {}
    for name in ["ld", "mm", "ve", "sc", "xin", "otok", "isc", "gat", "acc", "wa", "we", "xe",
                 "gs0", "gs1", "gs2", "gs3", "gs4", "gs5", "gs6", "gs7"]:
        sems[name] = Sem(es0, nc, "s_" + name)
    s_ld, s_mm, s_ve, s_sc = sems["ld"], sems["mm"], sems["ve"], sems["sc"]
    s_xin, s_otok, s_isc = sems["xin"], sems["otok"], sems["isc"]
    s_gat, s_acc, s_wa, s_we, s_xe = sems["gat"], sems["acc"], sems["wa"], sems["we"], sems["xe"]
    gslot = [sems[f"gs{j}"] for j in range(8)]

    consts = {"ones_col": ones_col, "ones_row": ones_row, "ones_bf": ones_bf}

    with ExitStack() as esA:
        ztile = esA.enter_context(nc.sbuf_tensor("ztile", [128, 1024], F32))
        idxfill = esA.enter_context(nc.sbuf_tensor("idxfill", [128, 256], U32))
        block = esA.enter_context(nc.Block())

        def _(s):
            s.dma_start(out=eye_sb[:, :], in_=eye).then_inc(s_ld.h, 16)
            s.dma_start(out=lg_sb[:, :], in_=log_gamma).then_inc(s_ld.h, 16)
            s.dma_start(out=alpha_n[:, :], in_=alpha_log).then_inc(s_ld.h, 16)
            s.dma_start(out=db_comb[:, :], in_=delta_bias).then_inc(s_ld.h, 16)
            s.dma_start(out=tmp8[:, :], in_=bp[:, 0:8]).then_inc(s_ld.h, 16)
            s.dma_start(out=bp_b[:, :], in_=bp[:, 8:16]).then_inc(s_ld.h, 16)
            s.dma_start(out=bp_c[:, :], in_=bp[:, 16:24]).then_inc(s_ld.h, 16)
            s.dma_start(out=gate_b_sb[:, :], in_=gate_b).then_inc(s_ld.h, 16)
            s.dma_start(out=gwt[:, :, :], in_=gate_wT.rearrange("(c p) e -> p c e", p=128)).then_inc(s_ld.h, 16)
            s.dma_start(out=ebase_sb[:, :], in_=ebase).then_inc(s_ld.h, 16)
        block.sync(_)
        s_ld.inc(16 * 10)

        ld_g = s_ld.n
        def _(v, ld_g=ld_g):
            v.wait_ge(s_ld.h, ld_g)
            v.memset(ones_col[:, :], 1.0)
            v.memset(ones_row[:, :], 1.0)
            v.memset(ones_bf[:, :], 1.0)
            v.memset(ones16[:, :], 1.0)
            v.memset(carry[:, :], 0.0)
            v.memset(cnt_vec[:, :], 0.0)
            v.memset(ztile[:, :], 0.0)
            v.memset(cpool[:, 0:1], 100000.0)
            v.drain()
            v.reciprocal(out=cpool[:, 0:1], in_=cpool[:, 0:1])
            v.drain()
            v.memset(cpool[:, 1:2], 1e-12)
            v.memset(cpool[:, 2:3], -1.0)
            v.memset(cpool[:, 3:4], 1.0)
            v.memset(Mcol[:, :], 1.0)
            v.drain()
            v.tensor_scalar(out=Mcol[:, :], in0=Mcol[:, :], scalar1=float(2.0 ** -29),
                            scalar2=1.0, op0=OP.add, op1=OP.subtract)
            v.drain()
            v.tensor_scalar(out=Mcol[:, :], in0=Mcol[:, :], scalar1=float(2.0 ** 58),
                            scalar2=None, op0=OP.mult)
            v.drain()
            v.tensor_scalar(out=negM[:, :], in0=Mcol[:, :], scalar1=-1.0, scalar2=None, op0=OP.mult)
            v.drain()
            r = v.tensor_copy(out=eye_bf[:, :], in_=eye_sb[:, :])
            r.then_inc(s_ve.h, 1)
        block.vector(_)
        s_ve.inc(1)
        def _(s, ld_g=ld_g):
            s.wait_ge(s_ld.h, ld_g)
            s.activation(out=alpha_n[:, :], in_=alpha_n[:, :], func=AF.Exp)
            s.add(alpha_n[:, :], alpha_n[:, :], 1.0)
            s.activation(out=alpha_n[:, :], in_=alpha_n[:, :], func=AF.Ln)
            r = s.mul(alpha_n[:, :], alpha_n[:, :], -1.0)
            r.then_inc(s_sc.h, 1)
        block.scalar(_)
        s_sc.inc(1)
        def _(v):
            r = v.tensor_tensor(out=db_comb[:, :], in0=db_comb[:, :], in1=tmp8[:, :], op=OP.add)
            r.then_inc(s_ve.h, 1)
        block.vector(_)
        s_ve.inc(1)

        ve_g = s_ve.n
        regbox = {}
        def _(g, ve_g=ve_g):
            regbox["slotbnd"] = g.alloc_register("slotbnd")
            g.reg_mov(regbox["slotbnd"], NE * CEXP - 1)
            regbox["tokbnd"] = g.alloc_register("tokbnd")
            g.reg_mov(regbox["tokbnd"], TC - 1)
            g.wait_ge(s_ve.h, ve_g)
            for r0 in range(TC, TOKP, 128):
                g.dma_start(out=out_tok[r0:r0 + 128, :], in_=ztile[:, :]).then_inc(s_otok.h, 16)
            g.iota(idxfill[:, :], [[1, 16], [0, 16]], base=TC, channel_multiplier=16, allow_small_or_imprecise_dtypes=True)
            g.dma_start(out=idx_tab.rearrange("(p c) x -> p (c x)", p=128),
                        in_=idxfill[:, :]).then_inc(s_isc.h, 16)
        block.gpsimd(_)
        s_otok.inc(16 * (NDUMP // 128))
        s_isc.inc(16)

        with ExitStack() as esS:
            emit_sigma(nc, block, esS, w_inT, scale1, consts, sems, "a")
        with ExitStack() as esS:
            emit_sigma(nc, block, esS, w_outT, scale2, consts, sems, "b")

        with ExitStack() as esS:
            pvb = esS.enter_context(nc.psum_tensor("pvb", [128, 1], F32))
            n2gb = esS.enter_context(nc.sbuf_tensor("n2gb", [1, NE], F32))
            pgb = esS.enter_context(nc.psum_tensor("pgb", [128, NE], F32))
            ve_g = s_ve.n
            def _(s, ve_g=ve_g):
                s.wait_ge(s_ve.h, ve_g)
                r = s.activation(out=lg_sb[:, :], in_=lg_sb[:, :], func=AF.Exp)
                r.then_inc(s_sc.h, 1)
            block.scalar(_)
            s_sc.inc(1)
            sc_g = s_sc.n
            def _(v, sc_g=sc_g):
                v.wait_ge(s_sc.h, sc_g)
                v.tensor_tensor(out=scale1[:, :], in0=scale1[:, :], in1=scale2[:, :], op=OP.mult)
                v.drain()
                v.reciprocal(out=scale1[:, :], in_=scale1[:, :])
                v.drain()
                r = v.tensor_tensor(out=scale1[:, :], in0=scale1[:, :], in1=lg_sb[:, :], op=OP.mult)
                r.then_inc(s_ve.h, 1)
            block.vector(_)
            s_ve.inc(1)
            ve_g = s_ve.n
            def _(t, ve_g=ve_g):
                t.wait_ge(s_ve.h, ve_g)
                t.matmul(pvb[:, :], ones_row[:1, :], scale1[:1, :], start=True, stop=True).then_inc(s_mm.h, 1)
                r = t.matmul(pgb[:1, :NE], gate_b_sb[:, :], eye_sb[:NE, :NE], start=True, stop=True)
                r.then_inc(s_mm.h, 1)
            block.tensor(_)
            s_mm.inc(2)
            mm_g = s_mm.n
            def _(v, mm_g=mm_g):
                v.wait_ge(s_mm.h, mm_g)
                v.tensor_copy(out=attn_scale[:, :], in_=pvb[:, :])
                r = v.tensor_copy(out=n2gb[:1, :], in_=pgb[:1, :NE])
                r.then_inc(s_ve.h, 1)
            block.vector(_)
            s_ve.inc(1)
            ve_g = s_ve.n
            def _(t, ve_g=ve_g):
                t.wait_ge(s_ve.h, ve_g)
                r = t.matmul(pgb[:, :], ones_row[:1, :], n2gb[:1, :], start=True, stop=True)
                r.then_inc(s_mm.h, 1)
            block.tensor(_)
            s_mm.inc(1)
            mm_g = s_mm.n
            def _(v, mm_g=mm_g):
                v.wait_ge(s_mm.h, mm_g)
                r = v.tensor_copy(out=gb_bcast[:, :], in_=pgb[:, :])
                r.then_inc(s_ve.h, 1)
            block.vector(_)
            s_ve.inc(1)

    esB = ExitStack()
    x_sb = esB.enter_context(nc.sbuf_tensor("x_sb", [128, 8, TT], F32))
    xnh = esB.enter_context(nc.sbuf_tensor("xnh", [128, 8, TT], BF16))
    xnl = esB.enter_context(nc.sbuf_tensor("xnl", [128, 8, TT], BF16))
    Ra = esB.enter_context(nc.sbuf_tensor("Ra", [128, 8, TT], F32))
    Rb = esB.enter_context(nc.sbuf_tensor("Rb", [128, 8, TT], F32))
    Rc = esB.enter_context(nc.sbuf_tensor("Rc", [128, 8, TT], F32))
    scr = esB.enter_context(nc.sbuf_tensor("scr", [128, 8 * TT], F32))
    sgad = esB.enter_context(nc.sbuf_tensor("sgad", [128, 2, TT], F32))
    pland = [esB.enter_context(nc.sbuf_tensor(f"pland{i}", [128, 8, 256], F32)) for i in range(2)]
    phl = [(esB.enter_context(nc.sbuf_tensor(f"ph{i}", [128, 8, 256], BF16)),
            esB.enter_context(nc.sbuf_tensor(f"pl{i}", [128, 8, 256], BF16))) for i in range(2)]
    stok = esB.enter_context(nc.sbuf_tensor("stok", [128, 4, NE], F32))
    btok = esB.enter_context(nc.sbuf_tensor("btok", [128, 4, NE], F32))
    rtok = esB.enter_context(nc.sbuf_tensor("rtok", [128, 4, NE], F32))
    thr8 = esB.enter_context(nc.sbuf_tensor("thr8", [128, 4], F32))
    invd8 = esB.enter_context(nc.sbuf_tensor("invd8", [128, 4], F32))
    mske = esB.enter_context(nc.sbuf_tensor("mske", [128, 4, NE], F32))
    slots_f = esB.enter_context(nc.sbuf_tensor("slots_f", [128, 4, NE], F32))
    slots_u = esB.enter_context(nc.sbuf_tensor("slots_u", [128, 4 * NE], U32))
    ids_u = esB.enter_context(nc.sbuf_tensor("ids_u", [128, 4], U32))
    tth = [esB.enter_context(nc.sbuf_tensor(f"tth{i}", [128, 1024], BF16)) for i in range(2)]
    ttl = [esB.enter_context(nc.sbuf_tensor(f"ttl{i}", [128, 1024], BF16)) for i in range(2)]
    otok = [esB.enter_context(nc.sbuf_tensor(f"otok{i}", [128, 1024], F32)) for i in range(2)]
    P = [esB.enter_context(nc.psum_tensor(f"P{i}", [128, 512], F32)) for i in range(8)]
    blockB = esB.enter_context(nc.Block())
    block = blockB

    stats = [Rc[:1, 0, 0:512], Rc[:1, 1, 0:512], Rc[:1, 2, 0:512], Rc[:1, 3, 0:512], Rc[:1, 4, 0:512]]
    scores_em = Rc[0:NE, 5, :]
    comb_em = Rc[0:NE, 6, :]
    m01 = Rc[0:NE, 7, :]
    incl_t = Rc[0:NE, 0, :]
    srel_t = Rc[0:NE, 1, :]
    tmp_t = Rc[0:NE, 2, :]

    def mm3(t, ps, wh, wl, xh, xl, first, last, inc=None):
        t.matmul(ps, wh, xh, start=first, stop=False)
        t.matmul(ps, wh, xl, start=False, stop=False)
        r = t.matmul(ps, wl, xh, start=False, stop=last)
        if inc:
            r.then_inc(s_mm.h, 1)
        return r

    for j in range(NTILE_B):
        t0 = j * TT

        mm_pre = s_mm.n
        def _(g, t0=t0, mm_pre=mm_pre):
            g.wait_ge(s_mm.h, mm_pre)
            g.dma_start(out=x_sb[:, :, :],
                        in_=xT[:, t0:t0 + TT].rearrange("(c p) t -> p c t", p=128)).then_inc(s_xin.h, 16)
        block.gpsimd(_)
        s_xin.inc(16)

        panels = ([wpT[:, p * 256:(p + 1) * 256] for p in range(12)]
                  + [w_inT[:, p * 256:(p + 1) * 256] for p in range(4)]
                  + [w_outT[:, p * 256:(p + 1) * 256] for p in range(4)])
        wa_base = s_wa.n
        panel_conv_ve = [None] * 20

        def emit_panel(pi, gate_mm):
            slot = pi % 2
            def _(s, pi=pi, gate_mm=gate_mm, slot=slot):
                if gate_mm is not None:
                    s.wait_ge(s_mm.h, gate_mm)
                s.wait_ge(s_ve.h, s_ve.n)
                s.dma_start(out=pland[slot][:, :, :],
                            in_=panels[pi].rearrange("(c p) n -> p c n", p=128)).then_inc(s_wa.h, 16)
            block.sync(_)
            s_wa.inc(16)
            wa_g = s_wa.n
            def _(v, pi=pi, slot=slot, wa_g=wa_g):
                v.wait_ge(s_wa.h, wa_g)
                v.tensor_copy(out=phl[slot][0][:, :, :], in_=pland[slot][:, :, :])
                r = v.tensor_tensor(out=phl[slot][1][:, :, :], in0=pland[slot][:, :, :],
                                    in1=phl[slot][0][:, :, :], op=OP.subtract)
                r.then_inc(s_ve.h, 1)
            block.vector(_)
            s_ve.inc(1)
            panel_conv_ve[pi] = s_ve.n

        emit_panel(0, s_mm.n)
        emit_panel(1, s_mm.n)
        pi_loaded = 2

        xin_gate = s_xin.n
        ve_pre = s_ve.n
        def _(s, xin_gate=xin_gate, ve_pre=ve_pre):
            s.wait_ge(s_xin.h, xin_gate)
            s.wait_ge(s_ve.h, ve_pre)
            for c in range(8):
                r = s.activation(out=scr[:, c * TT:(c + 1) * TT], in_=x_sb[:, c, :], func=AF.Square)
            r.then_inc(s_sc.h, 1)
        block.scalar(_)
        s_sc.inc(1)
        sc_sq = s_sc.n
        ve_pre = s_ve.n
        def _(t, sc_sq=sc_sq, ve_pre=ve_pre, xin_gate=xin_gate):
            t.wait_ge(s_xin.h, xin_gate)
            t.wait_ge(s_sc.h, sc_sq)
            t.wait_ge(s_ve.h, ve_pre)
            for c in range(8):
                t.matmul(P[0][:1, :], ones_col[:, :], x_sb[:, c, :], start=(c == 0), stop=(c == 7))
            for c in range(8):
                r = t.matmul(P[1][:1, :], ones_col[:, :], scr[:, c * TT:(c + 1) * TT],
                             start=(c == 0), stop=(c == 7))
            r.then_inc(s_mm.h, 1)
        block.tensor(_)
        s_mm.inc(1)
        mm_g = s_mm.n
        def _(v, mm_g=mm_g):
            v.wait_ge(s_mm.h, mm_g)
            v.tensor_scalar(out=stats[0], in0=P[0][:1, :], scalar1=1.0 / D, scalar2=None, op0=OP.mult)
            v.tensor_scalar(out=stats[1], in0=P[1][:1, :], scalar1=1.0 / D, scalar2=None, op0=OP.mult)
            v.drain()
            v.tensor_tensor(out=stats[2], in0=stats[0], in1=stats[0], op=OP.mult)
            v.drain()
            r = v.tensor_tensor(out=stats[1], in0=stats[1], in1=stats[2], op=OP.subtract)
            r.then_inc(s_ve.h, 1)
        block.vector(_)
        s_ve.inc(1)
        ve_g = s_ve.n
        def _(s, ve_g=ve_g):
            s.wait_ge(s_ve.h, ve_g)
            r = s.activation(out=stats[2], in_=stats[1], func=AF.Sqrt, bias=cpool[:1, 0:1])
            r.then_inc(s_sc.h, 1)
        block.scalar(_)
        s_sc.inc(1)
        sc_g = s_sc.n
        def _(v, sc_g=sc_g):
            v.wait_ge(s_sc.h, sc_g)
            v.reciprocal(out=stats[3], in_=stats[2])
            v.drain()
            r = v.tensor_tensor(out=stats[4], in0=stats[0], in1=stats[3], op=OP.mult)
            r.then_inc(s_ve.h, 1)
        block.vector(_)
        s_ve.inc(1)
        ve_g = s_ve.n
        def _(t, ve_g=ve_g):
            t.wait_ge(s_ve.h, ve_g)
            t.matmul(P[2][:, :], ones_row[:1, :], stats[3], start=True, stop=True)
            r = t.matmul(P[3][:, :], ones_row[:1, :], stats[4], start=True, stop=True)
            r.then_inc(s_mm.h, 1)
        block.tensor(_)
        s_mm.inc(1)
        mm_g = s_mm.n
        def _(v, mm_g=mm_g):
            v.wait_ge(s_mm.h, mm_g)
            for c in range(8):
                sl = slice(c * TT, (c + 1) * TT)
                v.tensor_tensor(out=scr[:, sl], in0=x_sb[:, c, :], in1=P[2][:, :], op=OP.mult)
                v.tensor_tensor(out=scr[:, sl], in0=scr[:, sl], in1=P[3][:, :], op=OP.subtract)
            for c in range(8):
                sl = slice(c * TT, (c + 1) * TT)
                v.tensor_copy(out=xnh[:, c, :], in_=scr[:, sl])
                r = v.tensor_tensor(out=xnl[:, c, :], in0=scr[:, sl], in1=xnh[:, c, :], op=OP.subtract)
            r.then_inc(s_ve.h, 1)
        block.vector(_)
        s_ve.inc(1)
        xn_ready = s_ve.n

        gate_mm_base = s_mm.n
        for c in range(8):
            mm_pre = s_mm.n
            def _(v, c=c, mm_pre=mm_pre, gate_mm_base=gate_mm_base):
                if c < 2:
                    v.wait_ge(s_mm.h, mm_pre)
                else:
                    v.wait_ge(s_mm.h, gate_mm_base + c - 1)
                sl = slice(c * TT, (c + 1) * TT)
                v.scalar_tensor_tensor(out=sgad[:, c % 2, :], in0=scr[:, sl], scalar=Mcol[:, :],
                                       in1=scr[:, sl], op0=OP.mult, op1=OP.add)
                r = v.scalar_tensor_tensor(out=sgad[:, c % 2, :], in0=scr[:, sl], scalar=negM[:, :],
                                           in1=sgad[:, c % 2, :], op0=OP.mult, op1=OP.add)
                r.then_inc(s_ve.h, 1)
            block.vector(_)
            s_ve.inc(1)
            ve_g = s_ve.n
            sc_pre = s_sc.n
            def _(t, c=c, ve_g=ve_g, sc_pre=sc_pre):
                t.wait_ge(s_ve.h, ve_g)
                if c == 0:
                    t.wait_ge(s_sc.h, sc_pre)
                r = t.matmul(P[4][:NE, :], gwt[:, c, :], sgad[:, c % 2, :],
                             start=(c == 0), stop=(c == 7))
                r.then_inc(s_mm.h, 1)
            block.tensor(_)
            s_mm.inc(1)
        mm_g = s_mm.n
        def _(s, mm_g=mm_g):
            s.wait_ge(s_mm.h, mm_g)
            r = s.activation(out=scores_em, in_=P[4][:NE, :], func=AF.Sigmoid)
            r.then_inc(s_sc.h, 1)
        block.scalar(_)
        s_sc.inc(1)

        sc_scores = s_sc.n
        for sub in range(4):
            sc_g = s_sc.n
            def _(t, sub=sub, sc_g=sc_g):
                t.wait_ge(s_sc.h, sc_scores)
                if sub > 0:
                    t.wait_ge(s_sc.h, sc_g)
                r = t.matmul(P[5][:, :NE], scores_em[:, sub * 128:(sub + 1) * 128], eye_sb[:NE, :NE], start=True, stop=True)
                r.then_inc(s_mm.h, 1)
            block.tensor(_)
            s_mm.inc(1)
            mm_g = s_mm.n
            def _(s, sub=sub, mm_g=mm_g):
                s.wait_ge(s_mm.h, mm_g)
                r = s.copy(out=stok[:, sub, :], in_=P[5][:, :NE])
                r.then_inc(s_sc.h, 1)
            block.scalar(_)
            s_sc.inc(1)
        sc_routing = s_sc.n
        xe_pre_comb = s_xe.n
        def _routing(v, sc_routing=sc_routing, xe_pre_comb=xe_pre_comb):
            v.wait_ge(s_sc.h, sc_routing)
            v.wait_ge(s_xe.h, xe_pre_comb)
            for sub in range(4):
                v.tensor_tensor(out=btok[:, sub, :], in0=stok[:, sub, :], in1=gb_bcast[:, :], op=OP.add)
            for sub in range(4):
                v.tensor_copy(out=rtok[:, sub, :], in_=btok[:, sub, :])
            for rnd in range(4):
                for sub in range(4):
                    v.tensor_reduce(out=thr8[:, sub:sub + 1], in_=rtok[:, sub, :], axis=AX.X, op=OP.max)
                if rnd == 1:
                    v.drain()
                if rnd < 3:
                    for sub in range(4):
                        v.tensor_scalar(out=mske[:, sub, :], in0=rtok[:, sub, :],
                                        scalar1=thr8[:, sub:sub + 1], scalar2=None, op0=OP.is_ge)
                    for sub in range(4):
                        v.scalar_tensor_tensor(out=rtok[:, sub, :], in0=mske[:, sub, :], scalar=-1e30,
                                               in1=rtok[:, sub, :], op0=OP.mult, op1=OP.add)
                else:
                    v.drain()
            for sub in range(4):
                v.tensor_scalar(out=mske[:, sub, :], in0=btok[:, sub, :],
                                scalar1=thr8[:, sub:sub + 1], scalar2=None, op0=OP.is_ge)
            for sub in range(4):
                v.tensor_tensor(out=rtok[:, sub, :], in0=stok[:, sub, :], in1=mske[:, sub, :], op=OP.mult)
            for sub in range(4):
                v.tensor_reduce(out=invd8[:, sub:sub + 1], in_=rtok[:, sub, :], axis=AX.X, op=OP.add)
            v.drain()
            v.reciprocal(out=invd8[:, :], in_=invd8[:, :])
            v.drain()
            for sub in range(4):
                v.tensor_scalar(out=btok[:, sub, :], in0=rtok[:, sub, :], scalar1=invd8[:, sub:sub + 1],
                                scalar2=2.5, op0=OP.mult, op1=OP.mult)
            v.drain()
            v.tensor_copy(out=rtok[:, :, :], in_=btok[:, :, :])
            v.drain()
            v.scalar_tensor_tensor(out=btok[:, :, :], in0=rtok[:, :, :], scalar=Mcol[:, :],
                                   in1=rtok[:, :, :], op0=OP.mult, op1=OP.add)
            v.drain()
            r = v.scalar_tensor_tensor(out=btok[:, :, :], in0=rtok[:, :, :], scalar=negM[:, :],
                                       in1=btok[:, :, :], op0=OP.mult, op1=OP.add)
            r.then_inc(s_ve.h, 1)
        block.vector(_routing)
        s_ve.inc(1)
        comb_ready_ve = s_ve.n

        def _(s, t0=t0, comb_ready_ve=comb_ready_ve):
            s.wait_ge(s_ve.h, comb_ready_ve)
            s.dma_start(out=comb_tok[t0:t0 + TT, :].rearrange("(s p) e -> p s e", p=128),
                        in_=btok[:, :, :]).then_inc(s_xe.h, 16)
        block.sync(_)
        s_xe.inc(16)

        for sub in range(4):
            sc_g = s_sc.n
            def _(t, sub=sub, sc_g=sc_g):
                t.wait_ge(s_ve.h, comb_ready_ve)
                if sub > 0:
                    t.wait_ge(s_sc.h, sc_g)
                r = t.matmul(P[6][:NE, :128], btok[:, sub, :], eye_sb[:, :], start=True, stop=True)
                r.then_inc(s_mm.h, 1)
            block.tensor(_)
            s_mm.inc(1)
            mm_g = s_mm.n
            def _(s, sub=sub, mm_g=mm_g):
                s.wait_ge(s_mm.h, mm_g)
                r = s.copy(out=comb_em[:, sub * 128:(sub + 1) * 128], in_=P[6][:NE, :128])
                r.then_inc(s_sc.h, 1)
            block.scalar(_)
            s_sc.inc(1)
        sc_g = s_sc.n
        def _(v, sc_g=sc_g):
            v.wait_ge(s_sc.h, sc_g)
            v.tensor_scalar(out=m01, in0=comb_em, scalar1=0.0, scalar2=None, op0=OP.is_gt)
            v.tensor_tensor_scan(out=incl_t, data0=ones16[:, :], data1=m01,
                                 initial=cnt_vec[:, :], op0=OP.mult, op1=OP.add)
            v.tensor_scalar(out=srel_t, in0=incl_t, scalar1=-1.0, scalar2=None, op0=OP.add)
            v.drain()
            v.tensor_copy(out=cnt_vec[:, :], in_=incl_t[:, TT - 1:TT])
            v.tensor_scalar(out=tmp_t, in0=srel_t, scalar1=float(CEXP), scalar2=None, op0=OP.is_lt)
            v.tensor_tensor(out=m01, in0=m01, in1=tmp_t, op=OP.mult)
            v.tensor_scalar(out=srel_t, in0=srel_t, scalar1=ebase_sb[:NE, :], scalar2=-BIGS,
                            op0=OP.add, op1=OP.add)
            v.tensor_tensor(out=srel_t, in0=srel_t, in1=m01, op=OP.mult)
            r = v.tensor_scalar(out=srel_t, in0=srel_t, scalar1=BIGS, scalar2=None, op0=OP.add)
            r.then_inc(s_ve.h, 1)
        block.vector(_)
        s_ve.inc(1)
        for sub in range(4):
            ve_g = s_ve.n
            sc_g = s_sc.n
            def _(t, sub=sub, ve_g=ve_g, sc_g=sc_g):
                t.wait_ge(s_ve.h, ve_g)
                if sub > 0:
                    t.wait_ge(s_sc.h, sc_g)
                r = t.matmul(P[7][:, :NE], srel_t[:, sub * 128:(sub + 1) * 128], eye_sb[:NE, :NE], start=True, stop=True)
                r.then_inc(s_mm.h, 1)
            block.tensor(_)
            s_mm.inc(1)
            mm_g = s_mm.n
            def _(s, sub=sub, mm_g=mm_g):
                s.wait_ge(s_mm.h, mm_g)
                r = s.copy(out=slots_f[:, sub, :], in_=P[7][:, :NE])
                r.then_inc(s_sc.h, 1)
            block.scalar(_)
            s_sc.inc(1)
        sc_g = s_sc.n
        isc_pre = s_isc.n
        def _(v, sc_g=sc_g, isc_pre=isc_pre):
            v.wait_ge(s_sc.h, sc_g)
            v.wait_ge(s_isc.h, isc_pre)
            r = v.tensor_copy(out=slots_u[:, :], in_=slots_f[:, :, :])
            r.then_inc(s_ve.h, 1)
        block.vector(_)
        s_ve.inc(1)
        ve_slots = s_ve.n
        isc_base = s_isc.n
        def _(g, t0=t0, ve_slots=ve_slots, isc_base=isc_base):
            g.wait_ge(s_ve.h, ve_slots)
            g.wait_ge(s_isc.h, isc_base)
            g.iota(ids_u[:, :], [[128, 4]], base=t0, channel_multiplier=1,
                   allow_small_or_imprecise_dtypes=True)
            n = 0
            for sub in range(4):
                for e in range(NE):
                    if n and n % 8 == 0:
                        g.wait_ge(s_isc.h, isc_base + 16 * (n - 8))
                    g.indirect_dma_start(
                        out=idx_tab,
                        out_offset=bass.IndirectOffsetOnAxis(ap=slots_u[:, sub * NE + e:sub * NE + e + 1], axis=0),
                        in_=ids_u[:, sub:sub + 1], in_offset=None,
                        bounds_check=regbox["slotbnd"], oob_is_err=False,
                    ).then_inc(s_isc.h, 16)
                    n += 1
        block.gpsimd(_)
        s_isc.inc(16 * 64)

        for sub in range(4):
            par = sub % 2
            xe_pre = s_xe.n
            for half in range(2):
                cs = range(half * 4, half * 4 + 4)
                sc_g = s_sc.n
                def _(t, sub=sub, half=half, cs=cs, sc_g=sc_g):
                    t.wait_ge(s_ve.h, xn_ready)
                    t.wait_ge(s_sc.h, sc_g)
                    for i, c in enumerate(cs):
                        t.matmul(P[5][:, i * 128:(i + 1) * 128], xnh[:, c, sub * 128:(sub + 1) * 128],
                                 eye_bf[:, :], start=(i == 0), stop=(i == 3))
                        r = t.matmul(P[6][:, i * 128:(i + 1) * 128], xnl[:, c, sub * 128:(sub + 1) * 128],
                                     eye_bf[:, :], start=(i == 0), stop=(i == 3))
                    r.then_inc(s_mm.h, 1)
                block.tensor(_)
                s_mm.inc(1)
                mm_g = s_mm.n
                def _(s, sub=sub, half=half, par=par, mm_g=mm_g, xe_pre=xe_pre):
                    s.wait_ge(s_mm.h, mm_g)
                    if half == 0:
                        s.wait_ge(s_xe.h, xe_pre)
                    s.copy(out=tth[par][:, half * 512:(half + 1) * 512], in_=P[5][:, :])
                    r = s.copy(out=ttl[par][:, half * 512:(half + 1) * 512], in_=P[6][:, :])
                    r.then_inc(s_sc.h, 1)
                block.scalar(_)
                s_sc.inc(1)
            sc_g = s_sc.n
            def _(s, sub=sub, par=par, t0=t0, sc_g=sc_g):
                s.wait_ge(s_sc.h, sc_g)
                r0 = t0 + sub * 128
                s.dma_start(out=xn_tok[r0:r0 + 128, 0:D], in_=tth[par][:, :]).then_inc(s_xe.h, 16)
                s.dma_start(out=xn_tok[r0:r0 + 128, D:2 * D], in_=ttl[par][:, :]).then_inc(s_xe.h, 16)
            block.sync(_)
            s_xe.inc(32)

        for pi in range(12):
            wa_gate = panel_conv_ve[pi]
            slot = pi % 2
            for blk in range(2):
                kblk = pi * 2 + blk
                sect = kblk // 8
                loc = kblk % 8
                pband = P[2 * (kblk % 2)]
                ve_g = s_ve.n
                sc_g = s_sc.n
                def _(t, pi=pi, blk=blk, slot=slot, wa_gate=wa_gate, pband=pband, ve_g=ve_g, sc_g=sc_g):
                    t.wait_ge(s_ve.h, max(wa_gate, ve_g))
                    t.wait_ge(s_sc.h, sc_g)
                    for c in range(8):
                        mm3(t, pband[:, :],
                            phl[slot][0][:, c, blk * 128:(blk + 1) * 128],
                            phl[slot][1][:, c, blk * 128:(blk + 1) * 128],
                            xnh[:, c, :], xnl[:, c, :],
                            first=(c == 0), last=(c == 7), inc=(c == 7))
                block.tensor(_)
                s_mm.inc(1)
                mm_g = s_mm.n
                if sect == 0:
                    def _(s, loc=loc, pband=pband, mm_g=mm_g):
                        s.wait_ge(s_mm.h, mm_g)
                        s.activation(out=scr[:, 0:TT], in_=pband[:, :], func=AF.Exp,
                                     bias=db_comb[:, loc:loc + 1])
                        s.add(scr[:, 0:TT], scr[:, 0:TT], 1.0)
                        s.activation(out=scr[:, 0:TT], in_=scr[:, 0:TT], func=AF.Ln)
                        r = s.activation(out=Ra[:, loc, :], in_=scr[:, 0:TT],
                                         func=AF.Exp, scale=alpha_n[:, loc:loc + 1])
                        r.then_inc(s_sc.h, 1)
                    block.scalar(_)
                    s_sc.inc(1)
                    sc_g = s_sc.n
                    def _(v, loc=loc, sc_g=sc_g):
                        v.wait_ge(s_sc.h, sc_g)
                        r = v.tensor_scalar(out=Ra[:, loc, :], in0=Ra[:, loc, :],
                                            scalar1=1.0 - EPS_A, scalar2=None, op0=OP.min)
                        r.then_inc(s_ve.h, 1)
                    block.vector(_)
                    s_ve.inc(1)
                else:
                    dst = Rb if sect == 1 else Rc
                    bias_t = bp_b if sect == 1 else bp_c
                    def _(s, loc=loc, pband=pband, dst=dst, bias_t=bias_t, mm_g=mm_g):
                        s.wait_ge(s_mm.h, mm_g)
                        r = s.activation(out=dst[:, loc, :], in_=pband[:, :],
                                         func=AF.Tanh, bias=bias_t[:, loc:loc + 1])
                        r.then_inc(s_sc.h, 1)
                    block.scalar(_)
                    s_sc.inc(1)
            if pi_loaded < 20:
                emit_panel(pi_loaded, s_mm.n)
                pi_loaded += 1

        pre_norm_ve = s_ve.n
        pre_norm_sc = s_sc.n
        for c in range(8):
            sc_g = s_sc.n
            def _(v, c=c, sc_g=sc_g):
                v.wait_ge(s_sc.h, sc_g)
                a_, b_, c_ = Ra[:, c, :], Rb[:, c, :], Rc[:, c, :]
                p_, r_, q_, d_ = scr[:, 512:1024], scr[:, 1024:1536], scr[:, 1536:2048], scr[:, 2048:2560]
                v.tensor_tensor(out=p_, in0=a_, in1=a_, op=OP.mult)
                v.tensor_tensor(out=q_, in0=c_, in1=c_, op=OP.mult)
                v.tensor_tensor(out=p_, in0=p_, in1=q_, op=OP.add)
                v.tensor_tensor(out=r_, in0=b_, in1=b_, op=OP.mult)
                v.tensor_tensor(out=q_, in0=a_, in1=b_, op=OP.mult)
                v.tensor_tensor(out=d_, in0=p_, in1=r_, op=OP.subtract)
                v.tensor_tensor(out=d_, in0=d_, in1=d_, op=OP.mult)
                v.tensor_tensor(out=q_, in0=q_, in1=q_, op=OP.mult)
                v.tensor_scalar(out=q_, in0=q_, scalar1=4.0, scalar2=None, op0=OP.mult)
                v.tensor_tensor(out=d_, in0=d_, in1=q_, op=OP.add)
                r2 = v.tensor_tensor(out=p_, in0=p_, in1=r_, op=OP.add)
                r2.then_inc(s_ve.h, 1)
            block.vector(_)
            s_ve.inc(1)
            ve_g = s_ve.n
            def _(s, ve_g=ve_g):
                s.wait_ge(s_ve.h, ve_g)
                r2 = s.activation(out=scr[:, 2560:3072], in_=scr[:, 2048:2560], func=AF.Sqrt,
                                  bias=cpool[:, 1:2])
                r2.then_inc(s_sc.h, 1)
            block.scalar(_)
            s_sc.inc(1)
            sc_g = s_sc.n
            def _(v, sc_g=sc_g):
                v.wait_ge(s_sc.h, sc_g)
                r2 = v.tensor_tensor(out=scr[:, 512:1024], in0=scr[:, 512:1024],
                                     in1=scr[:, 2560:3072], op=OP.add)
                r2.then_inc(s_ve.h, 1)
            block.vector(_)
            s_ve.inc(1)
            ve_g = s_ve.n
            def _(s, ve_g=ve_g):
                s.wait_ge(s_ve.h, ve_g)
                r2 = s.activation(out=scr[:, 512:1024], in_=scr[:, 512:1024], func=AF.Sqrt,
                                  scale=0.5, bias=cpool[:, 1:2])
                r2.then_inc(s_sc.h, 1)
            block.scalar(_)
            s_sc.inc(1)
            sc_g = s_sc.n
            def _(v, c=c, sc_g=sc_g):
                v.wait_ge(s_sc.h, sc_g)
                a_, b_, c_ = Ra[:, c, :], Rb[:, c, :], Rc[:, c, :]
                v.tensor_scalar(out=scr[:, 512:1024], in0=scr[:, 512:1024], scalar1=1.0,
                                scalar2=None, op0=OP.max)
                v.drain()
                v.reciprocal(out=scr[:, 1024:1536], in_=scr[:, 512:1024])
                v.drain()
                v.tensor_tensor(out=a_, in0=a_, in1=scr[:, 1024:1536], op=OP.mult)
                v.tensor_tensor(out=b_, in0=b_, in1=scr[:, 1024:1536], op=OP.mult)
                r2 = v.tensor_tensor(out=c_, in0=c_, in1=scr[:, 1024:1536], op=OP.mult)
                r2.then_inc(s_ve.h, 1)
            block.vector(_)
            s_ve.inc(1)

        rbmult_ve = {}
        for pp in range(4):
            pi = 12 + pp
            slot = pi % 2
            wa_gate = panel_conv_ve[pi]
            for blk in range(2):
                nblk = pp * 2 + blk
                pband = P[4 + (nblk % 2)]
                ve_w = rbmult_ve[nblk - 2] if nblk >= 2 else pre_norm_ve
                def _(t, pi=pi, blk=blk, slot=slot, wa_gate=wa_gate, pband=pband, ve_w=ve_w,
                      pre_norm_sc=pre_norm_sc):
                    t.wait_ge(s_ve.h, max(wa_gate, ve_w))
                    t.wait_ge(s_sc.h, pre_norm_sc)
                    for c in range(8):
                        mm3(t, pband[:, :],
                            phl[slot][0][:, c, blk * 128:(blk + 1) * 128],
                            phl[slot][1][:, c, blk * 128:(blk + 1) * 128],
                            xnh[:, c, :], xnl[:, c, :],
                            first=(c == 0), last=(c == 7), inc=(c == 7))
                block.tensor(_)
                s_mm.inc(1)
                mm_g = s_mm.n
                def _(v, nblk=nblk, pband=pband, mm_g=mm_g):
                    v.wait_ge(s_mm.h, mm_g)
                    r = v.tensor_tensor(out=Rb[:, nblk, :], in0=Rb[:, nblk, :],
                                        in1=pband[:, :], op=OP.mult)
                    r.then_inc(s_ve.h, 1)
                block.vector(_)
                s_ve.inc(1)
                rbmult_ve[nblk] = s_ve.n
            if pi_loaded < 20:
                emit_panel(pi_loaded, s_mm.n)
                pi_loaded += 1

        for c in range(8):
            def _(v, c=c):
                v.tensor_tensor_scan(out=scr[:, 0:TT], data0=Ra[:, c, :], data1=Rb[:, c, :],
                                     initial=carry[:, c:c + 1], op0=OP.mult, op1=OP.add)
                v.tensor_tensor(out=Ra[:, c, 0:1], in0=Rc[:, c, 0:1], in1=carry[:, c:c + 1], op=OP.mult)
                v.tensor_tensor(out=Ra[:, c, 1:], in0=Rc[:, c, 1:], in1=scr[:, 0:TT - 1], op=OP.mult)
                v.tensor_copy(out=carry[:, c:c + 1], in_=scr[:, TT - 1:TT])
                v.tensor_copy(out=xnh[:, c, :], in_=Ra[:, c, :])
                r = v.tensor_tensor(out=xnl[:, c, :], in0=Ra[:, c, :], in1=xnh[:, c, :], op=OP.subtract)
                r.then_inc(s_ve.h, 1)
            block.vector(_)
            s_ve.inc(1)

        for pp in range(4):
            pi = 16 + pp
            slot = pi % 2
            wa_gate = panel_conv_ve[pi]
            for blk in range(2):
                oblk = pp * 2 + blk
                pband = P[4 + (oblk % 2)]
                ve_g = s_ve.n
                def _(t, pi=pi, blk=blk, slot=slot, wa_gate=wa_gate, pband=pband, ve_g=ve_g):
                    t.wait_ge(s_ve.h, max(wa_gate, ve_g))
                    for c in range(8):
                        mm3(t, pband[:, :],
                            phl[slot][0][:, c, blk * 128:(blk + 1) * 128],
                            phl[slot][1][:, c, blk * 128:(blk + 1) * 128],
                            xnh[:, c, :], xnl[:, c, :],
                            first=(c == 0), last=(c == 7), inc=(c == 7))
                block.tensor(_)
                s_mm.inc(1)
                mm_g = s_mm.n
                def _(v, oblk=oblk, pband=pband, mm_g=mm_g):
                    v.wait_ge(s_mm.h, mm_g)
                    r = v.scalar_tensor_tensor(out=x_sb[:, oblk, :], in0=pband[:, :],
                                               scalar=attn_scale[:, :], in1=x_sb[:, oblk, :],
                                               op0=OP.mult, op1=OP.add)
                    r.then_inc(s_ve.h, 1)
                block.vector(_)
                s_ve.inc(1)
            if pi_loaded < 20:
                emit_panel(pi_loaded, s_mm.n)
                pi_loaded += 1

        for sub in range(4):
            par = sub % 2
            ot_pre = s_otok.n
            for half in range(2):
                cs = range(half * 4, half * 4 + 4)
                ve_g = s_ve.n
                sc_g = s_sc.n
                def _(t, sub=sub, half=half, cs=cs, ve_g=ve_g, sc_g=sc_g):
                    t.wait_ge(s_ve.h, ve_g)
                    t.wait_ge(s_sc.h, sc_g)
                    for i, c in enumerate(cs):
                        r = t.matmul(P[6][:, i * 128:(i + 1) * 128], x_sb[:, c, sub * 128:(sub + 1) * 128],
                                     eye_sb[:, :], start=(i == 0), stop=(i == 3))
                    r.then_inc(s_mm.h, 1)
                block.tensor(_)
                s_mm.inc(1)
                mm_g = s_mm.n
                def _(s, sub=sub, half=half, par=par, mm_g=mm_g, ot_pre=ot_pre):
                    s.wait_ge(s_mm.h, mm_g)
                    if half == 0:
                        s.wait_ge(s_otok.h, ot_pre)
                    r = s.copy(out=otok[par][:, half * 512:(half + 1) * 512], in_=P[6][:, :])
                    r.then_inc(s_sc.h, 1)
                block.scalar(_)
                s_sc.inc(1)
            sc_g = s_sc.n
            def _(g, sub=sub, par=par, t0=t0, sc_g=sc_g):
                g.wait_ge(s_sc.h, sc_g)
                r0 = t0 + sub * 128
                g.dma_start(out=out_tok[r0:r0 + 128, :], in_=otok[par][:, :]).then_inc(s_otok.h, 16)
            block.gpsimd(_)
            s_otok.inc(16)

    esB.close()

    esC = ExitStack()
    land13 = esC.enter_context(nc.sbuf_tensor("land13", [128, 8, 1024], F32))
    land2 = esC.enter_context(nc.sbuf_tensor("land2", [128, 4, 1024], F32))
    w13h = esC.enter_context(nc.sbuf_tensor("w13h", [128, 8, 1024], BF16))
    w13l = esC.enter_context(nc.sbuf_tensor("w13l", [128, 8, 1024], BF16))
    w2h = esC.enter_context(nc.sbuf_tensor("w2h", [128, 4, 1024], BF16))
    w2l = esC.enter_context(nc.sbuf_tensor("w2l", [128, 4, 1024], BF16))
    xg = esC.enter_context(nc.sbuf_tensor("xg", [128, 8, 2 * D], BF16))
    cg = esC.enter_context(nc.sbuf_tensor("cg", [128, 8, NE], F32))
    cekeep = esC.enter_context(nc.sbuf_tensor("cekeep", [128, 32], F32))
    xfh = esC.enter_context(nc.sbuf_tensor("xfh", [128, 8, 512], BF16))
    xfl = esC.enter_context(nc.sbuf_tensor("xfl", [128, 8, 512], BF16))
    s_sb = esC.enter_context(nc.sbuf_tensor("s_sb", [128, 4, 512], F32))
    hh = esC.enter_context(nc.sbuf_tensor("hh", [128, 4, 512], BF16))
    hl = esC.enter_context(nc.sbuf_tensor("hl", [128, 4, 512], BF16))
    y_fm = esC.enter_context(nc.sbuf_tensor("y_fm", [128, 8, 512], F32))
    y_tok = [esC.enter_context(nc.sbuf_tensor(f"y_tok{i}", [128, 1024], F32)) for i in range(4)]
    idx_sb = esC.enter_context(nc.sbuf_tensor("idx_sb", [128, 64], U32))
    PC = [esC.enter_context(nc.psum_tensor(f"PC{i}", [128, 512], F32)) for i in range(8)]
    blockC = esC.enter_context(nc.Block())
    block = blockC

    xe_total = s_xe.n
    isc_total = s_isc.n
    otok_total = s_otok.n

    def _(g):
        g.memset(xg[:, :, :], 0.0)
        g.memset(cg[:, :, :], 0.0)
        g.wait_ge(s_xe.h, xe_total)
        g.wait_ge(s_isc.h, isc_total)
        g.wait_ge(s_otok.h, otok_total)
    block.gpsimd(_)

    prev_yout_ve = s_ve.n
    prev_w2_mm = s_mm.n
    prev_silu_sc = s_sc.n
    prev_yfm_ve = s_ve.n
    last_xfl_ve = s_ve.n
    pending = {}
    next_gat = {}

    def emit_loads(te):
        sh = (te == NE)
        ep = te % 2
        gat_pre = [gslot[j].n for j in range(8)]
        acc_pre_e = s_acc.n
        ve_conv_pre = s_ve.n
        def _(s, te=te, sh=sh, ep=ep, gat_pre=gat_pre, ve_conv_pre=ve_conv_pre, acc_pre_e=acc_pre_e):
            s.wait_ge(s_ve.h, ve_conv_pre)
            if sh:
                s.dma_start(out=land13[:, :, :],
                            in_=w13sT.rearrange("(c p) f -> p c f", p=128)).then_inc(s_we.h, 16)
                s.dma_start(out=land2[:, :, :],
                            in_=w2sT.rearrange("(c p) d2 -> p c d2", p=128)).then_inc(s_we.h, 16)
            else:
                s.dma_start(out=land13[:, :, :],
                            in_=w13T[te].rearrange("(c p) f -> p c f", p=128)).then_inc(s_we.h, 16)
                s.dma_start(out=land2[:, :, :],
                            in_=w2T[te].rearrange("(c p) d2 -> p c d2", p=128)).then_inc(s_we.h, 16)
                for j in range(8):
                    s.wait_ge(gslot[j].h, gat_pre[j])
                s.wait_ge(s_isc.h, isc_total)
                s.wait_ge(s_acc.h, acc_pre_e)
                s.dma_start(out=idx_sb[:, ep * 32:ep * 32 + 16],
                            in_=idx_tab[te * CEXP:(te + 1) * CEXP, :].rearrange("(p k) x -> p (k x)", p=128)
                            ).then_inc(s_we.h, 16)
        block.sync(_)
        s_we.inc(32 if sh else 48)
        wr = s_we.n
        return wr, (wr if sh else wr - 16), (0 if sh else wr)

    pending[0] = emit_loads(0)
    for e in range(NE + 1):
        shared = (e == NE)
        NK = 32 if shared else NE
        epar = e % 2
        acc_expert_base = s_acc.n
        add_cnt = {}
        we_ready, we_w_ready, idx_ready = pending[e]
        conv13_c = []
        conv2_c = []
        for c in range(8):
            def _(v, c=c, we_w_ready=we_w_ready, prev_w2_mm=prev_w2_mm):
                if c == 0:
                    v.wait_ge(s_we.h, we_w_ready)
                    v.wait_ge(s_mm.h, prev_w2_mm)
                v.tensor_copy(out=w13h[:, c, :], in_=land13[:, c, :])
                r = v.tensor_tensor(out=w13l[:, c, :], in0=land13[:, c, :], in1=w13h[:, c, :], op=OP.subtract)
                r.then_inc(s_ve.h, 1)
            block.vector(_)
            s_ve.inc(1)
            conv13_c.append(s_ve.n)
        for c in range(4):
            def _(v, c=c):
                v.tensor_copy(out=w2h[:, c, :], in_=land2[:, c, :])
                r = v.tensor_tensor(out=w2l[:, c, :], in0=land2[:, c, :], in1=w2h[:, c, :], op=OP.subtract)
                r.then_inc(s_ve.h, 1)
            block.vector(_)
            s_ve.inc(1)
            conv2_c.append(s_ve.n)
        w_ready = conv13_c[0]
        if e + 1 <= NE:
            pending[e + 1] = emit_loads(e + 1)

        trans_mm_cnt = {}
        ce_ve_cnt = {}

        def emit_gather(te, k, mm_pre, ve_pre_g, we_pre):
            sh = (te == NE)
            ep = te % 2
            kk8 = k % 8
            def _(g, te=te, sh=sh, ep=ep, k=k, kk8=kk8, mm_pre=mm_pre, ve_pre_g=ve_pre_g, we_pre=we_pre):
                g.wait_ge(s_mm.h, mm_pre)
                g.wait_ge(s_ve.h, ve_pre_g)
                if we_pre:
                    g.wait_ge(s_we.h, we_pre)
                if sh:
                    g.dma_start(out=xg[:, kk8, :], in_=xn_tok[k * 128:(k + 1) * 128, :]
                                ).then_inc(gslot[kk8].h, 16)
                else:
                    g.indirect_dma_start(
                        out=xg[:, kk8, :], out_offset=None,
                        in_=xn_tok,
                        in_offset=bass.IndirectOffsetOnAxis(ap=idx_sb[:, ep * 32 + k:ep * 32 + k + 1], axis=0),
                        bounds_check=regbox["tokbnd"], oob_is_err=False,
                    ).then_inc(gslot[kk8].h, 16)
                    g.indirect_dma_start(
                        out=cg[:, kk8, :], out_offset=None,
                        in_=comb_tok,
                        in_offset=bass.IndirectOffsetOnAxis(ap=idx_sb[:, ep * 32 + k:ep * 32 + k + 1], axis=0),
                        bounds_check=regbox["tokbnd"], oob_is_err=False,
                    ).then_inc(gslot[kk8].h, 16)
            block.gpsimd(_)
            gslot[kk8].inc(16 if sh else 32)
            return gslot[kk8].n

        def emit_cecopy(k, e=e, shared=shared):
            if shared:
                ce_ve_cnt[k] = s_ve.n
                return
            kk8 = k % 8
            def _(v, k=k, kk8=kk8, gat_g=gat_cnt[k]):
                v.wait_ge(gslot[kk8].h, gat_g)
                r = v.tensor_copy(out=cekeep[:, k:k + 1], in_=cg[:, kk8, e:e + 1])
                r.then_inc(s_ve.h, 1)
            block.vector(_)
            s_ve.inc(1)
            ce_ve_cnt[k] = s_ve.n

        if e == 0:
            gat_cnt = {}
            for k in range(8):
                gat_cnt[k] = emit_gather(0, k, 0, 0, idx_ready)
        else:
            gat_cnt = dict(next_gat)
        next_gat = {}
        for k in range(min(8, NK)):
            emit_cecopy(k)

        for grp in range(NK // 4):
            for kk in range(4):
                k = grp * 4 + kk
                kk8 = k % 8
                gat_g = gat_cnt[k]
                for half in range(2):
                    cs = range(half * 4, half * 4 + 4)
                    sc_g = s_sc.n
                    ve_tr = prev_yfm_ve if (kk == 0 and half == 0) else last_xfl_ve
                    def _(t, k=k, kk8=kk8, half=half, cs=cs, gat_g=gat_g, sc_g=sc_g, ve_tr=ve_tr):
                        t.wait_ge(gslot[kk8].h, gat_g)
                        t.wait_ge(s_sc.h, sc_g)
                        t.wait_ge(s_ve.h, ve_tr)
                        for i, c in enumerate(cs):
                            t.matmul(PC[6][:, i * 128:(i + 1) * 128], xg[:, kk8, c * 128:(c + 1) * 128],
                                     eye_bf[:, :], start=(i == 0), stop=(i == 3))
                            r = t.matmul(PC[7][:, i * 128:(i + 1) * 128], xg[:, kk8, D + c * 128:D + (c + 1) * 128],
                                         eye_bf[:, :], start=(i == 0), stop=(i == 3))
                        r.then_inc(s_mm.h, 1)
                    block.tensor(_)
                    s_mm.inc(1)
                    mm_g = s_mm.n
                    def _(s, kk=kk, half=half, mm_g=mm_g):
                        s.wait_ge(s_mm.h, mm_g)
                        r = s.copy(out=xfh[:, half * 4:(half + 1) * 4, kk * 128:(kk + 1) * 128], in_=PC[6][:, :])
                        r.then_inc(s_sc.h, 1)
                    block.scalar(_)
                    s_sc.inc(1)
                    def _(v, kk=kk, half=half, mm_g=mm_g):
                        v.wait_ge(s_mm.h, mm_g)
                        r = v.tensor_copy(out=xfl[:, half * 4:(half + 1) * 4, kk * 128:(kk + 1) * 128], in_=PC[7][:, :])
                        r.then_inc(s_ve.h, 1)
                    block.vector(_)
                    s_ve.inc(1)
                    last_xfl_ve = s_ve.n
                trans_mm_cnt[k] = s_mm.n
                if k + 8 < NK:
                    gat_cnt[k + 8] = emit_gather(e, k + 8, trans_mm_cnt[k], ce_ve_cnt[k], 0)
                elif e + 1 <= NE:
                    next_gat[k - 8] = emit_gather(e + 1, k - 8, trans_mm_cnt[k], ce_ve_cnt[k],
                                                  pending[e + 1][2])
            xf_ready_sc = s_sc.n
            xf_ready_ve = s_ve.n

            def _(t, xf_ready_sc=xf_ready_sc, xf_ready_ve=xf_ready_ve, conv13_c=conv13_c,
                  prev_silu_sc=prev_silu_sc, prev_yfm_ve=prev_yfm_ve):
                t.wait_ge(s_sc.h, max(xf_ready_sc, prev_silu_sc))
                t.wait_ge(s_ve.h, max(xf_ready_ve, prev_yfm_ve))
                for c in range(8):
                    t.wait_ge(s_ve.h, conv13_c[c])
                    for fb in range(4):
                        mm3(t, PC[fb][:, :],
                            w13h[:, c, fb * 128:(fb + 1) * 128], w13l[:, c, fb * 128:(fb + 1) * 128],
                            xfh[:, c, :], xfl[:, c, :], first=(c == 0), last=(c == 7), inc=(c == 7))
            block.tensor(_)
            s_mm.inc(4)
            mm_gate = s_mm.n
            def _(s, mm_gate=mm_gate):
                for fb in range(4):
                    s.wait_ge(s_mm.h, mm_gate - 3 + fb)
                    r = s.activation(out=s_sb[:, fb, :], in_=PC[fb][:, :], func=AF.Silu)
                r.then_inc(s_sc.h, 1)
            block.scalar(_)
            s_sc.inc(1)
            prev_silu_sc = s_sc.n
            def _(t, prev_yout_ve=prev_yout_ve):
                t.wait_ge(s_ve.h, prev_yout_ve)
                for c in range(8):
                    for fb in range(4):
                        mm3(t, PC[4 + fb][:, :],
                            w13h[:, c, 512 + fb * 128:512 + (fb + 1) * 128],
                            w13l[:, c, 512 + fb * 128:512 + (fb + 1) * 128],
                            xfh[:, c, :], xfl[:, c, :], first=(c == 0), last=(c == 7), inc=(c == 7))
            block.tensor(_)
            s_mm.inc(4)
            mm_up = s_mm.n
            sc_silu = s_sc.n
            def _(v, mm_up=mm_up, sc_silu=sc_silu):
                v.wait_ge(s_sc.h, sc_silu)
                for fb in range(4):
                    v.wait_ge(s_mm.h, mm_up - 3 + fb)
                    v.tensor_tensor(out=s_sb[:, fb, :], in0=s_sb[:, fb, :], in1=PC[4 + fb][:, :], op=OP.mult)
                for fb in range(4):
                    v.tensor_copy(out=hh[:, fb, :], in_=s_sb[:, fb, :])
                    r = v.tensor_tensor(out=hl[:, fb, :], in0=s_sb[:, fb, :], in1=hh[:, fb, :], op=OP.subtract)
                r.then_inc(s_ve.h, 1)
            block.vector(_)
            s_ve.inc(1)
            h_ready = s_ve.n
            def _(t, h_ready=h_ready, conv2_c=conv2_c):
                t.wait_ge(s_ve.h, h_ready)
                for fb in range(4):
                    t.wait_ge(s_ve.h, conv2_c[fb])
                    for db in range(8):
                        mm3(t, PC[db][:, :],
                            w2h[:, fb, db * 128:(db + 1) * 128], w2l[:, fb, db * 128:(db + 1) * 128],
                            hh[:, fb, :], hl[:, fb, :], first=(fb == 0), last=(fb == 3), inc=(fb == 3))
            block.tensor(_)
            s_mm.inc(8)
            mm_y = s_mm.n
            prev_w2_mm = mm_y
            def _(v, mm_y=mm_y):
                for db in range(8):
                    v.wait_ge(s_mm.h, mm_y - 7 + db)
                    r = v.tensor_copy(out=y_fm[:, db, :], in_=PC[db][:, :])
                r.then_inc(s_ve.h, 1)
            block.vector(_)
            s_ve.inc(1)
            yfm_ready = s_ve.n
            prev_yfm_ve = yfm_ready

            for kk in range(4):
                k = grp * 4 + kk
                ypar = k % 4
                acc_pre = add_cnt.get(k - 4, acc_expert_base)
                for half in range(2):
                    dbs = range(half * 4, half * 4 + 4)
                    ve_g = s_ve.n
                    def _(t, kk=kk, half=half, dbs=dbs, yfm_ready=yfm_ready, ve_g=ve_g):
                        t.wait_ge(s_ve.h, max(yfm_ready, ve_g))
                        for i, db in enumerate(dbs):
                            r = t.matmul(PC[4 + half][:, i * 128:(i + 1) * 128], y_fm[:, db, kk * 128:(kk + 1) * 128],
                                         eye_sb[:, :], start=(i == 0), stop=(i == 3))
                        r.then_inc(s_mm.h, 1)
                    block.tensor(_)
                    s_mm.inc(1)
                    mm_g = s_mm.n
                    def _(v, k=k, kk=kk, half=half, ypar=ypar, mm_g=mm_g, shared=shared, acc_pre=acc_pre):
                        v.wait_ge(s_mm.h, mm_g)
                        if half == 0:
                            v.wait_ge(s_acc.h, acc_pre)
                        if shared:
                            r = v.tensor_copy(out=y_tok[ypar][:, half * 512:(half + 1) * 512],
                                              in_=PC[4 + half][:, :])
                        else:
                            r = v.tensor_scalar(out=y_tok[ypar][:, half * 512:(half + 1) * 512],
                                                in0=PC[4 + half][:, :],
                                                scalar1=cekeep[:, k:k + 1], scalar2=None, op0=OP.mult)
                        r.then_inc(s_ve.h, 1)
                    block.vector(_)
                    s_ve.inc(1)
                ve_g = s_ve.n
                def _(g, e=e, k=k, ypar=ypar, epar=epar, shared=shared, ve_g=ve_g,
                      acc_expert_base=acc_expert_base, first=(k == 0)):
                    g.wait_ge(s_ve.h, ve_g)
                    if first:
                        g.wait_ge(s_acc.h, acc_expert_base)
                    if shared:
                        g.dma_start(out=out_tok[k * 128:(k + 1) * 128, :], in_=y_tok[ypar][:, :],
                                    accum_op=OP.add).then_inc(s_acc.h, 16)
                    else:
                        g.indirect_dma_start(
                            out=out_tok,
                            out_offset=bass.IndirectOffsetOnAxis(ap=idx_sb[:, epar * 32 + k:epar * 32 + k + 1], axis=0),
                            in_=y_tok[ypar][:, :], in_offset=None,
                            compute_op=OP.add,
                        ).then_inc(s_acc.h, 16)
                block.gpsimd(_)
                s_acc.inc(16)
                add_cnt[k] = s_acc.n
            prev_yout_ve = s_ve.n

            for kk in range(4):
                k2 = (grp + 2) * 4 + kk
                if k2 < NK:
                    emit_cecopy(k2)

    acc_total = s_acc.n
    def _(g, acc_total=acc_total):
        g.wait_ge(s_acc.h, acc_total)
    block.gpsimd(_)
    esC.close()

    es0.close()
    return nc


_NC_CACHE = None


def kernel(L0_attn_u, L0_attn_W_in, L0_attn_W_out, L0_attn_Wp, L0_attn_bp,
           L0_attn_alpha_log, L0_attn_delta_bias, L0_attn_log_gamma,
           L0_ffn_input_ids, L0_ffn_gate_w, L0_ffn_gate_b, L0_ffn_w13, L0_ffn_w2,
           L0_ffn_w13_s, L0_ffn_w2_s):
    global _NC_CACHE
    u = np.asarray(L0_attn_u, dtype=np.float32)

    c = np.ascontiguousarray
    w_inT = c(np.asarray(L0_attn_W_in, np.float32).T)
    w_outT = c(np.asarray(L0_attn_W_out, np.float32).T)
    wpT = c(np.asarray(L0_attn_Wp, np.float32).T)
    gate_wT = c(np.asarray(L0_ffn_gate_w, np.float32).T)
    gate_b = c(np.asarray(L0_ffn_gate_b, np.float32).reshape(NE, 1))
    w13T = c(np.asarray(L0_ffn_w13, np.float32).transpose(0, 2, 1))
    w2T = c(np.asarray(L0_ffn_w2, np.float32).transpose(0, 2, 1))
    w13sT = c(np.asarray(L0_ffn_w13_s, np.float32).T)
    w2sT = c(np.asarray(L0_ffn_w2_s, np.float32).T)
    alpha_log = c(np.asarray(L0_attn_alpha_log, np.float32).reshape(8, 128).T)
    delta_bias = c(np.asarray(L0_attn_delta_bias, np.float32).reshape(8, 128).T)
    bp = c(np.asarray(L0_attn_bp, np.float32).reshape(3, 8, 128).transpose(2, 0, 1).reshape(128, 24))
    log_gamma = c(np.asarray(L0_attn_log_gamma, np.float32).reshape(1, 1))
    eye = np.eye(128, dtype=np.float32)
    ebase = (np.arange(NE, dtype=np.float32) * CEXP).reshape(NE, 1)

    shared = dict(w_inT=w_inT, w_outT=w_outT, wpT=wpT, gate_wT=gate_wT, gate_b=gate_b,
                  w13T=w13T, w2T=w2T, w13sT=w13sT, w2sT=w2sT, alpha_log=alpha_log,
                  delta_bias=delta_bias, bp=bp, log_gamma=log_gamma, eye=eye, ebase=ebase)

    in_maps = []
    for b in range(N_CORES):
        m = dict(shared)
        m["xT"] = c(u[b].T)
        in_maps.append(m)

    if _NC_CACHE is None:
        _NC_CACHE = build_program()
    nc = _NC_CACHE

    res = run_bass_kernel_spmd(nc, in_maps, core_ids=list(range(N_CORES)))
    out = np.stack([np.asarray(res.results[b]["out_tok"])[:TC] for b in range(N_CORES)], axis=0)
    return out


# version 18 (final)
# speedup vs baseline: 1.1303x; 1.0565x over previous
import numpy as np
from contextlib import ExitStack

import concourse.bass as bass
import concourse.mybir as mybir
from concourse.bass_utils import run_bass_kernel_spmd

F32 = mybir.dt.float32
BF16 = mybir.dt.bfloat16
AF = mybir.ActivationFunctionType
OP = mybir.AluOpType
AX = mybir.AxisListType

N_CORES = 8
D = 1024
TC = 4096
TT = 1024
NTILE = TC // TT
NG = TT // 512
NE = 16
FI = 512
NSQ = 14
NPW = 6
SIGMA_STAGE = 99
DEBUG_SIGMA = False
EPS_A = 1e-4
LN_EPS = 1e-5


class Sem:
    def __init__(self, es, nc, name):
        self.h = es.enter_context(nc.semaphore(name))
        self.n = 0

    def inc(self, k=1):
        self.n += k
        return self.n


def emit_sigma(nc, block, es, wT, scale_out, consts, sems, tag):
    s_mm, s_ve, s_sc, s_ld = sems["mm"], sems["ve"], sems["sc"], sems["ld"]
    ones_col, ones_row, ones_bf = consts["ones_col"], consts["ones_row"], consts["ones_bf"]

    wt = es.enter_context(nc.sbuf_tensor(f"sg_wt{tag}", [128, 8, 1024], F32))
    Bf = es.enter_context(nc.sbuf_tensor(f"sg_Bf{tag}", [128, 8, 1024], F32))
    M0 = es.enter_context(nc.sbuf_tensor(f"sg_M0{tag}", [128, 8, 1024], BF16))
    M1 = es.enter_context(nc.sbuf_tensor(f"sg_M1{tag}", [128, 8, 1024], BF16))
    junk = es.enter_context(nc.sbuf_tensor(f"sg_junk{tag}", [128, 512], BF16))
    acc = es.enter_context(nc.sbuf_tensor(f"sg_acc{tag}", [128, 16], F32))
    r1 = es.enter_context(nc.sbuf_tensor(f"sg_r1{tag}", [128, 1], F32))
    inv1 = es.enter_context(nc.sbuf_tensor(f"sg_inv1{tag}", [128, 1], F32))
    ssc = es.enter_context(nc.sbuf_tensor(f"sg_ssc{tag}", [1, 1], F32))
    vv = es.enter_context(nc.sbuf_tensor(f"sg_vv{tag}", [128, 8], F32))
    tt = es.enter_context(nc.sbuf_tensor(f"sg_tt{tag}", [128, 8], F32))
    prod = es.enter_context(nc.sbuf_tensor(f"sg_prod{tag}", [128, 8], F32))
    den_sb = es.enter_context(nc.sbuf_tensor(f"sg_den{tag}", [128, 8], F32))
    n2 = es.enter_context(nc.sbuf_tensor(f"sg_n2{tag}", [1, 2], F32))
    ps0 = es.enter_context(nc.psum_tensor(f"sg_ps0{tag}", [128, 512], F32))
    ps1 = es.enter_context(nc.psum_tensor(f"sg_ps1{tag}", [128, 512], F32))
    pv = es.enter_context(nc.psum_tensor(f"sg_pv{tag}", [128, 1], F32))
    pv2 = es.enter_context(nc.psum_tensor(f"sg_pv2{tag}", [128, 1], F32))

    ld_gate = s_ld.n + 16
    ve_prev = s_ve.n
    sc_prev = s_sc.n
    mm_prev = s_mm.n

    def _(s):
        s.wait_ge(s_ve.h, ve_prev)
        s.wait_ge(s_sc.h, sc_prev)
        s.wait_ge(s_mm.h, mm_prev)
        s.dma_start(out=wt[:, :, :], in_=wT.rearrange("(c p) n -> p c n", p=128)).then_inc(s_ld.h, 16)
    block.sync(_)
    s_ld.inc(16)

    def sq_pass(src, dst, is_b_build):
        mm_base = s_mm.n
        ve_base = s_ve.n
        sc_base = s_sc.n

        def _t(t):
            if is_b_build:
                t.wait_ge(s_ld.h, ld_gate)
            t.wait_ge(s_ve.h, ve_base)
            t.wait_ge(s_sc.h, sc_base)
            for k in range(16):
                iblk, jg = k // 2, k % 2
                ps = ps0 if jg == 0 else ps1
                if k >= 2:
                    t.wait_ge(s_ve.h, ve_base + k - 1)
                    t.wait_ge(s_sc.h, sc_base + k - 1)
                for d in range(8):
                    r = t.matmul(ps[:, :], src[:, d, iblk * 128:(iblk + 1) * 128],
                                 src[:, d, jg * 512:(jg + 1) * 512],
                                 start=(d == 0), stop=(d == 7))
                r.then_inc(s_mm.h, 1)
        block.tensor(_t)
        s_mm.inc(16)

        def _v(v):
            for k in range(16):
                iblk, jg = k // 2, k % 2
                ps = ps0 if jg == 0 else ps1
                v.wait_ge(s_mm.h, mm_base + k + 1)
                tgt = Bf if is_b_build else dst
                r = v.tensor_copy(out=tgt[:, iblk, jg * 512:(jg + 1) * 512], in_=ps[:, :])
                r.then_inc(s_ve.h, 1)
        block.vector(_v)
        s_ve.inc(16)

        def _s(s):
            for k in range(16):
                iblk, jg = k // 2, k % 2
                s.wait_ge(s_ve.h, ve_base + k + 1)
                src_sb = (Bf if is_b_build else dst)[:, iblk, jg * 512:(jg + 1) * 512]
                if is_b_build:
                    r = s.activation(out=M0[:, iblk, jg * 512:(jg + 1) * 512], in_=src_sb,
                                     func=AF.Copy, bias=0.0)
                else:
                    r = s.activation(out=junk[:, :], in_=src_sb, func=AF.Abs,
                                     accum_out=acc[:, k:k + 1])
                r.then_inc(s_sc.h, 1)
        block.scalar(_s)
        s_sc.inc(16)

    sq_pass(wt, None, True)

    if SIGMA_STAGE < 2:
        return
    bufs = [M0, M1]
    for it in range(NSQ):
        src, dst = bufs[it % 2], bufs[(it + 1) % 2]
        sq_pass(src, dst, False)

        sc_gate = s_sc.n
        def _v2(v, sc_gate=sc_gate):
            v.wait_ge(s_sc.h, sc_gate)
            r = v.tensor_reduce(out=r1[:, :], in_=acc[:, :], axis=AX.X, op=OP.add)
            r.then_inc(s_ve.h, 1)
        block.vector(_v2)
        s_ve.inc(1)
        ve_g = s_ve.n
        def _t2(t, ve_g=ve_g):
            t.wait_ge(s_ve.h, ve_g)
            r = t.matmul(pv[:1, :], ones_col[:, :], r1[:, :], start=True, stop=True)
            r.then_inc(s_mm.h, 1)
        block.tensor(_t2)
        s_mm.inc(1)
        mm_g = s_mm.n
        def _v3(v, mm_g=mm_g):
            v.wait_ge(s_mm.h, mm_g)
            r = v.tensor_copy(out=ssc[:, :], in_=pv[:1, :])
            r.then_inc(s_ve.h, 1)
        block.vector(_v3)
        s_ve.inc(1)
        ve_g2 = s_ve.n
        def _t3(t, ve_g2=ve_g2):
            t.wait_ge(s_ve.h, ve_g2)
            r = t.matmul(pv2[:, :], ones_row[:1, :], ssc[:1, :], start=True, stop=True)
            r.then_inc(s_mm.h, 1)
        block.tensor(_t3)
        s_mm.inc(1)
        mm_g2 = s_mm.n
        def _v4(v, dst=dst, mm_g2=mm_g2):
            v.wait_ge(s_mm.h, mm_g2)
            v.reciprocal(out=inv1[:, :], in_=pv2[:, :])
            v.drain()
            for c in range(8):
                r = v.tensor_scalar(out=dst[:, c, :], in0=dst[:, c, :],
                                    scalar1=inv1[:, :], scalar2=256.0,
                                    op0=OP.mult, op1=OP.mult)
            r.then_inc(s_ve.h, 1)
        block.vector(_v4)
        s_ve.inc(1)

    Mfin = bufs[NSQ % 2]
    if SIGMA_STAGE < 3:
        return

    for pblk in range(8):
        gate = s_ve.n
        def _t(t, pblk=pblk, gate=gate):
            t.wait_ge(s_ve.h, gate)
            for q in range(8):
                r = t.matmul(pv[:, :], Mfin[:, q, pblk * 128:(pblk + 1) * 128],
                             ones_bf[:, :], start=(q == 0), stop=(q == 7))
            r.then_inc(s_mm.h, 1)
        block.tensor(_t)
        s_mm.inc(1)
        mm_g = s_mm.n
        def _v(v, pblk=pblk, mm_g=mm_g):
            v.wait_ge(s_mm.h, mm_g)
            r = v.tensor_copy(out=vv[:, pblk:pblk + 1], in_=pv[:, :])
            r.then_inc(s_ve.h, 1)
        block.vector(_v)
        s_ve.inc(1)

    if SIGMA_STAGE < 4:
        return
    vbuf, tbuf = vv, tt
    for step in range(NPW + 1):
        for pblk in range(8):
            gate = s_ve.n
            def _t(t, pblk=pblk, vbuf=vbuf, gate=gate):
                t.wait_ge(s_ve.h, gate)
                for q in range(8):
                    r = t.matmul(pv[:, :], Bf[:, q, pblk * 128:(pblk + 1) * 128],
                                 vbuf[:, q:q + 1], start=(q == 0), stop=(q == 7))
                r.then_inc(s_mm.h, 1)
            block.tensor(_t)
            s_mm.inc(1)
            mm_g = s_mm.n
            def _v(v, pblk=pblk, tbuf=tbuf, mm_g=mm_g):
                v.wait_ge(s_mm.h, mm_g)
                r = v.tensor_copy(out=tbuf[:, pblk:pblk + 1], in_=pv[:, :])
                r.then_inc(s_ve.h, 1)
            block.vector(_v)
            s_ve.inc(1)
        vbuf, tbuf = tbuf, vbuf
    vlast, tlast = tbuf, vbuf

    if SIGMA_STAGE < 5:
        return
    def _v5(v):
        v.tensor_tensor(out=prod[:, :], in0=vlast[:, :], in1=tlast[:, :], op=OP.mult)
        r = v.tensor_tensor(out=den_sb[:, :], in0=vlast[:, :], in1=vlast[:, :], op=OP.mult)
        r.then_inc(s_ve.h, 1)
    block.vector(_v5)
    s_ve.inc(1)

    ve_g = s_ve.n
    def _t4(t, ve_g=ve_g):
        t.wait_ge(s_ve.h, ve_g)
        for q in range(8):
            t.matmul(pv[:1, :], ones_col[:, :], prod[:, q:q + 1], start=(q == 0), stop=(q == 7))
        for q in range(8):
            r = t.matmul(pv2[:1, :], ones_col[:, :], den_sb[:, q:q + 1], start=(q == 0), stop=(q == 7))
        r.then_inc(s_mm.h, 1)
    block.tensor(_t4)
    s_mm.inc(1)

    mm_g = s_mm.n
    def _v6(v, mm_g=mm_g):
        v.wait_ge(s_mm.h, mm_g)
        v.tensor_copy(out=n2[:, 0:1], in_=pv[:1, :])
        v.tensor_copy(out=n2[:, 1:2], in_=pv2[:1, :])
        v.drain()
        v.reciprocal(out=n2[:, 1:2], in_=n2[:, 1:2])
        v.drain()
        r = v.tensor_tensor(out=scale_out[:, :], in0=n2[:, 0:1], in1=n2[:, 1:2], op=OP.mult)
        r.then_inc(s_ve.h, 1)
    block.vector(_v6)
    s_ve.inc(1)

    ve_g = s_ve.n
    def _s2(s, ve_g=ve_g):
        s.wait_ge(s_ve.h, ve_g)
        r = s.activation(out=scale_out[:, :], in_=scale_out[:, :], func=AF.Sqrt)
        r.then_inc(s_sc.h, 1)
    block.scalar(_s2)
    s_sc.inc(1)
    sc_g = s_sc.n
    def _v7(v, sc_g=sc_g):
        v.wait_ge(s_sc.h, sc_g)
        r = v.tensor_scalar(out=scale_out[:, :], in0=scale_out[:, :], scalar1=1.0,
                            scalar2=None, op0=OP.max)
        r.then_inc(s_ve.h, 1)
    block.vector(_v7)
    s_ve.inc(1)


U32 = mybir.dt.uint32
TT = 512
NTILE_B = TC // TT
CEXP = 2048
NDUMP = 2048
DEBUG_DUMPS = False
TOKP = TC + NDUMP
BIGS = float(2 ** 24)


def build_program():
    nc = bass.Bass("TRN2", target_bir_lowering=False, detect_race_conditions=False,
                   dynamic_dma_scratch_size=16384)

    xT = nc.dram_tensor("xT", [D, TC], F32, kind="ExternalInput").ap()
    out_tok = nc.dram_tensor("out_tok", [TOKP, D], F32, kind="ExternalOutput").ap()
    _dk = dict(kind="ExternalOutput") if DEBUG_DUMPS else {}
    xn_tok = nc.dram_tensor("xn_tok", [TC, 2 * D], BF16, **_dk).ap()
    comb_tok = nc.dram_tensor("comb_tok", [TC, NE], F32, **_dk).ap()
    idx_tab = nc.dram_tensor("idx_tab", [NE * CEXP, 1], U32, **_dk).ap()
    w_inT = nc.dram_tensor("w_inT", [D, D], F32, kind="ExternalInput").ap()
    w_outT = nc.dram_tensor("w_outT", [D, D], F32, kind="ExternalInput").ap()
    wpT = nc.dram_tensor("wpT", [D, 3 * D], F32, kind="ExternalInput").ap()
    gate_wT = nc.dram_tensor("gate_wT", [D, NE], F32, kind="ExternalInput").ap()
    gate_b = nc.dram_tensor("gate_b", [NE, 1], F32, kind="ExternalInput").ap()
    w13T = nc.dram_tensor("w13T", [NE, D, 2 * FI], F32, kind="ExternalInput").ap()
    w2T = nc.dram_tensor("w2T", [NE, FI, D], F32, kind="ExternalInput").ap()
    w13sT = nc.dram_tensor("w13sT", [D, 2 * FI], F32, kind="ExternalInput").ap()
    w2sT = nc.dram_tensor("w2sT", [FI, D], F32, kind="ExternalInput").ap()
    alpha_log = nc.dram_tensor("alpha_log", [128, 8], F32, kind="ExternalInput").ap()
    delta_bias = nc.dram_tensor("delta_bias", [128, 8], F32, kind="ExternalInput").ap()
    bp = nc.dram_tensor("bp", [128, 24], F32, kind="ExternalInput").ap()
    log_gamma = nc.dram_tensor("log_gamma", [1, 1], F32, kind="ExternalInput").ap()
    eye = nc.dram_tensor("eye", [128, 128], F32, kind="ExternalInput").ap()
    ebase = nc.dram_tensor("ebase", [NE, 1], F32, kind="ExternalInput").ap()

    es0 = ExitStack()
    eye_sb = es0.enter_context(nc.sbuf_tensor("eye_sb", [128, 128], F32))
    eye_bf = es0.enter_context(nc.sbuf_tensor("eye_bf", [128, 128], BF16))
    ones_col = es0.enter_context(nc.sbuf_tensor("ones_col", [128, 1], F32))
    ones_row = es0.enter_context(nc.sbuf_tensor("ones_row", [1, 128], F32))
    ones_bf = es0.enter_context(nc.sbuf_tensor("ones_bf", [128, 1], BF16))
    ones16 = es0.enter_context(nc.sbuf_tensor("ones16", [NE, TT], F32))
    scale1 = es0.enter_context(nc.sbuf_tensor("scale1", [1, 1], F32))
    scale2 = es0.enter_context(nc.sbuf_tensor("scale2", [1, 1], F32))
    attn_scale = es0.enter_context(nc.sbuf_tensor("attn_scale", [128, 1], F32))
    lg_sb = es0.enter_context(nc.sbuf_tensor("lg_sb", [1, 1], F32))
    alpha_n = es0.enter_context(nc.sbuf_tensor("alpha_n", [128, 8], F32))
    db_comb = es0.enter_context(nc.sbuf_tensor("db_comb", [128, 8], F32))
    bp_b = es0.enter_context(nc.sbuf_tensor("bp_b", [128, 8], F32))
    bp_c = es0.enter_context(nc.sbuf_tensor("bp_c", [128, 8], F32))
    gate_b_sb = es0.enter_context(nc.sbuf_tensor("gate_b_sb", [NE, 1], F32))
    gb_bcast = es0.enter_context(nc.sbuf_tensor("gb_bcast", [128, NE], F32))
    ebase_sb = es0.enter_context(nc.sbuf_tensor("ebase_sb", [NE, 1], F32))
    gwt = es0.enter_context(nc.sbuf_tensor("gwt", [128, 8, NE], F32))
    carry = es0.enter_context(nc.sbuf_tensor("carry", [128, 8], F32))
    cnt_vec = es0.enter_context(nc.sbuf_tensor("cnt_vec", [NE, 1], F32))
    tmp8 = es0.enter_context(nc.sbuf_tensor("tmp8", [128, 8], F32))
    cpool = es0.enter_context(nc.sbuf_tensor("cpool", [128, 4], F32))
    Mcol = es0.enter_context(nc.sbuf_tensor("Mcol", [128, 1], F32))
    negM = es0.enter_context(nc.sbuf_tensor("negM", [128, 1], F32))

    sems = {}
    for name in ["ld", "mm", "ve", "sc", "xin", "otok", "isc", "gat", "acc", "wa", "we", "xe",
                 "gs0", "gs1", "gs2", "gs3", "gs4", "gs5", "gs6", "gs7"]:
        sems[name] = Sem(es0, nc, "s_" + name)
    s_ld, s_mm, s_ve, s_sc = sems["ld"], sems["mm"], sems["ve"], sems["sc"]
    s_xin, s_otok, s_isc = sems["xin"], sems["otok"], sems["isc"]
    s_gat, s_acc, s_wa, s_we, s_xe = sems["gat"], sems["acc"], sems["wa"], sems["we"], sems["xe"]
    gslot = [sems[f"gs{j}"] for j in range(8)]

    consts = {"ones_col": ones_col, "ones_row": ones_row, "ones_bf": ones_bf}

    with ExitStack() as esA:
        ztile = esA.enter_context(nc.sbuf_tensor("ztile", [128, 1024], F32))
        idxfill = esA.enter_context(nc.sbuf_tensor("idxfill", [128, 256], U32))
        block = esA.enter_context(nc.Block())

        def _(s):
            s.dma_start(out=eye_sb[:, :], in_=eye).then_inc(s_ld.h, 16)
            s.dma_start(out=lg_sb[:, :], in_=log_gamma).then_inc(s_ld.h, 16)
            s.dma_start(out=alpha_n[:, :], in_=alpha_log).then_inc(s_ld.h, 16)
            s.dma_start(out=db_comb[:, :], in_=delta_bias).then_inc(s_ld.h, 16)
            s.dma_start(out=tmp8[:, :], in_=bp[:, 0:8]).then_inc(s_ld.h, 16)
            s.dma_start(out=bp_b[:, :], in_=bp[:, 8:16]).then_inc(s_ld.h, 16)
            s.dma_start(out=bp_c[:, :], in_=bp[:, 16:24]).then_inc(s_ld.h, 16)
            s.dma_start(out=gate_b_sb[:, :], in_=gate_b).then_inc(s_ld.h, 16)
            s.dma_start(out=gwt[:, :, :], in_=gate_wT.rearrange("(c p) e -> p c e", p=128)).then_inc(s_ld.h, 16)
            s.dma_start(out=ebase_sb[:, :], in_=ebase).then_inc(s_ld.h, 16)
        block.sync(_)
        s_ld.inc(16 * 10)

        ld_g = s_ld.n
        def _(v, ld_g=ld_g):
            v.wait_ge(s_ld.h, ld_g)
            v.memset(ones_col[:, :], 1.0)
            v.memset(ones_row[:, :], 1.0)
            v.memset(ones_bf[:, :], 1.0)
            v.memset(ones16[:, :], 1.0)
            v.memset(carry[:, :], 0.0)
            v.memset(cnt_vec[:, :], 0.0)
            v.memset(ztile[:, :], 0.0)
            v.memset(cpool[:, 0:1], 100000.0)
            v.drain()
            v.reciprocal(out=cpool[:, 0:1], in_=cpool[:, 0:1])
            v.drain()
            v.memset(cpool[:, 1:2], 1e-12)
            v.memset(cpool[:, 2:3], -1.0)
            v.memset(cpool[:, 3:4], 1.0)
            v.memset(Mcol[:, :], 1.0)
            v.drain()
            v.tensor_scalar(out=Mcol[:, :], in0=Mcol[:, :], scalar1=float(2.0 ** -29),
                            scalar2=1.0, op0=OP.add, op1=OP.subtract)
            v.drain()
            v.tensor_scalar(out=Mcol[:, :], in0=Mcol[:, :], scalar1=float(2.0 ** 58),
                            scalar2=None, op0=OP.mult)
            v.drain()
            v.tensor_scalar(out=negM[:, :], in0=Mcol[:, :], scalar1=-1.0, scalar2=None, op0=OP.mult)
            v.drain()
            r = v.tensor_copy(out=eye_bf[:, :], in_=eye_sb[:, :])
            r.then_inc(s_ve.h, 1)
        block.vector(_)
        s_ve.inc(1)
        def _(s, ld_g=ld_g):
            s.wait_ge(s_ld.h, ld_g)
            s.activation(out=alpha_n[:, :], in_=alpha_n[:, :], func=AF.Exp)
            s.add(alpha_n[:, :], alpha_n[:, :], 1.0)
            s.activation(out=alpha_n[:, :], in_=alpha_n[:, :], func=AF.Ln)
            r = s.mul(alpha_n[:, :], alpha_n[:, :], -1.0)
            r.then_inc(s_sc.h, 1)
        block.scalar(_)
        s_sc.inc(1)
        def _(v):
            r = v.tensor_tensor(out=db_comb[:, :], in0=db_comb[:, :], in1=tmp8[:, :], op=OP.add)
            r.then_inc(s_ve.h, 1)
        block.vector(_)
        s_ve.inc(1)

        ve_g = s_ve.n
        regbox = {}
        def _(g, ve_g=ve_g):
            regbox["slotbnd"] = g.alloc_register("slotbnd")
            g.reg_mov(regbox["slotbnd"], NE * CEXP - 1)
            regbox["tokbnd"] = g.alloc_register("tokbnd")
            g.reg_mov(regbox["tokbnd"], TC - 1)
            g.wait_ge(s_ve.h, ve_g)
            for r0 in range(TC, TOKP, 128):
                g.dma_start(out=out_tok[r0:r0 + 128, :], in_=ztile[:, :]).then_inc(s_otok.h, 16)
            g.iota(idxfill[:, :], [[1, 16], [0, 16]], base=TC, channel_multiplier=16, allow_small_or_imprecise_dtypes=True)
            g.dma_start(out=idx_tab.rearrange("(p c) x -> p (c x)", p=128),
                        in_=idxfill[:, :]).then_inc(s_isc.h, 16)
        block.gpsimd(_)
        s_otok.inc(16 * (NDUMP // 128))
        s_isc.inc(16)

        with ExitStack() as esS:
            emit_sigma(nc, block, esS, w_inT, scale1, consts, sems, "a")
        with ExitStack() as esS:
            emit_sigma(nc, block, esS, w_outT, scale2, consts, sems, "b")

        with ExitStack() as esS:
            pvb = esS.enter_context(nc.psum_tensor("pvb", [128, 1], F32))
            n2gb = esS.enter_context(nc.sbuf_tensor("n2gb", [1, NE], F32))
            pgb = esS.enter_context(nc.psum_tensor("pgb", [128, NE], F32))
            ve_g = s_ve.n
            def _(s, ve_g=ve_g):
                s.wait_ge(s_ve.h, ve_g)
                r = s.activation(out=lg_sb[:, :], in_=lg_sb[:, :], func=AF.Exp)
                r.then_inc(s_sc.h, 1)
            block.scalar(_)
            s_sc.inc(1)
            sc_g = s_sc.n
            def _(v, sc_g=sc_g):
                v.wait_ge(s_sc.h, sc_g)
                v.tensor_tensor(out=scale1[:, :], in0=scale1[:, :], in1=scale2[:, :], op=OP.mult)
                v.drain()
                v.reciprocal(out=scale1[:, :], in_=scale1[:, :])
                v.drain()
                r = v.tensor_tensor(out=scale1[:, :], in0=scale1[:, :], in1=lg_sb[:, :], op=OP.mult)
                r.then_inc(s_ve.h, 1)
            block.vector(_)
            s_ve.inc(1)
            ve_g = s_ve.n
            def _(t, ve_g=ve_g):
                t.wait_ge(s_ve.h, ve_g)
                t.matmul(pvb[:, :], ones_row[:1, :], scale1[:1, :], start=True, stop=True).then_inc(s_mm.h, 1)
                r = t.matmul(pgb[:1, :NE], gate_b_sb[:, :], eye_sb[:NE, :NE], start=True, stop=True)
                r.then_inc(s_mm.h, 1)
            block.tensor(_)
            s_mm.inc(2)
            mm_g = s_mm.n
            def _(v, mm_g=mm_g):
                v.wait_ge(s_mm.h, mm_g)
                v.tensor_copy(out=attn_scale[:, :], in_=pvb[:, :])
                r = v.tensor_copy(out=n2gb[:1, :], in_=pgb[:1, :NE])
                r.then_inc(s_ve.h, 1)
            block.vector(_)
            s_ve.inc(1)
            ve_g = s_ve.n
            def _(t, ve_g=ve_g):
                t.wait_ge(s_ve.h, ve_g)
                r = t.matmul(pgb[:, :], ones_row[:1, :], n2gb[:1, :], start=True, stop=True)
                r.then_inc(s_mm.h, 1)
            block.tensor(_)
            s_mm.inc(1)
            mm_g = s_mm.n
            def _(v, mm_g=mm_g):
                v.wait_ge(s_mm.h, mm_g)
                r = v.tensor_copy(out=gb_bcast[:, :], in_=pgb[:, :])
                r.then_inc(s_ve.h, 1)
            block.vector(_)
            s_ve.inc(1)

    esB = ExitStack()
    x_sb = esB.enter_context(nc.sbuf_tensor("x_sb", [128, 8, TT], F32))
    xnh = esB.enter_context(nc.sbuf_tensor("xnh", [128, 8, TT], BF16))
    xnl = esB.enter_context(nc.sbuf_tensor("xnl", [128, 8, TT], BF16))
    Ra = esB.enter_context(nc.sbuf_tensor("Ra", [128, 8, TT], F32))
    Rb = esB.enter_context(nc.sbuf_tensor("Rb", [128, 8, TT], F32))
    Rc = esB.enter_context(nc.sbuf_tensor("Rc", [128, 8, TT], F32))
    scr = esB.enter_context(nc.sbuf_tensor("scr", [128, 8 * TT], F32))
    sgad = esB.enter_context(nc.sbuf_tensor("sgad", [128, 2, TT], F32))
    pland = [esB.enter_context(nc.sbuf_tensor(f"pland{i}", [128, 8, 256], F32)) for i in range(2)]
    phl = [(esB.enter_context(nc.sbuf_tensor(f"ph{i}", [128, 8, 256], BF16)),
            esB.enter_context(nc.sbuf_tensor(f"pl{i}", [128, 8, 256], BF16))) for i in range(2)]
    stok = esB.enter_context(nc.sbuf_tensor("stok", [128, 4, NE], F32))
    btok = esB.enter_context(nc.sbuf_tensor("btok", [128, 4, NE], F32))
    rtok = esB.enter_context(nc.sbuf_tensor("rtok", [128, 4, NE], F32))
    thr8 = esB.enter_context(nc.sbuf_tensor("thr8", [128, 4], F32))
    invd8 = esB.enter_context(nc.sbuf_tensor("invd8", [128, 4], F32))
    mske = esB.enter_context(nc.sbuf_tensor("mske", [128, 4, NE], F32))
    slots_f = esB.enter_context(nc.sbuf_tensor("slots_f", [128, 4, NE], F32))
    slots_u = esB.enter_context(nc.sbuf_tensor("slots_u", [128, 4 * NE], U32))
    ids_u = esB.enter_context(nc.sbuf_tensor("ids_u", [128, 4], U32))
    tth = [esB.enter_context(nc.sbuf_tensor(f"tth{i}", [128, 1024], BF16)) for i in range(2)]
    ttl = [esB.enter_context(nc.sbuf_tensor(f"ttl{i}", [128, 1024], BF16)) for i in range(2)]
    otok = [esB.enter_context(nc.sbuf_tensor(f"otok{i}", [128, 1024], F32)) for i in range(2)]
    P = [esB.enter_context(nc.psum_tensor(f"P{i}", [128, 512], F32)) for i in range(8)]
    blockB = esB.enter_context(nc.Block())
    block = blockB

    stats = [Rc[:1, 0, 0:512], Rc[:1, 1, 0:512], Rc[:1, 2, 0:512], Rc[:1, 3, 0:512], Rc[:1, 4, 0:512]]
    scores_em = Rc[0:NE, 5, :]
    comb_em = Rc[0:NE, 6, :]
    m01 = Rc[0:NE, 7, :]
    incl_t = Rc[0:NE, 0, :]
    srel_t = Rc[0:NE, 1, :]
    tmp_t = Rc[0:NE, 2, :]

    def mm3(t, ps, wh, wl, xh, xl, first, last, inc=None):
        t.matmul(ps, wh, xh, start=first, stop=False)
        t.matmul(ps, wh, xl, start=False, stop=False)
        r = t.matmul(ps, wl, xh, start=False, stop=last)
        if inc:
            r.then_inc(s_mm.h, 1)
        return r

    for j in range(NTILE_B):
        t0 = j * TT

        mm_pre = s_mm.n
        def _(g, t0=t0, mm_pre=mm_pre):
            g.wait_ge(s_mm.h, mm_pre)
            g.dma_start(out=x_sb[:, :, :],
                        in_=xT[:, t0:t0 + TT].rearrange("(c p) t -> p c t", p=128)).then_inc(s_xin.h, 16)
        block.gpsimd(_)
        s_xin.inc(16)

        panels = ([wpT[:, p * 256:(p + 1) * 256] for p in range(12)]
                  + [w_inT[:, p * 256:(p + 1) * 256] for p in range(4)]
                  + [w_outT[:, p * 256:(p + 1) * 256] for p in range(4)])
        wa_base = s_wa.n
        panel_conv_ve = [None] * 20

        def emit_panel(pi, gate_mm):
            slot = pi % 2
            def _(s, pi=pi, gate_mm=gate_mm, slot=slot):
                if gate_mm is not None:
                    s.wait_ge(s_mm.h, gate_mm)
                s.wait_ge(s_ve.h, s_ve.n)
                s.dma_start(out=pland[slot][:, :, :],
                            in_=panels[pi].rearrange("(c p) n -> p c n", p=128)).then_inc(s_wa.h, 16)
            block.sync(_)
            s_wa.inc(16)
            wa_g = s_wa.n
            def _(v, pi=pi, slot=slot, wa_g=wa_g):
                v.wait_ge(s_wa.h, wa_g)
                v.tensor_copy(out=phl[slot][0][:, :, :], in_=pland[slot][:, :, :])
                r = v.tensor_tensor(out=phl[slot][1][:, :, :], in0=pland[slot][:, :, :],
                                    in1=phl[slot][0][:, :, :], op=OP.subtract)
                r.then_inc(s_ve.h, 1)
            block.vector(_)
            s_ve.inc(1)
            panel_conv_ve[pi] = s_ve.n

        emit_panel(0, s_mm.n)
        emit_panel(1, s_mm.n)
        pi_loaded = 2

        xin_gate = s_xin.n
        ve_pre = s_ve.n
        def _(s, xin_gate=xin_gate, ve_pre=ve_pre):
            s.wait_ge(s_xin.h, xin_gate)
            s.wait_ge(s_ve.h, ve_pre)
            for c in range(8):
                r = s.activation(out=scr[:, c * TT:(c + 1) * TT], in_=x_sb[:, c, :], func=AF.Square)
            r.then_inc(s_sc.h, 1)
        block.scalar(_)
        s_sc.inc(1)
        sc_sq = s_sc.n
        ve_pre = s_ve.n
        def _(t, sc_sq=sc_sq, ve_pre=ve_pre, xin_gate=xin_gate):
            t.wait_ge(s_xin.h, xin_gate)
            t.wait_ge(s_sc.h, sc_sq)
            t.wait_ge(s_ve.h, ve_pre)
            for c in range(8):
                t.matmul(P[0][:1, :], ones_col[:, :], x_sb[:, c, :], start=(c == 0), stop=(c == 7))
            for c in range(8):
                r = t.matmul(P[1][:1, :], ones_col[:, :], scr[:, c * TT:(c + 1) * TT],
                             start=(c == 0), stop=(c == 7))
            r.then_inc(s_mm.h, 1)
        block.tensor(_)
        s_mm.inc(1)
        mm_g = s_mm.n
        def _(v, mm_g=mm_g):
            v.wait_ge(s_mm.h, mm_g)
            v.tensor_scalar(out=stats[0], in0=P[0][:1, :], scalar1=1.0 / D, scalar2=None, op0=OP.mult)
            v.tensor_scalar(out=stats[1], in0=P[1][:1, :], scalar1=1.0 / D, scalar2=None, op0=OP.mult)
            v.drain()
            v.tensor_tensor(out=stats[2], in0=stats[0], in1=stats[0], op=OP.mult)
            v.drain()
            r = v.tensor_tensor(out=stats[1], in0=stats[1], in1=stats[2], op=OP.subtract)
            r.then_inc(s_ve.h, 1)
        block.vector(_)
        s_ve.inc(1)
        ve_g = s_ve.n
        def _(s, ve_g=ve_g):
            s.wait_ge(s_ve.h, ve_g)
            r = s.activation(out=stats[2], in_=stats[1], func=AF.Sqrt, bias=cpool[:1, 0:1])
            r.then_inc(s_sc.h, 1)
        block.scalar(_)
        s_sc.inc(1)
        sc_g = s_sc.n
        def _(v, sc_g=sc_g):
            v.wait_ge(s_sc.h, sc_g)
            v.reciprocal(out=stats[3], in_=stats[2])
            v.drain()
            r = v.tensor_tensor(out=stats[4], in0=stats[0], in1=stats[3], op=OP.mult)
            r.then_inc(s_ve.h, 1)
        block.vector(_)
        s_ve.inc(1)
        ve_g = s_ve.n
        def _(t, ve_g=ve_g):
            t.wait_ge(s_ve.h, ve_g)
            t.matmul(P[2][:, :], ones_row[:1, :], stats[3], start=True, stop=True)
            r = t.matmul(P[3][:, :], ones_row[:1, :], stats[4], start=True, stop=True)
            r.then_inc(s_mm.h, 1)
        block.tensor(_)
        s_mm.inc(1)
        mm_g = s_mm.n
        def _(v, mm_g=mm_g):
            v.wait_ge(s_mm.h, mm_g)
            for c in range(8):
                sl = slice(c * TT, (c + 1) * TT)
                v.tensor_tensor(out=scr[:, sl], in0=x_sb[:, c, :], in1=P[2][:, :], op=OP.mult)
                v.tensor_tensor(out=scr[:, sl], in0=scr[:, sl], in1=P[3][:, :], op=OP.subtract)
            for c in range(8):
                sl = slice(c * TT, (c + 1) * TT)
                v.tensor_copy(out=xnh[:, c, :], in_=scr[:, sl])
                r = v.tensor_tensor(out=xnl[:, c, :], in0=scr[:, sl], in1=xnh[:, c, :], op=OP.subtract)
            r.then_inc(s_ve.h, 1)
        block.vector(_)
        s_ve.inc(1)
        xn_ready = s_ve.n

        gate_mm_base = s_mm.n
        for c in range(8):
            mm_pre = s_mm.n
            def _(v, c=c, mm_pre=mm_pre, gate_mm_base=gate_mm_base):
                if c < 2:
                    v.wait_ge(s_mm.h, mm_pre)
                else:
                    v.wait_ge(s_mm.h, gate_mm_base + c - 1)
                sl = slice(c * TT, (c + 1) * TT)
                v.scalar_tensor_tensor(out=sgad[:, c % 2, :], in0=scr[:, sl], scalar=Mcol[:, :],
                                       in1=scr[:, sl], op0=OP.mult, op1=OP.add)
                r = v.scalar_tensor_tensor(out=sgad[:, c % 2, :], in0=scr[:, sl], scalar=negM[:, :],
                                           in1=sgad[:, c % 2, :], op0=OP.mult, op1=OP.add)
                r.then_inc(s_ve.h, 1)
            block.vector(_)
            s_ve.inc(1)
            ve_g = s_ve.n
            sc_pre = s_sc.n
            def _(t, c=c, ve_g=ve_g, sc_pre=sc_pre):
                t.wait_ge(s_ve.h, ve_g)
                if c == 0:
                    t.wait_ge(s_sc.h, sc_pre)
                r = t.matmul(P[4][:NE, :], gwt[:, c, :], sgad[:, c % 2, :],
                             start=(c == 0), stop=(c == 7))
                r.then_inc(s_mm.h, 1)
            block.tensor(_)
            s_mm.inc(1)
        mm_g = s_mm.n
        def _(s, mm_g=mm_g):
            s.wait_ge(s_mm.h, mm_g)
            r = s.activation(out=scores_em, in_=P[4][:NE, :], func=AF.Sigmoid)
            r.then_inc(s_sc.h, 1)
        block.scalar(_)
        s_sc.inc(1)

        sc_scores = s_sc.n
        for sub in range(4):
            sc_g = s_sc.n
            def _(t, sub=sub, sc_g=sc_g):
                t.wait_ge(s_sc.h, sc_scores)
                if sub > 0:
                    t.wait_ge(s_sc.h, sc_g)
                r = t.matmul(P[5][:, :NE], scores_em[:, sub * 128:(sub + 1) * 128], eye_sb[:NE, :NE], start=True, stop=True)
                r.then_inc(s_mm.h, 1)
            block.tensor(_)
            s_mm.inc(1)
            mm_g = s_mm.n
            def _(s, sub=sub, mm_g=mm_g):
                s.wait_ge(s_mm.h, mm_g)
                r = s.copy(out=stok[:, sub, :], in_=P[5][:, :NE])
                r.then_inc(s_sc.h, 1)
            block.scalar(_)
            s_sc.inc(1)
        sc_routing = s_sc.n
        xe_pre_comb = s_xe.n
        def _routing(v, sc_routing=sc_routing, xe_pre_comb=xe_pre_comb):
            v.wait_ge(s_sc.h, sc_routing)
            v.wait_ge(s_xe.h, xe_pre_comb)
            for sub in range(4):
                v.tensor_tensor(out=btok[:, sub, :], in0=stok[:, sub, :], in1=gb_bcast[:, :], op=OP.add)
            for sub in range(4):
                v.tensor_copy(out=rtok[:, sub, :], in_=btok[:, sub, :])
            for rnd in range(4):
                for sub in range(4):
                    v.tensor_reduce(out=thr8[:, sub:sub + 1], in_=rtok[:, sub, :], axis=AX.X, op=OP.max)
                if rnd == 1:
                    v.drain()
                if rnd < 3:
                    for sub in range(4):
                        v.tensor_scalar(out=mske[:, sub, :], in0=rtok[:, sub, :],
                                        scalar1=thr8[:, sub:sub + 1], scalar2=None, op0=OP.is_ge)
                    for sub in range(4):
                        v.scalar_tensor_tensor(out=rtok[:, sub, :], in0=mske[:, sub, :], scalar=-1e30,
                                               in1=rtok[:, sub, :], op0=OP.mult, op1=OP.add)
                else:
                    v.drain()
            for sub in range(4):
                v.tensor_scalar(out=mske[:, sub, :], in0=btok[:, sub, :],
                                scalar1=thr8[:, sub:sub + 1], scalar2=None, op0=OP.is_ge)
            for sub in range(4):
                v.tensor_tensor(out=rtok[:, sub, :], in0=stok[:, sub, :], in1=mske[:, sub, :], op=OP.mult)
            for sub in range(4):
                v.tensor_reduce(out=invd8[:, sub:sub + 1], in_=rtok[:, sub, :], axis=AX.X, op=OP.add)
            v.drain()
            v.reciprocal(out=invd8[:, :], in_=invd8[:, :])
            v.drain()
            for sub in range(4):
                v.tensor_scalar(out=btok[:, sub, :], in0=rtok[:, sub, :], scalar1=invd8[:, sub:sub + 1],
                                scalar2=2.5, op0=OP.mult, op1=OP.mult)
            v.drain()
            v.tensor_copy(out=rtok[:, :, :], in_=btok[:, :, :])
            v.drain()
            v.scalar_tensor_tensor(out=btok[:, :, :], in0=rtok[:, :, :], scalar=Mcol[:, :],
                                   in1=rtok[:, :, :], op0=OP.mult, op1=OP.add)
            v.drain()
            r = v.scalar_tensor_tensor(out=btok[:, :, :], in0=rtok[:, :, :], scalar=negM[:, :],
                                       in1=btok[:, :, :], op0=OP.mult, op1=OP.add)
            r.then_inc(s_ve.h, 1)
        block.vector(_routing)
        s_ve.inc(1)
        comb_ready_ve = s_ve.n

        def _(s, t0=t0, comb_ready_ve=comb_ready_ve):
            s.wait_ge(s_ve.h, comb_ready_ve)
            s.dma_start(out=comb_tok[t0:t0 + TT, :].rearrange("(s p) e -> p s e", p=128),
                        in_=btok[:, :, :]).then_inc(s_xe.h, 16)
        block.sync(_)
        s_xe.inc(16)

        for sub in range(4):
            sc_g = s_sc.n
            def _(t, sub=sub, sc_g=sc_g):
                t.wait_ge(s_ve.h, comb_ready_ve)
                if sub > 0:
                    t.wait_ge(s_sc.h, sc_g)
                r = t.matmul(P[6][:NE, :128], btok[:, sub, :], eye_sb[:, :], start=True, stop=True)
                r.then_inc(s_mm.h, 1)
            block.tensor(_)
            s_mm.inc(1)
            mm_g = s_mm.n
            def _(s, sub=sub, mm_g=mm_g):
                s.wait_ge(s_mm.h, mm_g)
                r = s.copy(out=comb_em[:, sub * 128:(sub + 1) * 128], in_=P[6][:NE, :128])
                r.then_inc(s_sc.h, 1)
            block.scalar(_)
            s_sc.inc(1)
        sc_g = s_sc.n
        def _(v, sc_g=sc_g):
            v.wait_ge(s_sc.h, sc_g)
            v.tensor_scalar(out=m01, in0=comb_em, scalar1=0.0, scalar2=None, op0=OP.is_gt)
            v.tensor_tensor_scan(out=incl_t, data0=ones16[:, :], data1=m01,
                                 initial=cnt_vec[:, :], op0=OP.mult, op1=OP.add)
            v.tensor_scalar(out=srel_t, in0=incl_t, scalar1=-1.0, scalar2=None, op0=OP.add)
            v.drain()
            v.tensor_copy(out=cnt_vec[:, :], in_=incl_t[:, TT - 1:TT])
            v.tensor_scalar(out=tmp_t, in0=srel_t, scalar1=float(CEXP), scalar2=None, op0=OP.is_lt)
            v.tensor_tensor(out=m01, in0=m01, in1=tmp_t, op=OP.mult)
            v.tensor_scalar(out=srel_t, in0=srel_t, scalar1=ebase_sb[:NE, :], scalar2=-BIGS,
                            op0=OP.add, op1=OP.add)
            v.tensor_tensor(out=srel_t, in0=srel_t, in1=m01, op=OP.mult)
            r = v.tensor_scalar(out=srel_t, in0=srel_t, scalar1=BIGS, scalar2=None, op0=OP.add)
            r.then_inc(s_ve.h, 1)
        block.vector(_)
        s_ve.inc(1)
        for sub in range(4):
            ve_g = s_ve.n
            sc_g = s_sc.n
            def _(t, sub=sub, ve_g=ve_g, sc_g=sc_g):
                t.wait_ge(s_ve.h, ve_g)
                if sub > 0:
                    t.wait_ge(s_sc.h, sc_g)
                r = t.matmul(P[7][:, :NE], srel_t[:, sub * 128:(sub + 1) * 128], eye_sb[:NE, :NE], start=True, stop=True)
                r.then_inc(s_mm.h, 1)
            block.tensor(_)
            s_mm.inc(1)
            mm_g = s_mm.n
            def _(s, sub=sub, mm_g=mm_g):
                s.wait_ge(s_mm.h, mm_g)
                r = s.copy(out=slots_f[:, sub, :], in_=P[7][:, :NE])
                r.then_inc(s_sc.h, 1)
            block.scalar(_)
            s_sc.inc(1)
        sc_g = s_sc.n
        isc_pre = s_isc.n
        def _(v, sc_g=sc_g, isc_pre=isc_pre):
            v.wait_ge(s_sc.h, sc_g)
            v.wait_ge(s_isc.h, isc_pre)
            r = v.tensor_copy(out=slots_u[:, :], in_=slots_f[:, :, :])
            r.then_inc(s_ve.h, 1)
        block.vector(_)
        s_ve.inc(1)
        ve_slots = s_ve.n
        isc_base = s_isc.n
        def _(g, t0=t0, ve_slots=ve_slots, isc_base=isc_base):
            g.wait_ge(s_ve.h, ve_slots)
            g.wait_ge(s_isc.h, isc_base)
            g.iota(ids_u[:, :], [[128, 4]], base=t0, channel_multiplier=1,
                   allow_small_or_imprecise_dtypes=True)
            n = 0
            for sub in range(4):
                for e in range(NE):
                    if n and n % 8 == 0:
                        g.wait_ge(s_isc.h, isc_base + 16 * (n - 8))
                    g.indirect_dma_start(
                        out=idx_tab,
                        out_offset=bass.IndirectOffsetOnAxis(ap=slots_u[:, sub * NE + e:sub * NE + e + 1], axis=0),
                        in_=ids_u[:, sub:sub + 1], in_offset=None,
                        bounds_check=regbox["slotbnd"], oob_is_err=False,
                    ).then_inc(s_isc.h, 16)
                    n += 1
        block.gpsimd(_)
        s_isc.inc(16 * 64)

        for sub in range(4):
            par = sub % 2
            xe_pre = s_xe.n
            for half in range(2):
                cs = range(half * 4, half * 4 + 4)
                sc_g = s_sc.n
                def _(t, sub=sub, half=half, cs=cs, sc_g=sc_g):
                    t.wait_ge(s_ve.h, xn_ready)
                    t.wait_ge(s_sc.h, sc_g)
                    for i, c in enumerate(cs):
                        t.matmul(P[5][:, i * 128:(i + 1) * 128], xnh[:, c, sub * 128:(sub + 1) * 128],
                                 eye_bf[:, :], start=(i == 0), stop=(i == 3))
                        r = t.matmul(P[6][:, i * 128:(i + 1) * 128], xnl[:, c, sub * 128:(sub + 1) * 128],
                                     eye_bf[:, :], start=(i == 0), stop=(i == 3))
                    r.then_inc(s_mm.h, 1)
                block.tensor(_)
                s_mm.inc(1)
                mm_g = s_mm.n
                def _(s, sub=sub, half=half, par=par, mm_g=mm_g, xe_pre=xe_pre):
                    s.wait_ge(s_mm.h, mm_g)
                    if half == 0:
                        s.wait_ge(s_xe.h, xe_pre)
                    s.copy(out=tth[par][:, half * 512:(half + 1) * 512], in_=P[5][:, :])
                    r = s.copy(out=ttl[par][:, half * 512:(half + 1) * 512], in_=P[6][:, :])
                    r.then_inc(s_sc.h, 1)
                block.scalar(_)
                s_sc.inc(1)
            sc_g = s_sc.n
            def _(s, sub=sub, par=par, t0=t0, sc_g=sc_g):
                s.wait_ge(s_sc.h, sc_g)
                r0 = t0 + sub * 128
                s.dma_start(out=xn_tok[r0:r0 + 128, 0:D], in_=tth[par][:, :]).then_inc(s_xe.h, 16)
                s.dma_start(out=xn_tok[r0:r0 + 128, D:2 * D], in_=ttl[par][:, :]).then_inc(s_xe.h, 16)
            block.sync(_)
            s_xe.inc(32)

        for pi in range(12):
            wa_gate = panel_conv_ve[pi]
            slot = pi % 2
            for blk in range(2):
                kblk = pi * 2 + blk
                sect = kblk // 8
                loc = kblk % 8
                pband = P[2 * (kblk % 2)]
                ve_g = s_ve.n
                sc_g = s_sc.n
                def _(t, pi=pi, blk=blk, slot=slot, wa_gate=wa_gate, pband=pband, ve_g=ve_g, sc_g=sc_g):
                    t.wait_ge(s_ve.h, max(wa_gate, ve_g))
                    t.wait_ge(s_sc.h, sc_g)
                    for c in range(8):
                        mm3(t, pband[:, :],
                            phl[slot][0][:, c, blk * 128:(blk + 1) * 128],
                            phl[slot][1][:, c, blk * 128:(blk + 1) * 128],
                            xnh[:, c, :], xnl[:, c, :],
                            first=(c == 0), last=(c == 7), inc=(c == 7))
                block.tensor(_)
                s_mm.inc(1)
                mm_g = s_mm.n
                if sect == 0:
                    def _(s, loc=loc, pband=pband, mm_g=mm_g):
                        s.wait_ge(s_mm.h, mm_g)
                        s.activation(out=scr[:, 0:TT], in_=pband[:, :], func=AF.Exp,
                                     bias=db_comb[:, loc:loc + 1])
                        s.add(scr[:, 0:TT], scr[:, 0:TT], 1.0)
                        s.activation(out=scr[:, 0:TT], in_=scr[:, 0:TT], func=AF.Ln)
                        r = s.activation(out=Ra[:, loc, :], in_=scr[:, 0:TT],
                                         func=AF.Exp, scale=alpha_n[:, loc:loc + 1])
                        r.then_inc(s_sc.h, 1)
                    block.scalar(_)
                    s_sc.inc(1)
                    sc_g = s_sc.n
                    def _(v, loc=loc, sc_g=sc_g):
                        v.wait_ge(s_sc.h, sc_g)
                        r = v.tensor_scalar(out=Ra[:, loc, :], in0=Ra[:, loc, :],
                                            scalar1=1.0 - EPS_A, scalar2=None, op0=OP.min)
                        r.then_inc(s_ve.h, 1)
                    block.vector(_)
                    s_ve.inc(1)
                else:
                    dst = Rb if sect == 1 else Rc
                    bias_t = bp_b if sect == 1 else bp_c
                    def _(s, loc=loc, pband=pband, dst=dst, bias_t=bias_t, mm_g=mm_g):
                        s.wait_ge(s_mm.h, mm_g)
                        r = s.activation(out=dst[:, loc, :], in_=pband[:, :],
                                         func=AF.Tanh, bias=bias_t[:, loc:loc + 1])
                        r.then_inc(s_sc.h, 1)
                    block.scalar(_)
                    s_sc.inc(1)
            if pi_loaded < 20:
                emit_panel(pi_loaded, s_mm.n)
                pi_loaded += 1

        pre_norm_ve = s_ve.n
        pre_norm_sc = s_sc.n
        for c in range(8):
            sc_g = s_sc.n
            def _(v, c=c, sc_g=sc_g):
                v.wait_ge(s_sc.h, sc_g)
                a_, b_, c_ = Ra[:, c, :], Rb[:, c, :], Rc[:, c, :]
                p_, r_, q_, d_ = scr[:, 512:1024], scr[:, 1024:1536], scr[:, 1536:2048], scr[:, 2048:2560]
                v.tensor_tensor(out=p_, in0=a_, in1=a_, op=OP.mult)
                v.tensor_tensor(out=q_, in0=c_, in1=c_, op=OP.mult)
                v.tensor_tensor(out=p_, in0=p_, in1=q_, op=OP.add)
                v.tensor_tensor(out=r_, in0=b_, in1=b_, op=OP.mult)
                v.tensor_tensor(out=q_, in0=a_, in1=b_, op=OP.mult)
                v.tensor_tensor(out=d_, in0=p_, in1=r_, op=OP.subtract)
                v.tensor_tensor(out=d_, in0=d_, in1=d_, op=OP.mult)
                v.tensor_tensor(out=q_, in0=q_, in1=q_, op=OP.mult)
                v.tensor_scalar(out=q_, in0=q_, scalar1=4.0, scalar2=None, op0=OP.mult)
                v.tensor_tensor(out=d_, in0=d_, in1=q_, op=OP.add)
                r2 = v.tensor_tensor(out=p_, in0=p_, in1=r_, op=OP.add)
                r2.then_inc(s_ve.h, 1)
            block.vector(_)
            s_ve.inc(1)
            ve_g = s_ve.n
            def _(s, ve_g=ve_g):
                s.wait_ge(s_ve.h, ve_g)
                r2 = s.activation(out=scr[:, 2560:3072], in_=scr[:, 2048:2560], func=AF.Sqrt,
                                  bias=cpool[:, 1:2])
                r2.then_inc(s_sc.h, 1)
            block.scalar(_)
            s_sc.inc(1)
            sc_g = s_sc.n
            def _(v, sc_g=sc_g):
                v.wait_ge(s_sc.h, sc_g)
                r2 = v.tensor_tensor(out=scr[:, 512:1024], in0=scr[:, 512:1024],
                                     in1=scr[:, 2560:3072], op=OP.add)
                r2.then_inc(s_ve.h, 1)
            block.vector(_)
            s_ve.inc(1)
            ve_g = s_ve.n
            def _(s, ve_g=ve_g):
                s.wait_ge(s_ve.h, ve_g)
                r2 = s.activation(out=scr[:, 512:1024], in_=scr[:, 512:1024], func=AF.Sqrt,
                                  scale=0.5, bias=cpool[:, 1:2])
                r2.then_inc(s_sc.h, 1)
            block.scalar(_)
            s_sc.inc(1)
            sc_g = s_sc.n
            def _(v, c=c, sc_g=sc_g):
                v.wait_ge(s_sc.h, sc_g)
                a_, b_, c_ = Ra[:, c, :], Rb[:, c, :], Rc[:, c, :]
                v.tensor_scalar(out=scr[:, 512:1024], in0=scr[:, 512:1024], scalar1=1.0,
                                scalar2=None, op0=OP.max)
                v.drain()
                v.reciprocal(out=scr[:, 1024:1536], in_=scr[:, 512:1024])
                v.drain()
                v.tensor_tensor(out=a_, in0=a_, in1=scr[:, 1024:1536], op=OP.mult)
                v.tensor_tensor(out=b_, in0=b_, in1=scr[:, 1024:1536], op=OP.mult)
                r2 = v.tensor_tensor(out=c_, in0=c_, in1=scr[:, 1024:1536], op=OP.mult)
                r2.then_inc(s_ve.h, 1)
            block.vector(_)
            s_ve.inc(1)

        rbmult_ve = {}
        for pp in range(4):
            pi = 12 + pp
            slot = pi % 2
            wa_gate = panel_conv_ve[pi]
            for blk in range(2):
                nblk = pp * 2 + blk
                pband = P[4 + (nblk % 2)]
                ve_w = rbmult_ve[nblk - 2] if nblk >= 2 else pre_norm_ve
                def _(t, pi=pi, blk=blk, slot=slot, wa_gate=wa_gate, pband=pband, ve_w=ve_w,
                      pre_norm_sc=pre_norm_sc):
                    t.wait_ge(s_ve.h, max(wa_gate, ve_w))
                    t.wait_ge(s_sc.h, pre_norm_sc)
                    for c in range(8):
                        mm3(t, pband[:, :],
                            phl[slot][0][:, c, blk * 128:(blk + 1) * 128],
                            phl[slot][1][:, c, blk * 128:(blk + 1) * 128],
                            xnh[:, c, :], xnl[:, c, :],
                            first=(c == 0), last=(c == 7), inc=(c == 7))
                block.tensor(_)
                s_mm.inc(1)
                mm_g = s_mm.n
                def _(v, nblk=nblk, pband=pband, mm_g=mm_g):
                    v.wait_ge(s_mm.h, mm_g)
                    r = v.tensor_tensor(out=Rb[:, nblk, :], in0=Rb[:, nblk, :],
                                        in1=pband[:, :], op=OP.mult)
                    r.then_inc(s_ve.h, 1)
                block.vector(_)
                s_ve.inc(1)
                rbmult_ve[nblk] = s_ve.n
            if pi_loaded < 20:
                emit_panel(pi_loaded, s_mm.n)
                pi_loaded += 1

        for c in range(8):
            def _(v, c=c):
                v.tensor_tensor_scan(out=scr[:, 0:TT], data0=Ra[:, c, :], data1=Rb[:, c, :],
                                     initial=carry[:, c:c + 1], op0=OP.mult, op1=OP.add)
                v.tensor_tensor(out=Ra[:, c, 0:1], in0=Rc[:, c, 0:1], in1=carry[:, c:c + 1], op=OP.mult)
                v.tensor_tensor(out=Ra[:, c, 1:], in0=Rc[:, c, 1:], in1=scr[:, 0:TT - 1], op=OP.mult)
                v.tensor_copy(out=carry[:, c:c + 1], in_=scr[:, TT - 1:TT])
                v.tensor_copy(out=xnh[:, c, :], in_=Ra[:, c, :])
                r = v.tensor_tensor(out=xnl[:, c, :], in0=Ra[:, c, :], in1=xnh[:, c, :], op=OP.subtract)
                r.then_inc(s_ve.h, 1)
            block.vector(_)
            s_ve.inc(1)

        for pp in range(4):
            pi = 16 + pp
            slot = pi % 2
            wa_gate = panel_conv_ve[pi]
            for blk in range(2):
                oblk = pp * 2 + blk
                pband = P[4 + (oblk % 2)]
                ve_g = s_ve.n
                def _(t, pi=pi, blk=blk, slot=slot, wa_gate=wa_gate, pband=pband, ve_g=ve_g):
                    t.wait_ge(s_ve.h, max(wa_gate, ve_g))
                    for c in range(8):
                        mm3(t, pband[:, :],
                            phl[slot][0][:, c, blk * 128:(blk + 1) * 128],
                            phl[slot][1][:, c, blk * 128:(blk + 1) * 128],
                            xnh[:, c, :], xnl[:, c, :],
                            first=(c == 0), last=(c == 7), inc=(c == 7))
                block.tensor(_)
                s_mm.inc(1)
                mm_g = s_mm.n
                def _(v, oblk=oblk, pband=pband, mm_g=mm_g):
                    v.wait_ge(s_mm.h, mm_g)
                    r = v.scalar_tensor_tensor(out=x_sb[:, oblk, :], in0=pband[:, :],
                                               scalar=attn_scale[:, :], in1=x_sb[:, oblk, :],
                                               op0=OP.mult, op1=OP.add)
                    r.then_inc(s_ve.h, 1)
                block.vector(_)
                s_ve.inc(1)
            if pi_loaded < 20:
                emit_panel(pi_loaded, s_mm.n)
                pi_loaded += 1

        for sub in range(4):
            par = sub % 2
            ot_pre = s_otok.n
            for half in range(2):
                cs = range(half * 4, half * 4 + 4)
                ve_g = s_ve.n
                sc_g = s_sc.n
                def _(t, sub=sub, half=half, cs=cs, ve_g=ve_g, sc_g=sc_g):
                    t.wait_ge(s_ve.h, ve_g)
                    t.wait_ge(s_sc.h, sc_g)
                    for i, c in enumerate(cs):
                        r = t.matmul(P[6][:, i * 128:(i + 1) * 128], x_sb[:, c, sub * 128:(sub + 1) * 128],
                                     eye_sb[:, :], start=(i == 0), stop=(i == 3))
                    r.then_inc(s_mm.h, 1)
                block.tensor(_)
                s_mm.inc(1)
                mm_g = s_mm.n
                def _(s, sub=sub, half=half, par=par, mm_g=mm_g, ot_pre=ot_pre):
                    s.wait_ge(s_mm.h, mm_g)
                    if half == 0:
                        s.wait_ge(s_otok.h, ot_pre)
                    r = s.copy(out=otok[par][:, half * 512:(half + 1) * 512], in_=P[6][:, :])
                    r.then_inc(s_sc.h, 1)
                block.scalar(_)
                s_sc.inc(1)
            sc_g = s_sc.n
            def _(g, sub=sub, par=par, t0=t0, sc_g=sc_g):
                g.wait_ge(s_sc.h, sc_g)
                r0 = t0 + sub * 128
                g.dma_start(out=out_tok[r0:r0 + 128, :], in_=otok[par][:, :]).then_inc(s_otok.h, 16)
            block.gpsimd(_)
            s_otok.inc(16)

    esB.close()

    esC = ExitStack()
    land13 = esC.enter_context(nc.sbuf_tensor("land13", [128, 8, 1024], F32))
    land2 = esC.enter_context(nc.sbuf_tensor("land2", [128, 4, 1024], F32))
    w13h = esC.enter_context(nc.sbuf_tensor("w13h", [128, 8, 1024], BF16))
    w13l = esC.enter_context(nc.sbuf_tensor("w13l", [128, 8, 1024], BF16))
    w2h = esC.enter_context(nc.sbuf_tensor("w2h", [128, 4, 1024], BF16))
    w2l = esC.enter_context(nc.sbuf_tensor("w2l", [128, 4, 1024], BF16))
    xg = esC.enter_context(nc.sbuf_tensor("xg", [128, 8, 2 * D], BF16))
    cg = esC.enter_context(nc.sbuf_tensor("cg", [128, 8, NE], F32))
    cekeep = esC.enter_context(nc.sbuf_tensor("cekeep", [128, 32], F32))
    xfh = esC.enter_context(nc.sbuf_tensor("xfh", [128, 8, 512], BF16))
    xfl = esC.enter_context(nc.sbuf_tensor("xfl", [128, 8, 512], BF16))
    s_sb = esC.enter_context(nc.sbuf_tensor("s_sb", [128, 4, 512], F32))
    hh = esC.enter_context(nc.sbuf_tensor("hh", [128, 4, 512], BF16))
    hl = esC.enter_context(nc.sbuf_tensor("hl", [128, 4, 512], BF16))
    y_fm = esC.enter_context(nc.sbuf_tensor("y_fm", [128, 8, 512], F32))
    y_tok = [esC.enter_context(nc.sbuf_tensor(f"y_tok{i}", [128, 1024], F32)) for i in range(4)]
    idx_sb = esC.enter_context(nc.sbuf_tensor("idx_sb", [128, 64], U32))
    PC = [esC.enter_context(nc.psum_tensor(f"PC{i}", [128, 512], F32)) for i in range(8)]
    blockC = esC.enter_context(nc.Block())
    block = blockC

    xe_total = s_xe.n
    isc_total = s_isc.n
    otok_total = s_otok.n

    def _(g):
        g.memset(xg[:, :, :], 0.0)
        g.memset(cg[:, :, :], 0.0)
        g.wait_ge(s_xe.h, xe_total)
        g.wait_ge(s_isc.h, isc_total)
        g.wait_ge(s_otok.h, otok_total)
    block.gpsimd(_)

    prev_yout_ve = s_ve.n
    prev_w2_mm = s_mm.n
    prev_silu_sc = s_sc.n
    prev_yfm_ve = s_ve.n
    last_xfl_ve = s_ve.n
    pending = {}
    next_gat = {}

    def emit_loads(te):
        sh = (te == NE)
        ep = te % 2
        gat_pre = [gslot[j].n for j in range(8)]
        acc_pre_e = s_acc.n
        ve_conv_pre = s_ve.n
        def _(s, te=te, sh=sh, ep=ep, gat_pre=gat_pre, ve_conv_pre=ve_conv_pre, acc_pre_e=acc_pre_e):
            s.wait_ge(s_ve.h, ve_conv_pre)
            if sh:
                s.dma_start(out=land13[:, :, :],
                            in_=w13sT.rearrange("(c p) f -> p c f", p=128)).then_inc(s_we.h, 16)
                s.dma_start(out=land2[:, :, :],
                            in_=w2sT.rearrange("(c p) d2 -> p c d2", p=128)).then_inc(s_we.h, 16)
            else:
                s.dma_start(out=land13[:, :, :],
                            in_=w13T[te].rearrange("(c p) f -> p c f", p=128)).then_inc(s_we.h, 16)
                s.dma_start(out=land2[:, :, :],
                            in_=w2T[te].rearrange("(c p) d2 -> p c d2", p=128)).then_inc(s_we.h, 16)
                for j in range(8):
                    s.wait_ge(gslot[j].h, gat_pre[j])
                s.wait_ge(s_isc.h, isc_total)
                s.wait_ge(s_acc.h, acc_pre_e)
                s.dma_start(out=idx_sb[:, ep * 32:ep * 32 + 16],
                            in_=idx_tab[te * CEXP:(te + 1) * CEXP, :].rearrange("(p k) x -> p (k x)", p=128)
                            ).then_inc(s_we.h, 16)
        block.sync(_)
        s_we.inc(32 if sh else 48)
        wr = s_we.n
        return wr, (wr if sh else wr - 16), (0 if sh else wr)

    pending[0] = emit_loads(0)
    for e in range(NE + 1):
        shared = (e == NE)
        NK = 32 if shared else NE
        epar = e % 2
        acc_expert_base = s_acc.n
        add_cnt = {}
        we_ready, we_w_ready, idx_ready = pending[e]
        conv13_c = []
        conv2_c = []
        for c in range(8):
            def _(v, c=c, we_w_ready=we_w_ready, prev_w2_mm=prev_w2_mm):
                if c == 0:
                    v.wait_ge(s_we.h, we_w_ready)
                    v.wait_ge(s_mm.h, prev_w2_mm)
                v.tensor_copy(out=w13h[:, c, :], in_=land13[:, c, :])
                r = v.tensor_tensor(out=w13l[:, c, :], in0=land13[:, c, :], in1=w13h[:, c, :], op=OP.subtract)
                r.then_inc(s_ve.h, 1)
            block.vector(_)
            s_ve.inc(1)
            conv13_c.append(s_ve.n)
        for c in range(4):
            def _(v, c=c):
                v.tensor_copy(out=w2h[:, c, :], in_=land2[:, c, :])
                r = v.tensor_tensor(out=w2l[:, c, :], in0=land2[:, c, :], in1=w2h[:, c, :], op=OP.subtract)
                r.then_inc(s_ve.h, 1)
            block.vector(_)
            s_ve.inc(1)
            conv2_c.append(s_ve.n)
        w_ready = conv13_c[0]
        if e + 1 <= NE:
            pending[e + 1] = emit_loads(e + 1)

        trans_mm_cnt = {}
        ce_ve_cnt = {}

        def emit_gather(te, k, mm_pre, ve_pre_g, we_pre):
            sh = (te == NE)
            ep = te % 2
            kk8 = k % 8
            def _(g, te=te, sh=sh, ep=ep, k=k, kk8=kk8, mm_pre=mm_pre, ve_pre_g=ve_pre_g, we_pre=we_pre):
                g.wait_ge(s_mm.h, mm_pre)
                g.wait_ge(s_ve.h, ve_pre_g)
                if we_pre:
                    g.wait_ge(s_we.h, we_pre)
                if sh:
                    g.dma_start(out=xg[:, kk8, :], in_=xn_tok[k * 128:(k + 1) * 128, :]
                                ).then_inc(gslot[kk8].h, 16)
                else:
                    g.indirect_dma_start(
                        out=xg[:, kk8, :], out_offset=None,
                        in_=xn_tok,
                        in_offset=bass.IndirectOffsetOnAxis(ap=idx_sb[:, ep * 32 + k:ep * 32 + k + 1], axis=0),
                        bounds_check=regbox["tokbnd"], oob_is_err=False,
                    ).then_inc(gslot[kk8].h, 16)
                    g.indirect_dma_start(
                        out=cg[:, kk8, :], out_offset=None,
                        in_=comb_tok,
                        in_offset=bass.IndirectOffsetOnAxis(ap=idx_sb[:, ep * 32 + k:ep * 32 + k + 1], axis=0),
                        bounds_check=regbox["tokbnd"], oob_is_err=False,
                    ).then_inc(gslot[kk8].h, 16)
            block.gpsimd(_)
            gslot[kk8].inc(16 if sh else 32)
            return gslot[kk8].n

        def emit_cecopy(k, e=e, shared=shared):
            if shared:
                ce_ve_cnt[k] = s_ve.n
                return
            kk8 = k % 8
            def _(v, k=k, kk8=kk8, gat_g=gat_cnt[k]):
                v.wait_ge(gslot[kk8].h, gat_g)
                r = v.tensor_copy(out=cekeep[:, k:k + 1], in_=cg[:, kk8, e:e + 1])
                r.then_inc(s_ve.h, 1)
            block.vector(_)
            s_ve.inc(1)
            ce_ve_cnt[k] = s_ve.n

        if e == 0:
            gat_cnt = {}
            for k in range(8):
                gat_cnt[k] = emit_gather(0, k, 0, 0, idx_ready)
        else:
            gat_cnt = dict(next_gat)
        next_gat = {}
        for k in range(min(8, NK)):
            emit_cecopy(k)

        for grp in range(NK // 4):
            for kk in range(4):
                k = grp * 4 + kk
                kk8 = k % 8
                gat_g = gat_cnt[k]
                for half in range(2):
                    cs = range(half * 4, half * 4 + 4)
                    sc_g = s_sc.n
                    ve_tr = prev_yfm_ve if (kk == 0 and half == 0) else last_xfl_ve
                    def _(t, k=k, kk8=kk8, half=half, cs=cs, gat_g=gat_g, sc_g=sc_g, ve_tr=ve_tr):
                        t.wait_ge(gslot[kk8].h, gat_g)
                        t.wait_ge(s_sc.h, sc_g)
                        t.wait_ge(s_ve.h, ve_tr)
                        for i, c in enumerate(cs):
                            t.matmul(PC[6][:, i * 128:(i + 1) * 128], xg[:, kk8, c * 128:(c + 1) * 128],
                                     eye_bf[:, :], start=(i == 0), stop=(i == 3))
                            r = t.matmul(PC[7][:, i * 128:(i + 1) * 128], xg[:, kk8, D + c * 128:D + (c + 1) * 128],
                                         eye_bf[:, :], start=(i == 0), stop=(i == 3))
                        r.then_inc(s_mm.h, 1)
                    block.tensor(_)
                    s_mm.inc(1)
                    mm_g = s_mm.n
                    def _(s, kk=kk, half=half, mm_g=mm_g):
                        s.wait_ge(s_mm.h, mm_g)
                        r = s.copy(out=xfh[:, half * 4:(half + 1) * 4, kk * 128:(kk + 1) * 128], in_=PC[6][:, :])
                        r.then_inc(s_sc.h, 1)
                    block.scalar(_)
                    s_sc.inc(1)
                    def _(v, kk=kk, half=half, mm_g=mm_g):
                        v.wait_ge(s_mm.h, mm_g)
                        r = v.tensor_copy(out=xfl[:, half * 4:(half + 1) * 4, kk * 128:(kk + 1) * 128], in_=PC[7][:, :])
                        r.then_inc(s_ve.h, 1)
                    block.vector(_)
                    s_ve.inc(1)
                    last_xfl_ve = s_ve.n
                trans_mm_cnt[k] = s_mm.n
                if k + 8 < NK:
                    gat_cnt[k + 8] = emit_gather(e, k + 8, trans_mm_cnt[k], ce_ve_cnt[k], 0)
                elif e + 1 <= NE:
                    next_gat[k - 8] = emit_gather(e + 1, k - 8, trans_mm_cnt[k], ce_ve_cnt[k],
                                                  pending[e + 1][2])
            xf_ready_sc = s_sc.n
            xf_ready_ve = s_ve.n

            def _(t, xf_ready_sc=xf_ready_sc, xf_ready_ve=xf_ready_ve, conv13_c=conv13_c,
                  prev_silu_sc=prev_silu_sc, prev_yfm_ve=prev_yfm_ve):
                t.wait_ge(s_sc.h, max(xf_ready_sc, prev_silu_sc))
                t.wait_ge(s_ve.h, max(xf_ready_ve, prev_yfm_ve))
                for c in range(8):
                    t.wait_ge(s_ve.h, conv13_c[c])
                    for fb in range(4):
                        mm3(t, PC[fb][:, :],
                            w13h[:, c, fb * 128:(fb + 1) * 128], w13l[:, c, fb * 128:(fb + 1) * 128],
                            xfh[:, c, :], xfl[:, c, :], first=(c == 0), last=(c == 7), inc=(c == 7))
            block.tensor(_)
            s_mm.inc(4)
            mm_gate = s_mm.n
            def _(s, mm_gate=mm_gate):
                for fb in range(4):
                    s.wait_ge(s_mm.h, mm_gate - 3 + fb)
                    r = s.activation(out=s_sb[:, fb, :], in_=PC[fb][:, :], func=AF.Silu)
                r.then_inc(s_sc.h, 1)
            block.scalar(_)
            s_sc.inc(1)
            prev_silu_sc = s_sc.n
            def _(t, prev_yout_ve=prev_yout_ve):
                t.wait_ge(s_ve.h, prev_yout_ve)
                for c in range(8):
                    for fb in range(4):
                        mm3(t, PC[4 + fb][:, :],
                            w13h[:, c, 512 + fb * 128:512 + (fb + 1) * 128],
                            w13l[:, c, 512 + fb * 128:512 + (fb + 1) * 128],
                            xfh[:, c, :], xfl[:, c, :], first=(c == 0), last=(c == 7), inc=(c == 7))
            block.tensor(_)
            s_mm.inc(4)
            mm_up = s_mm.n
            sc_silu = s_sc.n
            def _(v, mm_up=mm_up, sc_silu=sc_silu):
                v.wait_ge(s_sc.h, sc_silu)
                for fb in range(4):
                    v.wait_ge(s_mm.h, mm_up - 3 + fb)
                    v.tensor_tensor(out=s_sb[:, fb, :], in0=s_sb[:, fb, :], in1=PC[4 + fb][:, :], op=OP.mult)
                for fb in range(4):
                    v.tensor_copy(out=hh[:, fb, :], in_=s_sb[:, fb, :])
                    r = v.tensor_tensor(out=hl[:, fb, :], in0=s_sb[:, fb, :], in1=hh[:, fb, :], op=OP.subtract)
                r.then_inc(s_ve.h, 1)
            block.vector(_)
            s_ve.inc(1)
            h_ready = s_ve.n
            def _(t, h_ready=h_ready, conv2_c=conv2_c):
                t.wait_ge(s_ve.h, h_ready)
                for fb in range(4):
                    t.wait_ge(s_ve.h, conv2_c[fb])
                    for db in range(8):
                        mm3(t, PC[db][:, :],
                            w2h[:, fb, db * 128:(db + 1) * 128], w2l[:, fb, db * 128:(db + 1) * 128],
                            hh[:, fb, :], hl[:, fb, :], first=(fb == 0), last=(fb == 3), inc=(fb == 3))
            block.tensor(_)
            s_mm.inc(8)
            mm_y = s_mm.n
            prev_w2_mm = mm_y
            def _(v, mm_y=mm_y):
                for db in range(8):
                    v.wait_ge(s_mm.h, mm_y - 7 + db)
                    r = v.tensor_copy(out=y_fm[:, db, :], in_=PC[db][:, :])
                r.then_inc(s_ve.h, 1)
            block.vector(_)
            s_ve.inc(1)
            yfm_ready = s_ve.n
            prev_yfm_ve = yfm_ready

            yout_evac_ve = {}
            for kk in range(4):
                k = grp * 4 + kk
                ypar = k % 4
                acc_pre = add_cnt.get(k - 4, acc_expert_base)
                for half in range(2):
                    dbs = range(half * 4, half * 4 + 4)
                    ve_g = yout_evac_ve.get(half, 0)
                    def _(t, kk=kk, half=half, dbs=dbs, yfm_ready=yfm_ready, ve_g=ve_g):
                        t.wait_ge(s_ve.h, max(yfm_ready, ve_g))
                        for i, db in enumerate(dbs):
                            r = t.matmul(PC[4 + half][:, i * 128:(i + 1) * 128], y_fm[:, db, kk * 128:(kk + 1) * 128],
                                         eye_sb[:, :], start=(i == 0), stop=(i == 3))
                        r.then_inc(s_mm.h, 1)
                    block.tensor(_)
                    s_mm.inc(1)
                    mm_g = s_mm.n
                    def _(v, k=k, kk=kk, half=half, ypar=ypar, mm_g=mm_g, shared=shared, acc_pre=acc_pre):
                        v.wait_ge(s_mm.h, mm_g)
                        if half == 0:
                            v.wait_ge(s_acc.h, acc_pre)
                        if shared:
                            r = v.tensor_copy(out=y_tok[ypar][:, half * 512:(half + 1) * 512],
                                              in_=PC[4 + half][:, :])
                        else:
                            r = v.tensor_scalar(out=y_tok[ypar][:, half * 512:(half + 1) * 512],
                                                in0=PC[4 + half][:, :],
                                                scalar1=cekeep[:, k:k + 1], scalar2=None, op0=OP.mult)
                        r.then_inc(s_ve.h, 1)
                    block.vector(_)
                    s_ve.inc(1)
                    yout_evac_ve[half] = s_ve.n
                ve_g = s_ve.n
                def _(g, e=e, k=k, ypar=ypar, epar=epar, shared=shared, ve_g=ve_g,
                      acc_expert_base=acc_expert_base, first=(k == 0)):
                    g.wait_ge(s_ve.h, ve_g)
                    if first:
                        g.wait_ge(s_acc.h, acc_expert_base)
                    if shared:
                        g.dma_start(out=out_tok[k * 128:(k + 1) * 128, :], in_=y_tok[ypar][:, :],
                                    accum_op=OP.add).then_inc(s_acc.h, 16)
                    else:
                        g.indirect_dma_start(
                            out=out_tok,
                            out_offset=bass.IndirectOffsetOnAxis(ap=idx_sb[:, epar * 32 + k:epar * 32 + k + 1], axis=0),
                            in_=y_tok[ypar][:, :], in_offset=None,
                            compute_op=OP.add,
                        ).then_inc(s_acc.h, 16)
                block.gpsimd(_)
                s_acc.inc(16)
                add_cnt[k] = s_acc.n
            prev_yout_ve = s_ve.n

            for kk in range(4):
                k2 = (grp + 2) * 4 + kk
                if k2 < NK:
                    emit_cecopy(k2)

    acc_total = s_acc.n
    def _(g, acc_total=acc_total):
        g.wait_ge(s_acc.h, acc_total)
    block.gpsimd(_)
    esC.close()

    es0.close()
    return nc


_NC_CACHE = None


def kernel(L0_attn_u, L0_attn_W_in, L0_attn_W_out, L0_attn_Wp, L0_attn_bp,
           L0_attn_alpha_log, L0_attn_delta_bias, L0_attn_log_gamma,
           L0_ffn_input_ids, L0_ffn_gate_w, L0_ffn_gate_b, L0_ffn_w13, L0_ffn_w2,
           L0_ffn_w13_s, L0_ffn_w2_s):
    global _NC_CACHE
    u = np.asarray(L0_attn_u, dtype=np.float32)

    c = np.ascontiguousarray
    w_inT = c(np.asarray(L0_attn_W_in, np.float32).T)
    w_outT = c(np.asarray(L0_attn_W_out, np.float32).T)
    wpT = c(np.asarray(L0_attn_Wp, np.float32).T)
    gate_wT = c(np.asarray(L0_ffn_gate_w, np.float32).T)
    gate_b = c(np.asarray(L0_ffn_gate_b, np.float32).reshape(NE, 1))
    w13T = c(np.asarray(L0_ffn_w13, np.float32).transpose(0, 2, 1))
    w2T = c(np.asarray(L0_ffn_w2, np.float32).transpose(0, 2, 1))
    w13sT = c(np.asarray(L0_ffn_w13_s, np.float32).T)
    w2sT = c(np.asarray(L0_ffn_w2_s, np.float32).T)
    alpha_log = c(np.asarray(L0_attn_alpha_log, np.float32).reshape(8, 128).T)
    delta_bias = c(np.asarray(L0_attn_delta_bias, np.float32).reshape(8, 128).T)
    bp = c(np.asarray(L0_attn_bp, np.float32).reshape(3, 8, 128).transpose(2, 0, 1).reshape(128, 24))
    log_gamma = c(np.asarray(L0_attn_log_gamma, np.float32).reshape(1, 1))
    eye = np.eye(128, dtype=np.float32)
    ebase = (np.arange(NE, dtype=np.float32) * CEXP).reshape(NE, 1)

    shared = dict(w_inT=w_inT, w_outT=w_outT, wpT=wpT, gate_wT=gate_wT, gate_b=gate_b,
                  w13T=w13T, w2T=w2T, w13sT=w13sT, w2sT=w2sT, alpha_log=alpha_log,
                  delta_bias=delta_bias, bp=bp, log_gamma=log_gamma, eye=eye, ebase=ebase)

    in_maps = []
    for b in range(N_CORES):
        m = dict(shared)
        m["xT"] = c(u[b].T)
        in_maps.append(m)

    if _NC_CACHE is None:
        _NC_CACHE = build_program()
    nc = _NC_CACHE

    res = run_bass_kernel_spmd(nc, in_maps, core_ids=list(range(N_CORES)))
    out = np.stack([np.asarray(res.results[b]["out_tok"])[:TC] for b in range(N_CORES)], axis=0)
    return out
